# version 15
# baseline (speedup 1.0000x reference)
import contextlib

"""Megakernel builder pieces — shared by dev scripts and kernel.py assembly.

Layout conventions (per core = one batch element):
  - Activations feature-major: X^T stored [128 part, cc(8), i] ; feature f = cc*128 + p.
  - DRAM activation scratch: [1024, S] f32, row f, col i.
  - Weights W[c, k] DRAM [1024, K]: lhsT chunks [128 c, k-128] natural slices.
  - matmul: out[M, N] = lhsT.T @ rhs, lhsT = [Kc on part, M], rhs = [Kc on part, N].

RetNet layer (pass R1 = qkv/retention/gn/gate/wo, pass R2 = ln2/ffn):
  see emit_r1 / emit_r2.
"""
import numpy as np
import ml_dtypes
import concourse.bass as bass
import concourse.mybir as mybir
from concourse.alu_op_type import AluOpType as ALU

F32 = mybir.dt.float32
BF16 = mybir.dt.bfloat16
AF = mybir.ActivationFunctionType
AX = mybir.AxisListType
BF = ml_dtypes.bfloat16

D = 1024
CC = 8
BLK = 512
CH = 128
HD = 128
H = 8
EPS_LN = 1e-5
EPS_RMS = 1e-6

GAMMAS = 1.0 - np.exp(np.linspace(np.log(1.0 / 32), np.log(1.0 / 512), 8))


class Ctr:

    def __init__(self, stack, nc, name):
        self.sem = stack.enter_context(nc.semaphore(name))
        self.n = 0

    def inc(self, instr, k=1):
        instr.then_inc(self.sem, k)
        self.n += k
        return instr

    def wait(self, eng, at=None):
        eng.wait_ge(self.sem, self.n if at is None else at)


def host_consts(S):
    o = {}
    o["ident"] = np.eye(128, dtype=BF)
    o["ones"] = np.ones((128, 128), dtype=BF)
    R = np.zeros((128, 128), dtype=np.float32)
    for j in range(64):
        R[2 * j + 1, 2 * j] = -1.0
        R[2 * j, 2 * j + 1] = 1.0
    o["rotmat"] = R.astype(BF)
    cm = (np.arange(128)[None, :] >= np.arange(128)[:, None]).astype(np.float32)
    o["maskT_f32"] = cm
    o["maskT_bf"] = cm.astype(BF)
    SCALE_BASE = 512
    scale_vec = (np.arange(0, HD, 2) + 0.4 * HD) / (1.4 * HD)
    power = (np.arange(S, dtype=np.float64) / SCALE_BASE)[:, None]
    scale = scale_vec[None, :] ** power
    half = HD // 2
    inv_freq = 1.0 / 10000.0 ** (np.arange(half, dtype=np.float64) / half)
    ang = np.arange(S, dtype=np.float64)[:, None] * inv_freq[None, :]
    sin, cos = np.sin(ang), np.cos(ang)
    dup = lambda m: np.repeat(m, 2, axis=-1).astype(np.float32)
    o["qsin"] = np.ascontiguousarray(dup(sin * scale).T).astype(BF)
    o["qcos"] = np.ascontiguousarray(dup(cos * scale).T).astype(BF)
    o["ksin"] = np.ascontiguousarray(dup(sin / scale).T).astype(BF)
    o["kcos"] = np.ascontiguousarray(dup(cos / scale).T).astype(BF)
    j = np.arange(128, dtype=np.float64)
    gq = np.stack([np.broadcast_to(g**j, (128, 128)) for g in GAMMAS]).astype(np.float32)
    gk = np.stack([np.broadcast_to(g**(-j), (128, 128)) for g in GAMMAS]).astype(np.float32)
    o["gq"] = gq.astype(BF)
    o["gk"] = gk.astype(BF)
    o["gammaC"] = (GAMMAS ** 128).astype(np.float64)
    colones = np.zeros((128, 8, 8), dtype=np.float32)
    for h in range(8):
        colones[:, h, h] = 1.0
    o["colones"] = colones.astype(BF)
    selrows = np.zeros((8, 8, 128), dtype=np.float32)
    for h in range(8):
        selrows[h, h, :] = 1.0
    o["selrows"] = selrows.astype(BF)
    return o


def col8(v):
    return np.ascontiguousarray(v.reshape(8, 128).T).astype(np.float32)


def retnet_layer_weights(inp, li):
    s = lambda n: inp[f"L0_attn_{n}"][li]
    o = {}
    o[f"rwq_l{li}"] = np.ascontiguousarray(s("WQ").transpose(1, 0, 2).reshape(D, D))
    o[f"rwk_l{li}"] = np.ascontiguousarray(s("WK").transpose(1, 0, 2).reshape(D, D))
    o[f"rwv_l{li}"] = np.ascontiguousarray(s("WV").transpose(1, 0, 2).reshape(D, D))
    o[f"rwg_l{li}"] = np.ascontiguousarray(s("WG"))
    o[f"rwo_l{li}"] = np.ascontiguousarray(s("WO"))
    o[f"rw1_l{li}"] = np.ascontiguousarray(s("ffn_w1"))
    o[f"rw2_l{li}"] = np.ascontiguousarray(s("ffn_w2"))
    o[f"rln1w_l{li}"] = col8(s("ln1_w"))
    o[f"rln1b_l{li}"] = col8(s("ln1_b"))
    o[f"rln2w_l{li}"] = col8(s("ln2_w"))
    o[f"rln2b_l{li}"] = col8(s("ln2_b"))
    gsel = np.zeros((8, 8, 128), np.float32)
    gw = np.asarray(s("gn_w"))
    for h in range(8):
        gsel[h, h, :] = gw[h * 128:(h + 1) * 128]
    o[f"rgnsel_l{li}"] = gsel
    o[f"rgnb_l{li}"] = col8(s("gn_b"))
    o[f"rfb1_l{li}"] = np.ascontiguousarray(s("ffn_b1").reshape(32, 128).T).astype(np.float32)
    o[f"rfb2_l{li}"] = col8(s("ffn_b2"))
    return o


def feat_dram(ap, b, w=BLK):
    return ap.rearrange("(c p) s -> p c s", p=128)[:, :, b * w:(b + 1) * w]


class GpPool:

    def __init__(self, stack, nc, name, fresh=True):
        self.stack, self.nc, self.name = stack, nc, name
        self.fresh = fresh
        self.sems = []
        self.k = 0
        if not fresh:
            self.sem = stack.enter_context(nc.semaphore(name))

    def dma(self, g, out, in_):
        self.k += 1
        if not self.fresh:
            g.dma_start(out=out, in_=in_).then_inc(self.sem, 16)
            return
        sem = self.stack.enter_context(self.nc.semaphore(f"{self.name}{self.k}"))
        g.dma_start(out=out, in_=in_).then_inc(sem, 16)
        self.sems.append(sem)

    def wait_at(self, eng, v):
        if not self.fresh:
            eng.wait_ge(self.sem, v)
            return
        for sem in self.sems[:v // 16]:
            eng.wait_ge(sem, 16)

    def wait_all(self, eng):
        if not self.fresh:
            eng.wait_ge(self.sem, 16 * self.k)
            return
        for sem in self.sems:
            eng.wait_ge(sem, 16)




"""RetNet pass-1 v2 (ln1 + qkv + xpos + chunked retention + groupnorm + gate + WO + residual).

Emission style: per phase, ONE closure per engine with computed wait values
(the toolchain reloads the ACT table at basic-block entries: first activation
per closure pays ~1.28us).

Norms: broadcast mu and rstd (bf16 rows), x_hat = (x - mu_bc) * rstd_bc.

PSUM banks (all [128,512] f32):
  0: PJ0   1: PJ1  (projections / rot / WO)
  2: PB2   (k-transpose bf16, retention scores h even, g-bias, GN sumsq h0-3, GN bc-rho h even)
  3: PSTY  (ln1 stats rows 0/32 | retention Y h even | v-bias row | GN sums h0-3 | GN bc-mu h even)
  4: RBC   (ln1 bc-rstd | retention Y h odd | GN sums h4-7 | GN bc-mu h odd)
  5: PBC   (ln1 bc-mu | retention scores h odd | k-transpose jc odd | GN sumsq h4-7 | GN bc-rho h odd)
  6: PS0   7: PS1  (retention state, heads 0-3 / 4-7, persistent per layer)
"""


class NS:
    pass


def flat2(t):
    return t.ap().rearrange("p a b -> p (a b)")


def ss(eng, ctr, ins):
    ctr.inc(ins)
    ctr.wait(eng)


def alloc_consts(nc, stack):
    c = NS()
    sb = lambda name, shp, dt: stack.enter_context(nc.sbuf_tensor(name, shp, dt))
    c.fconst = sb("c_fconst", [8, 8], F32)
    for i, v in enumerate([1e-5, 1e-6, 1e-12, 1e-38]):
        nc.const_aps.register(F32, v, c.fconst[0:1, i:i + 1])
    c.ident = sb("c_ident", [128, 128], BF16)
    c.ones = sb("c_ones", [128, 128], BF16)
    c.rotm = sb("c_rotm", [128, 128], BF16)
    c.maskf = sb("c_maskf", [128, 128], BF16)
    c.gq = sb("c_gq", [128, H, 128], BF16)
    c.gk = sb("c_gk", [128, H, 128], BF16)
    c.colones = sb("c_colones", [128, H, H], BF16)
    c.selrows = sb("c_selrows", [H, H, 128], BF16)
    return c


def load_consts(blk, c, dram, ctr):
    def _(sync):
        ctr.inc(sync.dma_start(out=c.fconst[:], in_=dram["fconst"].ap()), 16)
        for name, dst in [("ident", c.ident), ("ones", c.ones), ("rotmat", c.rotm),
                          ("maskT_bf", c.maskf)]:
            ctr.inc(sync.dma_start(out=dst[:], in_=dram[name].ap()), 16)
        for name, dst in [("gq", c.gq), ("gk", c.gk)]:
            ctr.inc(sync.dma_start(out=dst[:], in_=dram[name].ap().rearrange("h p j -> p h j")), 16)
        ctr.inc(sync.dma_start(out=c.colones[:], in_=dram["colones"].ap()), 16)
        ctr.inc(sync.dma_start(out=c.selrows[:], in_=dram["selrows"].ap()), 16)
    blk.sync(_)


def alloc_r1(nc, stack, pfx="", w1buf=None):
    st = NS()
    sb = lambda name, shp, dt: stack.enter_context(nc.sbuf_tensor(pfx + name, shp, dt))
    st.xs = [sb("r1_xs0", [128, CC, BLK], F32), sb("r1_xs1", [128, CC, BLK], F32)]
    st.tab = sb("r1_tab0", [128, 2, BLK], BF16)
    st.xn1 = sb("r1_xn1", [128, CC, BLK], BF16)
    st.qtmp = sb("r1_qtmp0", [128, BLK], BF16)
    st.qs = sb("r1_qs", [128, H, BLK], BF16)
    st.ks = sb("r1_ks", [128, H, BLK], BF16)
    st.kst = sb("r1_kst", [128, 4, H, 128], BF16)
    st.vt = sb("r1_vt", [128, 4, D], BF16)
    st.gbf = sb("r1_gbf", [128, H, BLK], BF16)
    st.abf = [sb(f"r1_abf{i}", [128, 128], BF16) for i in range(4)]
    st.yblk = sb("r1_yblk", [128, H, BLK], BF16)
    st.rotq = st.yblk.ap()[:, 0, :]
    st.xmid = sb("r1_xmid", [128, CC, BLK], F32)
    st.sbf = [sb("r1_sbf0", [128, D], BF16), sb("r1_sbf1", [128, D], BF16)]
    st.g_mu = sb("r1_gmu", [H, BLK], BF16)
    st.g_var = sb("r1_gvar", [H, BLK], BF16)
    st.g_t = sb("r1_gt", [H, BLK], BF16)
    st.g_rstd = sb("r1_grstd", [H, BLK], BF16)
    st.rmu = st.g_mu.ap()[0:1, :]
    st.rvar = st.g_var.ap()[0:1, :]
    st.rrstd = st.g_rstd.ap()[0:1, :]
    st.rt = st.g_t
    if w1buf is not None:
        st.wq = w1buf.ap()[:, :, 0 * D:1 * D]
        st.wk = w1buf.ap()[:, :, 1 * D:2 * D]
        st.wv = w1buf.ap()[:, :, 2 * D:3 * D]
        st.wg = w1buf.ap()[:, :, 3 * D:4 * D]
    else:
        st.wq = sb("r1_wq", [128, CC, D], BF16)
        st.wk = sb("r1_wk", [128, CC, D], BF16)
        st.wv = sb("r1_wv", [128, CC, D], BF16)
        st.wg = sb("r1_wg", [128, CC, D], BF16)
    st.wo = sb("r1_wo", [128, CC, D], BF16)
    st.qb = sb("r1_qb", [128, CC], F32)
    st.kb = sb("r1_kb", [128, CC], F32)
    st.gb = sb("r1_gb", [128, CC], F32)
    st.vbrow = sb("r1_vbrow", [1, D], BF16)
    st.lnw = sb("r1_lnw", [128, CC], F32)
    st.lnbb = sb("r1_lnbb", [128, CC], BF16)
    st.gnsel = sb("r1_gnsel", [8, 8, 128], BF16)
    st.gnb = sb("r1_gnb", [128, CC], F32)
    st.xbf = st.qs
    st.sq = st.ks
    st.zbf = st.xn1
    st.xn1f = st.xmid
    ps = lambda name, shp, dt: stack.enter_context(nc.psum_tensor(pfx + name, shp, dt))
    st.PJ = [ps("r1_pj0", [128, BLK], F32), ps("r1_pj1", [128, BLK], F32)]
    st.PB2 = ps("r1_pb2", [128, BLK], F32)
    st.PSTY = ps("r1_psty", [128, BLK], F32)
    st.RBC = ps("r1_rbc", [128, BLK], F32)
    st.PBC = ps("r1_pbc", [128, BLK], F32)
    st.PS = [ps("r1_ps0", [128, BLK], F32), ps("r1_ps1", [128, BLK], F32)]
    st.pb2bf = st.PB2.ap().bitcast(BF16)
    st.pbcbf = st.PBC.ap().bitcast(BF16)
    return st


R1_CTRS = ["ssv", "sss", "cld", "wld", "xld0", "xld1", "tld", "bmm", "bevac", "fold",
           "nmm", "nrow", "nact", "nbc", "nxn",
           "pj", "pjev", "rot", "rotev", "xpos", "ktr", "ktrev",
           "sc", "msk", "ymm", "yev", "umm", "sts", "sbf",
           "gnz", "womm", "wadd", "xout"]


def emit_norm_rows(blk, st, Ct, n, eps, rms=False, width=BLK):
    inv = 1.0 / n
    MU, VAR, T = st.rmu[0:1, 0:width], st.rvar[0:1, 0:width], st.rt[0:1, 0:width]
    RSTD = st.rrstd[0:1, 0:width]

    def _(v):
        Ct["nmm"].wait(v)
        if not rms:
            v.tensor_scalar(out=MU, in0=st.PSTY[0:1, 0:width], scalar1=inv, scalar2=None, op0=ALU.mult)
            ss(v, Ct["ssv"], v.tensor_scalar(out=VAR, in0=st.PSTY[32:33, 0:width], scalar1=inv, scalar2=None, op0=ALU.mult))
            ss(v, Ct["ssv"], v.tensor_tensor(out=T, in0=MU, in1=MU, op=ALU.mult))
            ins = v.tensor_tensor(out=VAR, in0=VAR, in1=T, op=ALU.subtract)
        else:
            ins = v.tensor_scalar(out=VAR, in0=st.PSTY[32:33, 0:width], scalar1=inv, scalar2=None, op0=ALU.mult)
        Ct["nrow"].inc(ins)
    blk.vector(_)

    def _(s):
        Ct["nrow"].wait(s)
        ss(s, Ct["sss"], s.activation(out=T, in_=VAR, func=AF.Ln, bias=float(eps), scale=1.0))
        ins = s.activation(out=RSTD, in_=T, func=AF.Exp, bias=0.0, scale=-0.5)
        Ct["nact"].inc(ins)
    blk.scalar(_)


def emit_r1(nc, blk, st, c, dram, Ct, li, S, in32, out32):
    NB = S // BLK
    gp = Ct["gp"]
    marks = {}

    def _(g):
        Ct["womm"].wait(g)
        Ct["bmm"].wait(g)
        if "wcast" in Ct:
            Ct["wcast"].wait(g)
        for wname, dst in [(f"rwq_l{li}", st.wq), (f"rwk_l{li}", st.wk),
                           (f"rwv_l{li}", st.wv), (f"rwg_l{li}", st.wg),
                           (f"rwo_l{li}", st.wo)]:
            if wname + "_bf" in dram:
                wname = wname + "_bf"
            gp.dma(g, dst[:], dram[wname].ap().rearrange("(c p) k -> p c k", p=128))
        gp.dma(g, st.lnbb[:], dram[f"rln1b_l{li}"].ap())
        gp.dma(g, st.gnsel[:], dram[f"rgnsel_l{li}"].ap())
    blk.gpsimd(_)

    def _(sync):
        for vname, dst in [(f"rln1w_l{li}", st.lnw),
                           (f"rgnb_l{li}", st.gnb)]:
            Ct["wld"].inc(sync.dma_start(out=dst[:], in_=dram[vname].ap()), 16)
        Ct["wadd"].wait(sync)
        Ct["xpos"].wait(sync)
        Ct["xld0"].inc(sync.dma_start(out=st.xs[0][:], in_=feat_dram(in32, 0)), 16)
        for ti, tn in enumerate(["qcos", "qsin"]):
            Ct["tld"].inc(sync.dma_start(out=st.tab[:, ti, :], in_=dram[tn].ap()[:, 0:BLK]), 16)
    blk.sync(_)

    def _(t):
        gp.wait_all(t)
        Ct["wld"].wait(t)
        Ct["cld"].wait(t)
        Ct["wadd"].wait(t)
        Ct["sbf"].wait(t)
        for kc in range(CC):
            for cc in range(CC):
                t.matmul(st.PJ[0][:, kc:kc + 1], st.wq[:, cc, kc * 128:(kc + 1) * 128],
                         st.lnbb[:, cc:cc + 1], start=(cc == 0), stop=(cc == CC - 1))
            for cc in range(CC):
                t.matmul(st.PJ[1][:, kc:kc + 1], st.wk[:, cc, kc * 128:(kc + 1) * 128],
                         st.lnbb[:, cc:cc + 1], start=(cc == 0), stop=(cc == CC - 1))
            for cc in range(CC):
                ins = t.matmul(st.PB2[:, kc:kc + 1], st.wg[:, cc, kc * 128:(kc + 1) * 128],
                               st.lnbb[:, cc:cc + 1], start=(cc == 0), stop=(cc == CC - 1))
        for cc in range(CC):
            ins = t.matmul(st.PSTY[0:1, 0:BLK], st.lnbb[:, cc:cc + 1], st.wv[:, cc, 0:BLK],
                           start=(cc == 0), stop=(cc == CC - 1))
        Ct["bmm"].inc(ins)
    blk.tensor(_)

    def _(s):
        Ct["bmm"].wait(s)
        s.copy(out=st.qb[:], in_=st.PJ[0][:, 0:CC])
        s.copy(out=st.kb[:], in_=st.PJ[1][:, 0:CC])
        s.copy(out=st.gb[:], in_=st.PB2[:, 0:CC])
        ins = s.copy(out=st.vbrow[0:1, 0:BLK], in_=st.PSTY[0:1, 0:BLK])
        Ct["bevac"].inc(ins)
    blk.scalar(_)

    def _(t):
        Ct["bevac"].wait(t)
        for cc in range(CC):
            ins = t.matmul(st.PSTY[0:1, 0:BLK], st.lnbb[:, cc:cc + 1], st.wv[:, cc, BLK:D],
                           start=(cc == 0), stop=(cc == CC - 1))
        Ct["bmm"].inc(ins)
    blk.tensor(_)

    def _(s):
        Ct["bmm"].wait(s)
        ins = s.copy(out=st.vbrow[0:1, BLK:D], in_=st.PSTY[0:1, 0:BLK])
        Ct["bevac"].inc(ins)
    blk.scalar(_)

    def _(v):
        Ct["bevac"].wait(v)
        for w in [st.wq, st.wk, st.wv, st.wg]:
            for cc in range(CC):
                ins = v.tensor_scalar(out=w[:, cc, :], in0=w[:, cc, :],
                                      scalar1=st.lnw[:, cc:cc + 1], scalar2=None, op0=ALU.mult)
        Ct["fold"].inc(ins)
    blk.vector(_)

    for b in range(NB):
        par = b % 2

        if b + 1 < NB:
            def _(sync, b=b):
                if b >= 1:
                    Ct["wadd"].wait(sync, at=marks[("xfree", b - 1)])
                Ct["xld" + str((b + 1) % 2)].inc(
                    sync.dma_start(out=st.xs[(b + 1) % 2][:], in_=feat_dram(in32, b + 1)), 16)
            blk.sync(_)
        if b >= 1:
            def _(sync, b=b):
                Ct["xpos"].wait(sync, at=marks[("tfree", b)])
                for ti, tn in enumerate(["qcos", "qsin"]):
                    Ct["tld"].inc(sync.dma_start(out=st.tab[:, ti, :],
                                                 in_=dram[tn].ap()[:, b * BLK:(b + 1) * BLK]), 16)
            blk.sync(_)

        def _(s, b=b, par=par):
            xc = Ct["xld" + str(par)]
            s.wait_ge(xc.sem, xc.n)
            if b == 0:
                Ct["fold"].wait(s)
            if ("gnmm", b - 1) in marks:
                Ct["nmm"].wait(s, at=marks[("gnmm", b - 1)])
            ss(s, Ct["sss"], s.copy(out=flat2(st.xbf), in_=flat2(st.xs[par])))
            ins = s.square(out=flat2(st.sq), in_=flat2(st.xbf))
            Ct["nact"].inc(ins)
        blk.scalar(_)

        def _(t):
            Ct["nact"].wait(t)
            Ct["nrow"].wait(t)
            for cc in range(CC):
                t.matmul(st.PSTY[0:1, :], c.ones[:, 0:1], st.xbf[:, cc, :],
                         start=(cc == 0), stop=(cc == CC - 1))
            for cc in range(CC):
                ins = t.matmul(st.PSTY[32:33, :], c.ones[:, 0:1], st.sq[:, cc, :],
                               start=(cc == 0), stop=(cc == CC - 1), tile_position=(0, 32))
            Ct["nmm"].inc(ins)
        blk.tensor(_)

        emit_norm_rows(blk, st, Ct, D, EPS_LN)

        def _(t):
            Ct["nact"].wait(t)
            Ct["nrow"].wait(t)
            Ct["nxn"].wait(t)
            Ct["gnz"].wait(t)
            Ct["yev"].wait(t)
            Ct["msk"].wait(t)
            t.matmul(st.RBC[:, :], c.ones[0:1, 0:128], st.rrstd[0:1, :], start=True, stop=True)
            ins = t.matmul(st.PBC[:, :], c.ones[0:1, 0:128], st.rmu[0:1, :], start=True, stop=True)
            Ct["nbc"].inc(ins)
        blk.tensor(_)

        def _(v, par=par, b=b):
            Ct["nbc"].wait(v)
            v.wait_ge(Ct["xout"].sem, 16 * (li * NB + b))
            rb = st.RBC.ap().unsqueeze(1).broadcast_to((128, CC, BLK))
            mb = st.PBC.ap().unsqueeze(1).broadcast_to((128, CC, BLK))
            ss(v, Ct["ssv"], v.tensor_tensor(out=st.xn1f[:], in0=st.xs[par][:], in1=mb, op=ALU.subtract))
            ins = v.tensor_tensor(out=st.xn1[:], in0=st.xn1f[:], in1=rb, op=ALU.mult)
            Ct["nxn"].inc(ins)
        blk.vector(_)
        marks[("nxn", b)] = Ct["nxn"].n

        for side, wmat, bcol, gtile, dstbuf in [
                ("q", st.wq, st.qb, c.gq, st.qs),
                ("k", st.wk, st.kb, c.gk, st.ks)]:
            if side == "k":
                def _(sync, b=b):
                    Ct["xpos"].wait(sync)
                    for ti, tn in enumerate(["kcos", "ksin"]):
                        Ct["tld"].inc(sync.dma_start(out=st.tab[:, ti, :],
                                                     in_=dram[tn].ap()[:, b * BLK:(b + 1) * BLK]), 16)
                blk.sync(_)
            pj0, pjev0, rot0, rotev0, xpos0 = (Ct["pj"].n, Ct["pjev"].n, Ct["rot"].n,
                                               Ct["rotev"].n, Ct["xpos"].n)
            tld_tgt = Ct["tld"].n

            prjb = lambda kc: [st.PJ[0], st.PJ[1], st.PB2][kc % 3]
            rotb = lambda kc: [st.PSTY, st.RBC, st.PBC][kc % 3]

            def _(t, b=b, wmat=wmat, pjev0=pjev0, rotev0=rotev0):
                Ct["nxn"].wait(t, at=marks[("nxn", b)])
                Ct["wadd"].wait(t, at=marks.get("pjW", 0))
                for kc in range(CC):
                    Ct["pjev"].wait(t, at=max(pjev0, pjev0 + kc - 2))
                    for cc in range(CC):
                        ins = t.matmul(prjb(kc)[:, :], wmat[:, cc, kc * 128:(kc + 1) * 128],
                                       st.xn1[:, cc, :], start=(cc == 0), stop=(cc == CC - 1))
                    Ct["pj"].inc(ins)
                    if kc >= 1:
                        Ct["pjev"].wait(t, at=pjev0 + kc)
                        Ct["rotev"].wait(t, at=max(rotev0, rotev0 + kc - 3))
                        ins = t.matmul(rotb(kc - 1)[:, :], c.rotm[:], st.qtmp[:],
                                       start=True, stop=True)
                        Ct["rot"].inc(ins)
                Ct["pjev"].wait(t, at=pjev0 + CC)
                Ct["rotev"].wait(t, at=max(rotev0, rotev0 + CC - 3))
                ins = t.matmul(rotb(CC - 1)[:, :], c.rotm[:], st.qtmp[:],
                               start=True, stop=True)
                Ct["rot"].inc(ins)
            blk.tensor(_)

            def _(s, bcol=bcol, pj0=pj0, rot0=rot0, xpos0=xpos0):
                for kc in range(CC):
                    s.wait_ge(Ct["pj"].sem, pj0 + kc + 1)
                    s.wait_ge(Ct["rot"].sem, max(0, rot0 + kc))
                    s.wait_ge(Ct["xpos"].sem, max(0, xpos0 + kc))
                    ins = s.activation(out=st.qtmp[:], in_=prjb(kc)[:, :],
                                       func=AF.Identity, bias=bcol[:, kc:kc + 1], scale=1.0)
                    Ct["pjev"].inc(ins)
            blk.scalar(_)

            def _(v, gtile=gtile, dstbuf=dstbuf, pjev0=pjev0, rot0=rot0, tld_tgt=tld_tgt):
                v.wait_ge(Ct["tld"].sem, tld_tgt)
                for kc in range(CC):
                    v.wait_ge(Ct["pjev"].sem, pjev0 + kc + 1)
                    v.wait_ge(Ct["rot"].sem, rot0 + kc + 1)
                    ss(v, Ct["ssv"], v.tensor_tensor(out=st.qtmp[:], in0=st.qtmp[:],
                                                     in1=st.tab[:, 0, :], op=ALU.mult))
                    ins = v.tensor_tensor(out=st.rotq[:], in0=rotb(kc)[:, :],
                                          in1=st.tab[:, 1, :], op=ALU.mult)
                    Ct["rotev"].inc(ins)
                    Ct["rotev"].wait(v)
                    ss(v, Ct["ssv"], v.tensor_tensor(out=st.qtmp[:], in0=st.qtmp[:], in1=st.rotq[:], op=ALU.add))
                    g4 = gtile.ap()[:, kc, :].unsqueeze(1).broadcast_to((128, 4, 128))
                    t3v = st.qtmp.ap().rearrange("p (j i) -> p j i", j=4)
                    dv = dstbuf.ap()[:, kc, :].rearrange("p (j i) -> p j i", j=4)
                    ins = v.tensor_tensor(out=dv, in0=t3v, in1=g4, op=ALU.mult)
                    Ct["xpos"].inc(ins)
            blk.vector(_)
        marks[("tfree", b + 1)] = Ct["xpos"].n

        ktr0, ktrev0 = Ct["ktr"].n, Ct["ktrev"].n

        def _(t, ktrev0=ktrev0):
            Ct["xpos"].wait(t)
            Ct["nxn"].wait(t)
            i = 0
            for kc in range(CC):
                for jc in range(4):
                    tgt = st.pb2bf if jc % 2 == 0 else st.pbcbf
                    if i >= 2:
                        Ct["ktrev"].wait(t, at=ktrev0 + i - 1)
                    ins = t.transpose(tgt[:, 0:128], st.ks[:, kc, jc * 128:(jc + 1) * 128], c.ident[:])
                    Ct["ktr"].inc(ins)
                    i += 1
        blk.tensor(_)

        def _(v, ktr0=ktr0):
            i = 0
            for kc in range(CC):
                for jc in range(4):
                    tgt = st.pb2bf if jc % 2 == 0 else st.pbcbf
                    v.wait_ge(Ct["ktr"].sem, ktr0 + i + 1)
                    ins = v.tensor_copy(out=st.kst[:, jc, kc, :], in_=tgt[:, 0:128])
                    Ct["ktrev"].inc(ins)
                    i += 1
        blk.vector(_)

        pjv0, pjevv0 = Ct["pj"].n, Ct["pjev"].n

        def _(t, pjevv0=pjevv0):
            Ct["rotev"].wait(t)
            Ct["bevac"].wait(t)
            i = 0
            for jc in range(4):
                for half in range(2):
                    if i >= 2:
                        Ct["pjev"].wait(t, at=pjevv0 + i - 1)
                    for cc in range(CC):
                        t.matmul(st.PJ[i % 2][:, :], st.xn1[:, cc, jc * 128:(jc + 1) * 128],
                                 st.wv[:, cc, half * BLK:(half + 1) * BLK],
                                 start=(cc == 0), stop=False)
                    ins = t.matmul(st.PJ[i % 2][:, :], c.ones[0:1, 0:128],
                                   st.vbrow[0:1, half * BLK:(half + 1) * BLK], start=False, stop=True)
                    Ct["pj"].inc(ins)
                    i += 1
            for kc in range(CC):
                Ct["pjev"].wait(t, at=pjevv0 + i - 1)
                for cc in range(CC):
                    ins = t.matmul(st.PJ[i % 2][:, :], st.wg[:, cc, kc * 128:(kc + 1) * 128],
                                   st.xn1[:, cc, :], start=(cc == 0), stop=(cc == CC - 1))
                Ct["pj"].inc(ins)
                i += 1
        blk.tensor(_)

        def _(s, pjv0=pjv0):
            i = 0
            for jc in range(4):
                for half in range(2):
                    s.wait_ge(Ct["pj"].sem, pjv0 + i + 1)
                    ins = s.copy(out=st.vt[:, jc, half * BLK:(half + 1) * BLK], in_=st.PJ[i % 2][:, :])
                    Ct["pjev"].inc(ins)
                    i += 1
            for kc in range(CC):
                s.wait_ge(Ct["pj"].sem, pjv0 + i + 1)
                ins = s.activation(out=st.gbf[:, kc, :], in_=st.PJ[i % 2][:, :],
                                   func=AF.Silu, bias=st.gb[:, kc:kc + 1], scale=1.0)
                Ct["pjev"].inc(ins)
                i += 1
        blk.scalar(_)

        for ci in range(4):
            first = (b == 0 and ci == 0)
            gci = b * 4 + ci
            sc0, msk0, ymm0, yev0 = Ct["sc"].n, Ct["msk"].n, Ct["ymm"].n, Ct["yev"].n

            scob = lambda h: [st.PB2, st.PBC, st.PJ[0]][h % 3]
            yslb = lambda h: [st.PSTY, st.RBC, st.PJ[1]][h % 3]

            def _(t, ci=ci, gci=gci, first=first, msk0=msk0, yev0=yev0):
                if ci == 0:
                    Ct["xpos"].wait(t)
                    Ct["pjev"].wait(t)
                    Ct["ktrev"].wait(t)
                    Ct["nbc"].wait(t)
                Ct["msk"].wait(t, at=msk0)

                def scores(h):
                    ins = t.matmul(scob(h)[:, 0:128], st.ks[:, h, ci * 128:(ci + 1) * 128],
                                   st.qs[:, h, ci * 128:(ci + 1) * 128], start=True, stop=True)
                    Ct["sc"].inc(ins)

                for h in range(3):
                    scores(h)
                Ct["sbf"].wait(t)
                for h in range(H):
                    ins = t.matmul(st.PS[h // 4][:, (h % 4) * 128:(h % 4 + 1) * 128],
                                   st.kst[:, ci, h, :], st.vt[:, ci, h * 128:(h + 1) * 128],
                                   start=first, stop=True, skip_group_check=True)
                Ct["umm"].inc(ins)
                Ct["yev"].wait(t, at=yev0)
                for h in range(H):
                    ysl = yslb(h)[:, 0:128]
                    Ct["msk"].wait(t, at=msk0 + h + 1)
                    if h >= 3:
                        Ct["yev"].wait(t, at=yev0 + h - 2)
                    if first:
                        ins = t.matmul(ysl, st.vt[:, ci, h * 128:(h + 1) * 128], st.abf[h % 4][:],
                                       start=True, stop=True)
                    else:
                        t.matmul(ysl, st.vt[:, ci, h * 128:(h + 1) * 128], st.abf[h % 4][:],
                                 start=True, stop=False)
                        ins = t.matmul(ysl, st.sbf[(gci - 1) % 2][:, h * 128:(h + 1) * 128],
                                       st.qs[:, h, ci * 128:(ci + 1) * 128], start=False, stop=True)
                    Ct["ymm"].inc(ins)
                    if h + 3 < H:
                        scores(h + 3)
            blk.tensor(_)

            def _(v, sc0=sc0, ymm0=ymm0):
                def mask(h):
                    v.wait_ge(Ct["sc"].sem, sc0 + h + 1)
                    if h >= 4:
                        v.wait_ge(Ct["ymm"].sem, ymm0 + h - 3)
                    ins = v.tensor_tensor(out=st.abf[h % 4][:], in0=scob(h)[:, 0:128],
                                          in1=c.maskf[:], op=ALU.mult)
                    Ct["msk"].inc(ins)

                for h in range(3):
                    mask(h)
                Ct["umm"].wait(v)
                for h in range(H):
                    ins = v.tensor_scalar(out=st.PS[h // 4][:, (h % 4) * 128:(h % 4 + 1) * 128],
                                          in0=st.PS[h // 4][:, (h % 4) * 128:(h % 4 + 1) * 128],
                                          scalar1=float(GAMMAS[h] ** 128), scalar2=None, op0=ALU.mult)
                Ct["sts"].inc(ins)
                for h in range(3, H):
                    mask(h)
            blk.vector(_)

            def _(s, ci=ci, gci=gci, ymm0=ymm0):
                Ct["sts"].wait(s)
                s.copy(out=st.sbf[gci % 2][:, 0:BLK], in_=st.PS[0][:, :])
                ins = s.copy(out=st.sbf[gci % 2][:, BLK:D], in_=st.PS[1][:, :])
                Ct["sbf"].inc(ins)
                for h in range(H):
                    s.wait_ge(Ct["ymm"].sem, ymm0 + h + 1)
                    ins = s.copy(out=st.yblk[:, h, ci * 128:(ci + 1) * 128], in_=yslb(h)[:, 0:128])
                    Ct["yev"].inc(ins)
            blk.scalar(_)

        def _(s):
            Ct["yev"].wait(s)
            ins = s.square(out=flat2(st.sq), in_=flat2(st.yblk))
            Ct["nact"].inc(ins)
        blk.scalar(_)

        def _(t):
            Ct["nact"].wait(t)
            Ct["nrow"].wait(t)
            Ct["msk"].wait(t)
            Ct["gnz"].wait(t)
            Ct["nxn"].wait(t)
            for h in range(H):
                t.matmul(st.PSTY[0:8, :], c.colones[:, h, :], st.yblk[:, h, :],
                         start=(h == 0), stop=(h == H - 1), skip_group_check=True)
            for h in range(H):
                ins = t.matmul(st.RBC[0:8, :], c.colones[:, h, :], st.sq[:, h, :],
                               start=(h == 0), stop=(h == H - 1), skip_group_check=True)
            Ct["nmm"].inc(ins)
        blk.tensor(_)
        marks[("gnmm", b)] = Ct["nmm"].n

        def _(v):
            Ct["nmm"].wait(v)
            inv = 1.0 / HD
            v.tensor_scalar(out=st.g_mu[:, :], in0=st.PSTY[0:8, :], scalar1=inv,
                            scalar2=None, op0=ALU.mult)
            ss(v, Ct["ssv"], v.tensor_scalar(out=st.g_var[:, :], in0=st.RBC[0:8, :],
                                             scalar1=inv, scalar2=None, op0=ALU.mult))
            ss(v, Ct["ssv"], v.tensor_tensor(out=st.g_t[:, :], in0=st.g_mu[:, :],
                                             in1=st.g_mu[:, :], op=ALU.mult))
            ins = v.tensor_tensor(out=st.g_var[:, :], in0=st.g_var[:, :],
                                  in1=st.g_t[:, :], op=ALU.subtract)
            Ct["nrow"].inc(ins)
        blk.vector(_)
        marks[("gnrows", b)] = Ct["nrow"].n

        def _(s, b=b):
            s.wait_ge(Ct["nrow"].sem, marks[("gnrows", b)])
            ss(s, Ct["sss"], s.activation(out=st.g_t[:, :], in_=st.g_var[:, :],
                                          func=AF.Ln, bias=c.fconst[0:8, 0:1], scale=1.0))
            ins = s.activation(out=st.g_rstd[:, :], in_=st.g_t[:, :],
                               func=AF.Exp, bias=0.0, scale=-0.5)
            Ct["nact"].inc(ins)
        blk.scalar(_)
        marks[("gnact", b)] = Ct["nact"].n

        gnbc0, gnz0 = Ct["nbc"].n, Ct["gnz"].n

        gmub = lambda h: [st.PSTY, st.RBC, st.PJ[0]][h % 3][:, :]
        grsb = lambda h: [st.PB2, st.PBC, st.PJ[1]][h % 3][:, :]

        def _(t, b=b, gnz0=gnz0):
            Ct["nact"].wait(t, at=marks[("gnact", b)])
            Ct["nxn"].wait(t)
            for h in range(H):
                if h >= 3:
                    Ct["gnz"].wait(t, at=gnz0 + h - 2)
                t.matmul(gmub(h), c.selrows[:, h, :], st.g_mu[:, :], start=True, stop=True)
                ins = t.matmul(grsb(h), st.gnsel[:, h, :], st.g_rstd[:, :], start=True, stop=True)
                Ct["nbc"].inc(ins)
        blk.tensor(_)

        def _(v, gnbc0=gnbc0, gnz0=gnz0):
            Ct["pjev"].wait(v)
            ytp = st.xmid.ap()[:, 0, :]
            for h in range(H):
                v.wait_ge(Ct["nbc"].sem, gnbc0 + h + 1)
                if h >= 1:
                    v.wait_ge(Ct["gnz"].sem, gnz0 + h)
                ss(v, Ct["ssv"], v.tensor_tensor(out=ytp, in0=st.yblk[:, h, :], in1=gmub(h),
                                                 op=ALU.subtract))
                ss(v, Ct["ssv"], v.tensor_tensor(out=ytp, in0=ytp, in1=grsb(h), op=ALU.mult))
                ins = v.scalar_tensor_tensor(out=st.zbf[:, h, :], in0=ytp,
                                             scalar=st.gnb[:, h:h + 1], in1=st.gbf[:, h, :],
                                             op0=ALU.add, op1=ALU.mult)
                Ct["gnz"].inc(ins)
        blk.vector(_)

        womm0, wadd0 = Ct["womm"].n, Ct["wadd"].n

        def _(t, wadd0=wadd0):
            Ct["gnz"].wait(t)
            Ct["pjev"].wait(t)
            for kc in range(CC):
                if kc >= 2:
                    Ct["wadd"].wait(t, at=wadd0 + kc - 1)
                for cc in range(CC):
                    ins = t.matmul(st.PJ[kc % 2][:, :], st.wo[:, cc, kc * 128:(kc + 1) * 128],
                                   st.zbf[:, cc, :], start=(cc == 0), stop=(cc == CC - 1))
                Ct["womm"].inc(ins)
        blk.tensor(_)

        def _(v, par=par, womm0=womm0):
            for kc in range(CC):
                v.wait_ge(Ct["womm"].sem, womm0 + kc + 1)
                ins = v.tensor_tensor(out=st.xmid[:, kc, :], in0=st.PJ[kc % 2][:, :],
                                      in1=st.xs[par][:, kc, :], op=ALU.add)
                Ct["wadd"].inc(ins)
        blk.vector(_)
        marks["pjW"] = Ct["wadd"].n
        marks[("xfree", b)] = Ct["wadd"].n

        def _(s, b=b):
            Ct["wadd"].wait(s)
            Ct["xout"].inc(s.dma_start(out=feat_dram(out32, b), in_=st.xmid[:]), 16)
        blk.scalar(_)

    def _(s):
        s.wait_ge(Ct["xout"].sem, 16 * (li + 1) * NB)
    blk.scalar(_)




"""RetNet pass-2: ln2 + FFN (4096 hidden, silu) + residual [+ final combine].

PSUM banks: 0: PH (ffn1, two 256-halves)  1-4: POUT (8 oc slices, [128,256] each)
            5: PSTY (stats rows)  6: RBC  7: PBC
Sub-block width 256 (SW), 2 subs per 512-block.
"""

SW = 256
FC = 32


def alloc_r2(nc, stack, final, pfx="", w1buf=None):
    st = NS()
    sb = lambda name, shp, dt: stack.enter_context(nc.sbuf_tensor(pfx + name, shp, dt))
    st.w1 = w1buf if w1buf is not None else sb("r2_w1", [128, CC, 4096], BF16)
    st.w2 = sb("r2_w2", [128, FC, D], BF16)
    st.xin = [sb("r2_xin0", [128, CC, SW], F32), sb("r2_xin1", [128, CC, SW], F32)]
    st.xbf = sb("r2_xbf", [128, CC, SW], BF16)
    st.sq = sb("r2_sq", [128, CC, SW], BF16)
    st.xn2b = [sb("r2_xn2a", [128, CC, SW], BF16), sb("r2_xn2b", [128, CC, SW], BF16)]
    st.hbf = [sb(f"r2_hbf{i}", [128, SW], BF16) for i in range(3)]
    st.xnext = [sb("r2_xn0", [128, CC, SW], F32), sb("r2_xn1", [128, CC, SW], F32)]
    st.rmu = sb("r2_rmu", [1, SW], BF16)
    st.rvar = sb("r2_rvar", [1, SW], BF16)
    st.rt = sb("r2_rt", [1, SW], BF16)
    st.rrstd = sb("r2_rrstd", [1, SW], BF16)
    st.fbias = sb("r2_fbias", [128, FC], F32)
    st.fb1 = sb("r2_fb1", [128, FC], F32)
    st.fb2 = sb("r2_fb2", [128, CC], F32)
    st.lnw = sb("r2_lnw", [128, CC], F32)
    st.lnbb = sb("r2_lnbb", [128, CC], BF16)
    if final:
        st.x0s = sb("r2_x0s", [128, CC, SW], F32)
        st.bts = sb("r2_bts", [128, CC, SW], F32)
    ps = lambda name, shp, dt: stack.enter_context(nc.psum_tensor(pfx + name, shp, dt))
    st.PHB = [ps("r2_ph0", [128, 2 * SW], F32), ps("r2_ph1", [128, 2 * SW], F32)]
    st.PO = [ps(f"r2_po{i}", [128, 2 * SW], F32) for i in range(4)]
    st.PSTY = ps("r2_psty", [128, 2 * SW], F32)
    st.RP = ps("r2_rp", [128, 2 * SW], F32)
    return st


R2_CTRS = ["wld2", "xld20", "xld21", "fbmm", "fbev", "fold2",
           "f1", "silu", "f2", "oev", "xout20", "xout21", "fx0", "fbt"]


def emit_r2(nc, blk, st, c, dram, Ct, li, S, in32, out32, final, x0T=None, b32=None):
    NB = S // BLK
    NSUB = S // SW
    gp = Ct["gp"]
    marks = {}

    def _(g):
        Ct["pjev"].wait(g)
        if "wcast" in Ct:
            Ct["wcast"].wait(g)
        w1n = f"rw1_l{li}_bf" if f"rw1_l{li}_bf" in dram else f"rw1_l{li}"
        gp.dma(g, st.w1[:], dram[w1n].ap().rearrange("(c p) k -> p c k", p=128))
        Ct["wadd"].wait(g)
        Ct["xout"].wait(g)
        gp.dma(g, st.lnbb[:], dram[f"rln2b_l{li}"].ap())
        st.w1mark = 16 * gp.k
        w2n = f"rw2_l{li}_bf" if f"rw2_l{li}_bf" in dram else f"rw2_l{li}"
        gp.dma(g, st.w2[:], dram[w2n].ap().rearrange("(f p) o -> p f o", p=128))
        st.w2mark = 16 * gp.k
    blk.gpsimd(_)

    def _(sync):
        Ct["wld2"].inc(sync.dma_start(out=st.lnw[:], in_=dram[f"rln2w_l{li}"].ap()), 16)
        Ct["wld2"].inc(sync.dma_start(out=st.fb1[:], in_=dram[f"rfb1_l{li}"].ap()), 16)
        Ct["wld2"].inc(sync.dma_start(out=st.fb2[:], in_=dram[f"rfb2_l{li}"].ap()), 16)
        Ct["xld20"].inc(sync.dma_start(out=st.xin[0][:], in_=feat_dram(in32, 0, SW)), 16)
    blk.sync(_)

    def _(t):
        gp.wait_at(t, st.w1mark)
        Ct["wld2"].wait(t)
        Ct["cld"].wait(t)
        Ct["wadd"].wait(t)
        for fc in range(FC):
            for cc in range(CC):
                ins = t.matmul(st.PHB[0][:, fc:fc + 1], st.w1[:, cc, fc * 128:(fc + 1) * 128],
                               st.lnbb[:, cc:cc + 1], start=(cc == 0), stop=(cc == CC - 1))
        Ct["fbmm"].inc(ins)
    blk.tensor(_)

    def _(s):
        Ct["fbmm"].wait(s)
        for fc in range(FC):
            ins = s.activation(out=st.fbias[:, fc:fc + 1], in_=st.PHB[0][:, fc:fc + 1],
                               func=AF.Identity, bias=st.fb1[:, fc:fc + 1], scale=1.0)
        Ct["fbev"].inc(ins)
    blk.scalar(_)

    def _(v):
        Ct["fbev"].wait(v)
        for cc in range(CC):
            ins = v.tensor_scalar(out=st.w1[:, cc, :], in0=st.w1[:, cc, :],
                                  scalar1=st.lnw[:, cc:cc + 1], scalar2=None, op0=ALU.mult)
        Ct["fold2"].inc(ins)
    blk.vector(_)

    M = {}

    def stats_scalar(s, sub):
        par = sub % 2
        xc = Ct["xld2" + str(par)]
        s.wait_ge(xc.sem, xc.n)
        if sub == 0:
            Ct["fold2"].wait(s)
        if sub >= 1:
            Ct["nmm"].wait(s, at=M[(sub - 1, "nmm")])
        ss(s, Ct["sss"], s.copy(out=flat2(st.xbf), in_=flat2(st.xin[par])))
        ins = s.square(out=flat2(st.sq), in_=flat2(st.xbf))
        Ct["nact"].inc(ins)
        assert M[(sub, "nactsq")] == Ct["nact"].n, (sub, "nactsq")

    def stats_mm(t, sub):
        Ct["nact"].wait(t, at=M[(sub, "nactsq")])
        if sub >= 1:
            Ct["nxn"].wait(t, at=M[(sub - 1, "nxn")])
        for cc in range(CC):
            t.matmul(st.PSTY[0:1, 0:SW], c.ones[:, 0:1], st.xbf[:, cc, :],
                     start=(cc == 0), stop=(cc == CC - 1))
        for cc in range(CC):
            ins = t.matmul(st.PSTY[32:33, 0:SW], c.ones[:, 0:1], st.sq[:, cc, :],
                           start=(cc == 0), stop=(cc == CC - 1), tile_position=(0, 32))
        Ct["nmm"].inc(ins)
        assert M[(sub, "nmm")] == Ct["nmm"].n, (sub, "nmm")

    def rows_dve(v, sub):
        inv = 1.0 / D
        MU, VAR, T = st.rmu[0:1, 0:SW], st.rvar[0:1, 0:SW], st.rt[0:1, 0:SW]
        Ct["nmm"].wait(v, at=M[(sub, "nmm")])
        if sub >= 1:
            Ct["nbc"].wait(v, at=M[(sub - 1, "nbc")])
        v.tensor_scalar(out=MU, in0=st.PSTY[0:1, 0:SW], scalar1=inv, scalar2=None, op0=ALU.mult)
        ss(v, Ct["ssv"], v.tensor_scalar(out=VAR, in0=st.PSTY[32:33, 0:SW], scalar1=inv,
                                         scalar2=None, op0=ALU.mult))
        ss(v, Ct["ssv"], v.tensor_tensor(out=T, in0=MU, in1=MU, op=ALU.mult))
        ins = v.tensor_tensor(out=VAR, in0=VAR, in1=T, op=ALU.subtract)
        Ct["nrow"].inc(ins)
        assert M[(sub, "nrow")] == Ct["nrow"].n, (sub, "nrow")

    def rows_act(s, sub):
        T = st.rt[0:1, 0:SW]
        Ct["nrow"].wait(s, at=M[(sub, "nrow")])
        ss(s, Ct["sss"], s.activation(out=T, in_=st.rvar[0:1, 0:SW], func=AF.Ln,
                                      bias=EPS_LN, scale=1.0))
        ins = s.activation(out=st.rrstd[0:1, 0:SW], in_=T, func=AF.Exp, bias=0.0, scale=-0.5)
        Ct["nact"].inc(ins)
        assert M[(sub, "nactr")] == Ct["nact"].n, (sub, "nactr")

    def bc_mm(t, sub):
        Ct["nact"].wait(t, at=M[(sub, "nactr")])
        Ct["nrow"].wait(t, at=M[(sub, "nrow")])
        t.matmul(st.RP[:, 0:SW], c.ones[0:1, 0:128], st.rrstd[0:1, :], start=True, stop=True)
        ins = t.matmul(st.PSTY[:, 0:SW], c.ones[0:1, 0:128], st.rmu[0:1, :], start=True, stop=True)
        Ct["nbc"].inc(ins)
        assert M[(sub, "nbc")] == Ct["nbc"].n, (sub, "nbc")

    def xn2_dve(v, sub):
        par = sub % 2
        Ct["nbc"].wait(v, at=M[(sub, "nbc")])
        if sub >= 2:
            Ct["f1"].wait(v, at=M[(sub - 1, "f1end")])
        rb = st.RP.ap()[:, 0:SW].unsqueeze(1).broadcast_to((128, CC, SW))
        mb = st.PSTY.ap()[:, 0:SW].unsqueeze(1).broadcast_to((128, CC, SW))
        ss(v, Ct["ssv"], v.tensor_tensor(out=st.xn2b[par][:], in0=st.xin[par][:], in1=mb,
                                         op=ALU.subtract))
        ins = v.tensor_tensor(out=st.xn2b[par][:], in0=st.xn2b[par][:], in1=rb, op=ALU.mult)
        Ct["nxn"].inc(ins)
        assert M[(sub, "nxn")] == Ct["nxn"].n, (sub, "nxn")

    nact0, nmm0, nrow0 = Ct["nact"].n, Ct["nmm"].n, Ct["nrow"].n
    nbc0, nxn0, f1b0 = Ct["nbc"].n, Ct["nxn"].n, Ct["f1"].n
    for s_ in range(NSUB):
        M[(s_, "nactsq")] = nact0 + 2 * s_ + 1
        M[(s_, "nactr")] = nact0 + 2 * s_ + 2
        M[(s_, "nmm")] = nmm0 + s_ + 1
        M[(s_, "nrow")] = nrow0 + s_ + 1
        M[(s_, "nbc")] = nbc0 + s_ + 1
        M[(s_, "nxn")] = nxn0 + s_ + 1
        M[(s_, "f1end")] = f1b0 + 32 * (s_ + 1)

    def _(s):
        stats_scalar(s, 0)
    blk.scalar(_)

    def _(t):
        stats_mm(t, 0)
    blk.tensor(_)

    def _(v):
        rows_dve(v, 0)
    blk.vector(_)

    def _(s):
        rows_act(s, 0)
    blk.scalar(_)

    def _(t):
        bc_mm(t, 0)
    blk.tensor(_)

    def _(v):
        xn2_dve(v, 0)
    blk.vector(_)

    for sub in range(NSUB):
        par = sub % 2

        if sub + 1 < NSUB:
            def _(sync, sub=sub):
                if sub >= 1:
                    Ct["wadd"].wait(sync, at=marks[("xfree2", sub - 1)])
                Ct["xld2" + str((sub + 1) % 2)].inc(
                    sync.dma_start(out=st.xin[(sub + 1) % 2][:], in_=feat_dram(in32, sub + 1, SW)), 16)
            blk.sync(_)
        if final:
            def _(sync, sub=sub):
                if sub >= 1:
                    Ct["wadd"].wait(sync, at=marks[("ffree", sub - 1)])
                Ct["fx0"].inc(sync.dma_start(out=st.x0s[:], in_=feat_dram(x0T, sub, SW)), 16)
                Ct["fbt"].inc(sync.dma_start(out=st.bts[:], in_=feat_dram(b32, sub, SW)), 16)
            blk.sync(_)

        f10, silu0, f20 = Ct["f1"].n, Ct["silu"].n, Ct["f2"].n

        def _(t, sub=sub, par=par, f10=f10, silu0=silu0):
            Ct["nxn"].wait(t, at=M[(sub, "nxn")])

            def f1(fc):
                Ct["silu"].wait(t, at=max(silu0, silu0 + fc - 1))
                for cc in range(CC):
                    ins = t.matmul(st.PHB[fc % 2][:, 0:SW],
                                   st.w1[:, cc, fc * 128:(fc + 1) * 128], st.xn2b[par][:, cc, :],
                                   start=(cc == 0), stop=(cc == CC - 1))
                Ct["f1"].inc(ins)

            def f2(fc):
                Ct["silu"].wait(t, at=silu0 + fc + 1)
                if fc == 0 and sub == 0:
                    gp.wait_at(t, st.w2mark)
                if fc == 0 and ("odone", sub - 1) in marks:
                    Ct["wadd"].wait(t, at=marks[("odone", sub - 1)])
                for oc in range(CC):
                    ins = t.matmul(st.PO[oc // 2][:, (oc % 2) * SW:(oc % 2 + 1) * SW],
                                   st.w2[:, fc, oc * 128:(oc + 1) * 128], st.hbf[fc % 3][:],
                                   start=(fc == 0 and oc % 2 == 0), stop=(fc == FC - 1),
                                   skip_group_check=True)
                Ct["f2"].inc(ins)

            f1(0)
            f1(1)
            f1(2)
            for fc in range(FC):
                f2(fc)
                if fc + 3 < FC:
                    f1(fc + 3)
                if fc == 2 and sub + 1 < NSUB:
                    stats_mm(t, sub + 1)
                if fc == 8 and sub + 1 < NSUB:
                    bc_mm(t, sub + 1)
            assert M[(sub, "f1end")] == Ct["f1"].n
        blk.tensor(_)

        def _(s, sub=sub, f10=f10, f20=f20):
            for fc in range(FC):
                s.wait_ge(Ct["f1"].sem, f10 + fc + 1)
                ins = s.activation(out=st.hbf[fc % 3][:], in_=st.PHB[fc % 2][:, 0:SW],
                                   func=AF.Silu, bias=st.fbias[:, fc:fc + 1], scale=1.0)
                Ct["silu"].inc(ins)
                if fc == 1 and sub + 1 < NSUB:
                    stats_scalar(s, sub + 1)
                if fc == 6 and sub + 1 < NSUB:
                    rows_act(s, sub + 1)
        blk.scalar(_)
        marks[("f2done", sub)] = Ct["f2"].n

        def _(v, sub=sub, par=par):
            if sub + 1 < NSUB:
                rows_dve(v, sub + 1)
                xn2_dve(v, sub + 1)
            Ct["f2"].wait(v)
            if final:
                Ct["fx0"].wait(v)
                Ct["fbt"].wait(v)
            xo = Ct["xout2" + str(par)]
            v.wait_ge(xo.sem, xo.n)
            for oc in range(CC):
                po = st.PO[oc // 2][:, (oc % 2) * SW:(oc % 2 + 1) * SW]
                ins = v.scalar_tensor_tensor(out=st.xnext[par][:, oc, :], in0=po,
                                             scalar=st.fb2[:, oc:oc + 1], in1=st.xin[par][:, oc, :],
                                             op0=ALU.add, op1=ALU.add)
                if final:
                    ss(v, Ct["ssv"], ins)
                    ss(v, Ct["ssv"], v.tensor_tensor(out=st.xnext[par][:, oc, :],
                                                     in0=st.xnext[par][:, oc, :],
                                                     in1=st.x0s[:, oc, :], op=ALU.add))
                    ins = v.tensor_tensor(out=st.xnext[par][:, oc, :], in0=st.xnext[par][:, oc, :],
                                          in1=st.bts[:, oc, :], op=ALU.add)
                Ct["wadd"].inc(ins)
        blk.vector(_)
        marks[("odone", sub)] = Ct["wadd"].n
        marks[("xfree2", sub)] = Ct["wadd"].n
        marks[("ffree", sub)] = Ct["wadd"].n

        def _(s, sub=sub, par=par):
            Ct["wadd"].wait(s)
            Ct["xout2" + str(par)].inc(s.dma_start(out=feat_dram(out32, sub, SW), in_=st.xnext[par][:]), 16)
        blk.scalar(_)

    if True:
        def _(s):
            s.wait_ge(Ct["xout20"].sem, Ct["xout20"].n)
            s.wait_ge(Ct["xout21"].sem, Ct["xout21"].n)
        blk.scalar(_)




"""Branch-B pass 1: module LN + rmsnorm + GQA causal flash attention + wo + residual.

Per core: x0T -> xn (stored xn32) -> h = xn + gqa(rms(xn)) (stored h32).

Layouts (see kernel host prep):
  wqT_pad [1024, 2048]: q-head h -> out chunk h, rows 0:64 data, 64:128 zero.
  wkT_pad [1024, 512]:  kv-head g -> chunk g, rows 0:64 data.
  wvT     [1024, 256].
  woT_pad [2048, 1024]: rows = 16 heads x 128 (64 data + 64 zero).
  kauginit [128, 4, S] bf16: row 64 = -1 (score shift flag), rows 65.. = 0.
  vauginit [128, 32, 4, 65] bf16: col 64 = 1 (l accumulator), else 0.

Softmax: s~ = q.k - c;  P = exp(s~/8); normalize by l at evac. c = ||q||*max_j||k_j||
(prefix max per kv head) >= max_j q.k — exact softmax shift, P<=1.

PSUM: 0/1: PS scores ping-pong   2: PSTY (stats rows 0/32; qn2 row 64)
      3: RBC (bc / linv-bc even hq)   4: PBC (bc mu / linv-bc odd hq)
      5/6: PJ0/PJ1 (Q proj; flash O accum per hq%2; wo)   7: PK (K/V proj)
"""

NQH = 16
NKV = 4


def alloc_b1(nc, stack, S=4096, pfx="b1_"):
    st = NS()
    sb = lambda name, shp, dt: stack.enter_context(nc.sbuf_tensor(pfx + name, shp, dt))
    st.xnblk = sb("xnblk", [128, CC, BLK], F32)
    st.xrbf = sb("xrbf", [128, CC, BLK], BF16)
    st.anw = sb("anw", [128, CC], F32)
    st.wq = sb("wq", [128, CC, 2048], BF16)
    st.wk = sb("wk", [128, CC, 512], BF16)
    st.wv = sb("wv", [128, CC, 256], BF16)
    st.wo = sb("wo", [128, 16, D], BF16)
    st.kaug = sb("kaug", [128, NKV, S], BF16)
    st.vaug = sb("vaug", [128, S // 128, NKV, 65], BF16)
    st.qs = sb("qs", [128, NQH, BLK], BF16)
    st.obf = sb("obf", [128, NQH, BLK], BF16)
    st.bsqs = sb("bsqs", [128, BLK], BF16)
    st.otmp = sb("otmp", [64, BLK], BF16)
    st.lrows = sb("lrows", [65, BLK], F32)
    st.lrowsb = sb("lrowsb", [65, BLK], BF16)
    st.pt = [sb("pt0", [128, BLK], BF16), sb("pt1", [128, BLK], BF16),
             sb("pt2", [128, BLK], BF16)]
    st.rmu = sb("rmu", [1, BLK], BF16)
    st.rvar = sb("rvar", [1, BLK], BF16)
    st.rrstd = sb("rrstd", [1, BLK], BF16)
    st.maxk = sb("maxk", [1, 8], F32)
    st.mtmp = sb("mtmp", [1, 2], F32)
    st.crow = st.lrows.ap().bitcast(BF16)[:, 0:BLK]
    st.rt = st.bsqs[0:1, :]
    st.qsA = st.qs.ap()[:, 0:8, :]
    st.sqA = st.qs.ap()[:, 8:16, :]
    ps = lambda name, shp, dt: stack.enter_context(nc.psum_tensor(pfx + name, shp, dt))
    st.PS = [ps("ps0", [128, BLK], F32), ps("ps1", [128, BLK], F32)]
    st.PSTY = ps("psty", [128, BLK], F32)
    st.RBC = ps("rbc", [128, BLK], F32)
    st.PBC = ps("pbc", [128, BLK], F32)
    st.PJ = [ps("pj0", [128, BLK], F32), ps("pj1", [128, BLK], F32)]
    st.PK = ps("pk", [128, BLK], F32)
    return st


B1_CTRS = ["bx", "binit", "bfold", "xnst", "hst",
           "bnmm", "bnrow", "bnact", "bnbc", "bnxn",
           "bpj", "bpjev", "bkv", "bkvev",
           "bsq", "bqn", "bkn", "bqnact",
           "fsc", "fexp", "fmsk", "fpv", "foev", "fbc", "fodve",
           "bwo", "bwadd"]


def emit_b1(nc, blk, st, c, dram, Ct, S, x0T, xn32, h32):
    NB = S // BLK
    gp = Ct["gp"]
    marks = {}

    def _(g):
        gp.dma(g, st.wq[:], dram["bwqT"].ap().rearrange("(c p) k -> p c k", p=128))
        gp.dma(g, st.wk[:], dram["bwkT"].ap().rearrange("(c p) k -> p c k", p=128))
        gp.dma(g, st.wv[:], dram["bwvT"].ap().rearrange("(c p) k -> p c k", p=128))
        st.qkvmark = 16 * gp.k
        gp.dma(g, st.wo[:], dram["bwoT"].ap().rearrange("(c p) k -> p c k", p=128))
        st.womark = 16 * gp.k
    blk.gpsimd(_)

    def _(sync):
        Ct["binit"].inc(sync.dma_start(out=st.kaug[:], in_=dram["kauginit"].ap().rearrange(
            "p (g s) -> p g s", g=NKV)), 16)
        Ct["binit"].inc(sync.dma_start(out=st.vaug[:], in_=dram["vauginit"].ap().rearrange(
            "p (j g a) -> p j g a", j=S // 128, g=NKV)), 16)
        Ct["binit"].inc(sync.dma_start(out=flat2(st.obf), in_=dram["zeros16"].ap()), 16)
        Ct["binit"].inc(sync.dma_start(out=st.anw[:], in_=dram["banw"].ap()), 16)
        Ct["bx"].inc(sync.dma_start(out=st.xnblk[:], in_=feat_dram(x0T, 0)), 16)
    blk.sync(_)

    def _(v):
        gp.wait_at(v, st.qkvmark)
        Ct["binit"].wait(v)
        for w in [st.wq, st.wk, st.wv]:
            for cc in range(CC):
                ins = v.tensor_scalar(out=w[:, cc, :], in0=w[:, cc, :],
                                      scalar1=st.anw[:, cc:cc + 1], scalar2=None, op0=ALU.mult)
        Ct["bfold"].inc(ins)
    blk.vector(_)

    for b in range(NB):
        def _(s, b=b):
            s.wait_ge(Ct["bx"].sem, Ct["bx"].n)
            if b == 0:
                Ct["cld"].wait(s)
                Ct["binit"].wait(s)
            if b >= 1:
                Ct["fsc"].wait(s, at=marks[("qsfree", b - 1)])
                Ct["fodve"].wait(s, at=marks[("fldone", b - 1)])
            ss(s, Ct["sss"], s.copy(out=flat2s(st.qsA), in_=flat2(st.xnblk)))
            ins = s.square(out=flat2s(st.sqA), in_=flat2s(st.qsA))
            Ct["bnact"].inc(ins)
        blk.scalar(_)

        def _(t):
            Ct["bnact"].wait(t)
            Ct["bnrow"].wait(t)
            Ct["bqn"].wait(t)
            for cc in range(CC):
                t.matmul(st.PSTY[0:1, :], c.ones[:, 0:1], st.qsA[:, cc, :],
                         start=(cc == 0), stop=(cc == CC - 1))
            for cc in range(CC):
                ins = t.matmul(st.PSTY[32:33, :], c.ones[:, 0:1], st.sqA[:, cc, :],
                               start=(cc == 0), stop=(cc == CC - 1), tile_position=(0, 32))
            Ct["bnmm"].inc(ins)
        blk.tensor(_)

        def _(v):
            Ct["bnmm"].wait(v)
            v.tensor_scalar(out=st.rmu[0:1, :], in0=st.PSTY[0:1, :], scalar1=1.0 / D, scalar2=None, op0=ALU.mult)
            ss(v, Ct["ssv"], v.tensor_scalar(out=st.rvar[0:1, :], in0=st.PSTY[32:33, :], scalar1=1.0 / D,
                                             scalar2=None, op0=ALU.mult))
            ss(v, Ct["ssv"], v.tensor_tensor(out=st.rt[0:1, :], in0=st.rmu[0:1, :], in1=st.rmu[0:1, :],
                                             op=ALU.mult))
            ins = v.tensor_tensor(out=st.rvar[0:1, :], in0=st.rvar[0:1, :], in1=st.rt[0:1, :], op=ALU.subtract)
            Ct["bnrow"].inc(ins)
        blk.vector(_)

        def _(s):
            Ct["bnrow"].wait(s)
            ss(s, Ct["sss"], s.activation(out=st.rt[0:1, :], in_=st.rvar[0:1, :], func=AF.Ln,
                                          bias=EPS_LN, scale=1.0))
            ins = s.activation(out=st.rrstd[0:1, :], in_=st.rt[0:1, :], func=AF.Exp, bias=0.0, scale=-0.5)
            Ct["bnact"].inc(ins)
        blk.scalar(_)

        def _(t):
            Ct["bnact"].wait(t)
            Ct["bnrow"].wait(t)
            Ct["bnxn"].wait(t)
            t.matmul(st.RBC[:, :], c.ones[0:1, 0:128], st.rrstd[0:1, :], start=True, stop=True)
            ins = t.matmul(st.PBC[:, :], c.ones[0:1, 0:128], st.rmu[0:1, :], start=True, stop=True)
            Ct["bnbc"].inc(ins)
        blk.tensor(_)

        def _(v, b=b):
            Ct["bnbc"].wait(v)
            rb = st.RBC.ap().unsqueeze(1).broadcast_to((128, CC, BLK))
            mb = st.PBC.ap().unsqueeze(1).broadcast_to((128, CC, BLK))
            ss(v, Ct["ssv"], v.tensor_tensor(out=st.xnblk[:], in0=st.xnblk[:], in1=mb, op=ALU.subtract))
            ins = v.tensor_tensor(out=st.xnblk[:], in0=st.xnblk[:], in1=rb, op=ALU.mult)
            Ct["bnxn"].inc(ins)
        blk.vector(_)
        marks[("xn", b)] = Ct["bnxn"].n

        def _(s, b=b):
            Ct["bnxn"].wait(s, at=marks[("xn", b)])
            Ct["xnst"].inc(s.dma_start(out=feat_dram(xn32, b), in_=st.xnblk[:]), 16)
        blk.scalar(_)

        def _(s, b=b):
            Ct["bnxn"].wait(s, at=marks[("xn", b)])
            ss(s, Ct["sss"], s.copy(out=flat2s(st.qsA), in_=flat2(st.xnblk)))
            ins = s.square(out=flat2s(st.sqA), in_=flat2s(st.qsA))
            Ct["bnact"].inc(ins)
        blk.scalar(_)

        def _(t):
            Ct["bnact"].wait(t)
            Ct["bnrow"].wait(t)
            for cc in range(CC):
                ins = t.matmul(st.PSTY[32:33, :], c.ones[:, 0:1], st.sqA[:, cc, :],
                               start=(cc == 0), stop=(cc == CC - 1), tile_position=(0, 32))
            Ct["bnmm"].inc(ins)
        blk.tensor(_)

        def _(v):
            Ct["bnmm"].wait(v)
            ins = v.tensor_scalar(out=st.rvar[0:1, :], in0=st.PSTY[32:33, :], scalar1=1.0 / D,
                                  scalar2=None, op0=ALU.mult)
            Ct["bnrow"].inc(ins)
        blk.vector(_)

        def _(s):
            Ct["bnrow"].wait(s)
            ss(s, Ct["sss"], s.activation(out=st.rt[0:1, :], in_=st.rvar[0:1, :], func=AF.Ln,
                                          bias=EPS_RMS, scale=1.0))
            ins = s.activation(out=st.rrstd[0:1, :], in_=st.rt[0:1, :], func=AF.Exp, bias=0.0, scale=-0.5)
            Ct["bnact"].inc(ins)
        blk.scalar(_)

        def _(t):
            Ct["bnact"].wait(t)
            Ct["bnxn"].wait(t)
            ins = t.matmul(st.RBC[:, :], c.ones[0:1, 0:128], st.rrstd[0:1, :], start=True, stop=True)
            Ct["bnbc"].inc(ins)
        blk.tensor(_)

        def _(v):
            Ct["bnbc"].wait(v)
            rb = st.RBC.ap().unsqueeze(1).broadcast_to((128, CC, BLK))
            ins = v.tensor_tensor(out=st.xrbf[:], in0=st.xnblk[:], in1=rb, op=ALU.mult)
            Ct["bnxn"].inc(ins)
        blk.vector(_)
        marks[("xr", b)] = Ct["bnxn"].n

        bpj0, bpjev0 = Ct["bpj"].n, Ct["bpjev"].n
        bkv0, bkvev0 = Ct["bkv"].n, Ct["bkvev"].n

        def _(t, b=b, bpjev0=bpjev0, bkvev0=bkvev0):
            if b == 0:
                Ct["bfold"].wait(t)
            Ct["bnxn"].wait(t, at=marks[("xr", b)])
            Ct["bwadd"].wait(t)
            Ct["fodve"].wait(t)
            for hq in range(NQH):
                Ct["bpjev"].wait(t, at=max(bpjev0, bpjev0 + hq - 1))
                for cc in range(CC):
                    ins = t.matmul(st.PJ[hq % 2][:, :], st.wq[:, cc, hq * 128:(hq + 1) * 128],
                                   st.xrbf[:, cc, :], start=(cc == 0), stop=(cc == CC - 1))
                Ct["bpj"].inc(ins)
            for g in range(NKV):
                Ct["bkvev"].wait(t, at=max(bkvev0, bkvev0 + g))
                for cc in range(CC):
                    ins = t.matmul(st.PK[:, :], st.wk[:, cc, g * 128:(g + 1) * 128],
                                   st.xrbf[:, cc, :], start=(cc == 0), stop=(cc == CC - 1))
                Ct["bkv"].inc(ins)
            for jc in range(4):
                Ct["bkvev"].wait(t, at=bkvev0 + 4 + jc)
                for cc in range(CC):
                    ins = t.matmul(st.PK[:, 0:256], st.xrbf[:, cc, jc * 128:(jc + 1) * 128],
                                   st.wv[:, cc, :], start=(cc == 0), stop=(cc == CC - 1))
                Ct["bkv"].inc(ins)
        blk.tensor(_)

        def _(s, b=b, bpj0=bpj0, bkv0=bkv0):
            for hq in range(NQH):
                s.wait_ge(Ct["bpj"].sem, bpj0 + hq + 1)
                ins = s.copy(out=st.qs[:, hq, :], in_=st.PJ[hq % 2][:, :])
                Ct["bpjev"].inc(ins)
            for g in range(NKV):
                s.wait_ge(Ct["bkv"].sem, bkv0 + g + 1)
                ins = s.copy(out=st.kaug[0:64, g, b * BLK:(b + 1) * BLK], in_=st.PK[0:64, :])
                Ct["bkvev"].inc(ins)
            for jc in range(4):
                s.wait_ge(Ct["bkv"].sem, bkv0 + 4 + jc + 1)
                src = st.PK.ap()[:, 0:256].rearrange("p (g a) -> p g a", g=4)
                dst = st.vaug.ap()[:, b * 4 + jc, :, 0:64]
                ins = s.copy(out=dst, in_=src)
                Ct["bkvev"].inc(ins)
        blk.scalar(_)
        marks[("kv", b)] = Ct["bkvev"].n
        marks[("qev", b)] = Ct["bpjev"].n

        bsq0, bqn0, bkn0, bqa0 = Ct["bsq"].n, Ct["bqn"].n, Ct["bkn"].n, Ct["bqnact"].n

        def _(s, b=b, bqn0=bqn0):
            Ct["bkvev"].wait(s, at=marks[("kv", b)])
            for g in range(NKV):
                s.wait_ge(Ct["bqn"].sem, max(bqn0, bqn0 + g))
                ins = s.square(out=st.bsqs[0:64, :], in_=st.kaug[0:64, g, b * BLK:(b + 1) * BLK])
                Ct["bsq"].inc(ins)
        blk.scalar(_)

        def _(t, bsq0=bsq0, bkn0=bkn0):
            for g in range(NKV):
                Ct["bsq"].wait(t, at=bsq0 + g + 1)
                Ct["bkn"].wait(t, at=max(bkn0, bkn0 + g))
                ins = t.matmul(st.PSTY[0:1, :], c.ones[0:64, 0:1], st.bsqs[0:64, :],
                               start=True, stop=True)
                Ct["bqn"].inc(ins)
        blk.tensor(_)

        def _(v, b=b, bqn0=bqn0):
            for g in range(NKV):
                v.wait_ge(Ct["bqn"].sem, bqn0 + g + 1)
                if b == 0:
                    ins = v.reduce_max(st.maxk[0:1, g:g + 1], st.PSTY[0:1, :], axis=AX.X)
                else:
                    ss(v, Ct["ssv"], v.reduce_max(st.mtmp[0:1, 0:1], st.PSTY[0:1, :], axis=AX.X))
                    ins = v.tensor_tensor(out=st.maxk[0:1, g:g + 1], in0=st.maxk[0:1, g:g + 1],
                                          in1=st.mtmp[0:1, 0:1], op=ALU.max)
                Ct["bkn"].inc(ins)
                Ct["bkn"].wait(v)
        blk.vector(_)

        def _(s, b=b, bkn0=bkn0):
            Ct["bkn"].wait(s, at=bkn0 + NKV)
            ins = s.activation(out=st.maxk[0:1, 4:8], in_=st.maxk[0:1, 0:4], func=AF.Ln,
                               bias=1e-38, scale=1.0)
            Ct["bqnact"].inc(ins)
        blk.scalar(_)

        def _(v):
            Ct["bqnact"].wait(v)
            ins = v.tensor_scalar(out=st.maxk[0:1, 4:8], in0=st.maxk[0:1, 4:8], scalar1=0.5,
                                  scalar2=None, op0=ALU.mult)
            Ct["bkn"].inc(ins)
        blk.vector(_)
        lnmk_mark = Ct["bkn"].n

        bsq1, bqn1 = Ct["bsq"].n, Ct["bqn"].n
        bqa1 = Ct["bqnact"].n

        qbuf = lambda h: st.bsqs[0:64, :] if h % 2 == 0 else st.crow[0:64, :]
        qbank = lambda h: st.PK

        def _(s, b=b, bqn1=bqn1, lnmk_mark=lnmk_mark):
            Ct["bkn"].wait(s, at=lnmk_mark)
            Ct["bpjev"].wait(s, at=marks[("qev", b)])
            for h in (0, 1):
                ins = s.square(out=qbuf(h), in_=st.qs[0:64, h, :])
                Ct["bsq"].inc(ins)
            s.wait_ge(Ct["bqn"].sem, bqn1 + 1)
            ins = s.activation(out=st.rvar[0:1, :], in_=qbank(0)[64:65, :],
                               func=AF.Ln, bias=1e-12, scale=1.0)
            Ct["bqnact"].inc(ins)
            Ct["bqnact"].wait(s)
            ins = s.activation(out=st.qs[64:65, 0, :], in_=st.rvar[0:1, :],
                               func=AF.Exp, bias=st.maxk[0:1, 4:5], scale=0.5)
            Ct["bqnact"].inc(ins)
        blk.scalar(_)

        def _(t, b=b, bsq1=bsq1, bqa1=bqa1):
            Ct["bkvev"].wait(t, at=marks[("kv", b)])
            for h in (0, 1):
                Ct["bsq"].wait(t, at=bsq1 + h + 1)
                if h == 1:
                    Ct["bqnact"].wait(t, at=bqa1 + 1)
                ins = t.matmul(qbank(h)[64:65, :], c.ones[0:64, 0:1], qbuf(h),
                               start=True, stop=True, tile_position=(0, 64))
                Ct["bqn"].inc(ins)
        blk.tensor(_)

        nj = 4 * (b + 1)
        fsc0, fexp0, fmsk0, fpv0 = Ct["fsc"].n, Ct["fexp"].n, Ct["fmsk"].n, Ct["fpv"].n
        foev0, fbc0, fodve0 = Ct["foev"].n, Ct["fbc"].n, Ct["fodve"].n

        def _(t, b=b, nj=nj, bsq1=bsq1, bqa1=bqa1, fexp0=fexp0, fmsk0=fmsk0,
              foev0=foev0, fodve0=fodve0, fbc0=fbc0):
            for hq in range(NQH):
                kvh = hq // 4
                po = st.PJ[hq % 2]
                Ct["foev"].wait(t, at=max(foev0, foev0 + hq - 1))
                Ct["fodve"].wait(t, at=max(fodve0, fodve0 + 3 * hq - 5))
                e0 = fexp0 + hq * nj
                m0 = fmsk0 + hq * 4
                G0 = hq * nj
                psb = lambda g: (st.PS + [st.PSTY])[g % 3]
                Ct["bqnact"].wait(t, at=bqa1 + 2 * hq + 2)
                for jp in (0, 1):
                    if jp < nj:
                        Ct["fexp"].wait(t, at=max(fexp0, fexp0 + G0 + jp - 2))
                        ins = t.matmul(psb(G0 + jp)[:, 0:BLK], st.kaug[:, kvh, jp * 128:(jp + 1) * 128],
                                       st.qs[:, hq, :], start=True, stop=True)
                        Ct["fsc"].inc(ins)
                if hq + 2 < NQH:
                    Ct["bsq"].wait(t, at=bsq1 + hq + 3)
                    Ct["bqnact"].wait(t, at=bqa1 + 2 * hq + 3)
                    ins = t.matmul(qbank(hq)[64:65, :], c.ones[0:64, 0:1], qbuf(hq),
                                   start=True, stop=True, tile_position=(0, 64))
                    Ct["bqn"].inc(ins)
                for j in range(nj):
                    diag = j >= nj - 4
                    m = j - (nj - 4)
                    if j + 2 < nj:
                        Ct["fexp"].wait(t, at=max(fexp0, fexp0 + G0 + j))
                        ins = t.matmul(psb(G0 + j + 2)[:, 0:BLK],
                                       st.kaug[:, kvh, (j + 2) * 128:(j + 3) * 128],
                                       st.qs[:, hq, :], start=True, stop=True)
                        Ct["fsc"].inc(ins)
                    if j == min(3, nj - 2) and hq >= 1:
                        Ct["fodve"].wait(t, at=fodve0 + 3 * (hq - 1) + 2)
                        bcb = st.RBC if (hq - 1) % 2 == 0 else st.PBC
                        Ct["fbc"].wait(t, at=max(fbc0, fbc0 + hq - 2))
                        ins = t.matmul(bcb[0:64, :], c.ones[64:65, 0:64], st.lrowsb[64:65, :],
                                       start=True, stop=True)
                        Ct["fbc"].inc(ins)
                    Ct["fexp"].wait(t, at=e0 + j + 1)
                    if diag:
                        Ct["fmsk"].wait(t, at=m0 + m + 1)
                        s0 = m * 128
                    else:
                        s0 = 0
                    ins = t.matmul(po[0:65, s0:BLK], st.vaug[:, j, kvh, :],
                                   st.pt[(G0 + j) % 3][:, s0:BLK],
                                   start=(j == 0), stop=(j == nj - 1), skip_group_check=True)
                    Ct["fpv"].inc(ins)
            Ct["fodve"].wait(t, at=fodve0 + 3 * (NQH - 1) + 2)
            Ct["fbc"].wait(t, at=max(fbc0, fbc0 + NQH - 2))
            ins = t.matmul(st.PBC[0:64, :], c.ones[64:65, 0:64], st.lrowsb[64:65, :],
                           start=True, stop=True)
            Ct["fbc"].inc(ins)
        blk.tensor(_)
        marks[("qsfree", b)] = Ct["fsc"].n

        def _(s, b=b, nj=nj, fsc0=fsc0, fpv0=fpv0, fodve0=fodve0, bqn1=bqn1):
            for hq in range(NQH):
                if hq + 2 < NQH:
                    s.wait_ge(Ct["bqn"].sem, bqn1 + hq + 2)
                    ins = s.square(out=qbuf(hq + 2), in_=st.qs[0:64, hq + 2, :])
                    Ct["bsq"].inc(ins)
                if hq + 1 < NQH:
                    s.wait_ge(Ct["bqn"].sem, bqn1 + hq + 2)
                    Ct["bqnact"].wait(s)
                    ins = s.activation(out=st.rvar[0:1, :], in_=qbank(hq + 1)[64:65, :],
                                       func=AF.Ln, bias=1e-12, scale=1.0)
                    Ct["bqnact"].inc(ins)
                    Ct["bqnact"].wait(s)
                    ins = s.activation(out=st.qs[64:65, hq + 1, :], in_=st.rvar[0:1, :],
                                       func=AF.Exp,
                                       bias=st.maxk[0:1, 4 + (hq + 1) // 4:5 + (hq + 1) // 4],
                                       scale=0.5)
                    Ct["bqnact"].inc(ins)
                s0 = fsc0 + hq * nj
                p0 = fpv0 + hq * nj
                G0 = hq * nj
                psb = lambda g: (st.PS + [st.PSTY])[g % 3]
                for j in range(nj):
                    s.wait_ge(Ct["fsc"].sem, s0 + j + 1)
                    s.wait_ge(Ct["fpv"].sem, max(p0, p0 + j - 2))
                    ins = s.activation(out=st.pt[(G0 + j) % 3][:], in_=psb(G0 + j)[:, 0:BLK],
                                       func=AF.Exp, bias=0.0, scale=0.125)
                    Ct["fexp"].inc(ins)
                s.wait_ge(Ct["fpv"].sem, p0 + nj)
                s.wait_ge(Ct["fodve"].sem, fodve0 + 3 * hq)
                ins = s.copy(out=st.otmp[0:64, :], in_=st.PJ[hq % 2][0:64, :])
                Ct["foev"].inc(ins)
        blk.scalar(_)

        def _(v, b=b, nj=nj, fexp0=fexp0, foev0=foev0, fbc0=fbc0):
            def rct(hq):
                v.wait_ge(Ct["foev"].sem, foev0 + hq + 1)
                v.wait_ge(Ct["fbc"].sem, max(fbc0, fbc0 + hq))
                ins = v.reciprocal(st.lrows[64:65, :], st.PJ[hq % 2][64:65, :])
                Ct["fodve"].inc(ins)
                Ct["fodve"].wait(v)
                ins = v.tensor_copy(out=st.lrowsb[64:65, :], in_=st.lrows[64:65, :])
                Ct["fodve"].inc(ins)
                v.wait_ge(Ct["fbc"].sem, fbc0 + hq + 1)
                bcb = st.RBC if hq % 2 == 0 else st.PBC
                ins = v.tensor_tensor(out=st.obf[0:64, hq, :], in0=st.otmp[0:64, :],
                                      in1=bcb[0:64, :], op=ALU.mult)
                Ct["fodve"].inc(ins)

            for hq in range(NQH):
                e0 = fexp0 + hq * nj
                G0 = hq * nj

                def masks(hq=hq, e0=e0, G0=G0):
                    for m in range(4):
                        j = nj - 4 + m
                        v.wait_ge(Ct["fexp"].sem, e0 + j + 1)
                        ins = v.tensor_tensor(out=st.pt[(G0 + j) % 3][:, m * 128:(m + 1) * 128],
                                              in0=st.pt[(G0 + j) % 3][:, m * 128:(m + 1) * 128],
                                              in1=c.maskf[:], op=ALU.mult)
                        Ct["fmsk"].inc(ins)

                if nj > 4:
                    if hq >= 1:
                        rct(hq - 1)
                    masks()
                else:
                    masks()
                    if hq >= 1:
                        rct(hq - 1)
            rct(NQH - 1)
        blk.vector(_)
        marks[("po", 0)] = Ct["foev"].n
        marks[("po", 1)] = Ct["foev"].n
        marks[("fldone", b)] = Ct["fodve"].n

        bwo0 = Ct["bwo"].n

        bwadd0 = Ct["bwadd"].n

        def _(t, b=b, bwadd0=bwadd0):
            if b == 0:
                gp.wait_at(t, st.womark)
            Ct["fodve"].wait(t, at=marks[("fldone", b)])
            for oc in range(CC):
                Ct["bwadd"].wait(t, at=max(bwadd0, bwadd0 + oc - 1))
                for cc in range(NQH):
                    ins = t.matmul(st.PJ[oc % 2][:, :], st.wo[:, cc, oc * 128:(oc + 1) * 128],
                                   st.obf[:, cc, :], start=(cc == 0), stop=(cc == NQH - 1))
                Ct["bwo"].inc(ins)
        blk.tensor(_)

        def _(v, b=b, bwo0=bwo0):
            v.wait_ge(Ct["xnst"].sem, 16 * (b + 1))
            for oc in range(CC):
                v.wait_ge(Ct["bwo"].sem, bwo0 + oc + 1)
                ins = v.tensor_tensor(out=st.xnblk[:, oc, :], in0=st.PJ[oc % 2][:, :],
                                      in1=st.xnblk[:, oc, :], op=ALU.add)
                Ct["bwadd"].inc(ins)
        blk.vector(_)

        def _(s, b=b):
            Ct["bwadd"].wait(s)
            Ct["hst"].inc(s.dma_start(out=feat_dram(h32, b), in_=st.xnblk[:]), 16)
        blk.scalar(_)

        if b + 1 < NB:
            def _(sync, b=b):
                sync.wait_ge(Ct["hst"].sem, 16 * (b + 1))
                sync.wait_ge(Ct["xnst"].sem, 16 * (b + 1))
                Ct["bx"].inc(sync.dma_start(out=st.xnblk[:], in_=feat_dram(x0T, b + 1)), 16)
            blk.sync(_)

    def _(s):
        s.wait_ge(Ct["hst"].sem, 16 * NB)
        s.wait_ge(Ct["xnst"].sem, 16 * NB)
    blk.scalar(_)


def flat2s(ap):
    return ap.rearrange("p a b -> p (a b)")




"""Branch-B pass 3: h -> rms -> top-2-of-8 gating -> dense-weighted experts -> + h -> B32.

Dense formulation (exactly the reference's math): out = sum_e comb[:,e] * (t @ We.T),
computed as sum_e ((comb_e*t) @ We.T) accumulated in PSUM across (e, cc).
Gating: top-2 softmax: comb_e = [l_e >= m2] * exp(l_e - m1) / (1 + exp(m2 - m1)).

Sub width 256. PSUM: 0-3: PEX (8 oc slices) 4: PG (logits, comb transpose via bitcast)
5: PSTY (rms stats) 6: RBC (bc) 7: spare.
"""

SW = 256
NE = 8


def alloc_b3(nc, stack, pfx="b3_"):
    st = NS()
    sb = lambda name, shp, dt: stack.enter_context(nc.sbuf_tensor(pfx + name, shp, dt))
    st.expw = sb("expw", [128, NE, CC, D], BF16)
    st.gw = sb("gw", [128, CC, 8], BF16)
    st.gwf = sb("gwf", [128, CC, 8], F32)
    st.glo = sb("glo", [128, CC, 8], BF16)
    st.tlo = sb("tlo", [128, CC, SW], BF16)
    st.fnw = sb("fnw", [128, CC], F32)
    st.hs = [sb("hs0", [128, CC, SW], F32), sb("hs1", [128, CC, SW], F32)]
    st.tbf = [sb("tbf0", [128, CC, SW], BF16), sb("tbf1", [128, CC, SW], BF16)]
    st.bsq = sb("bsq", [128, CC, SW], BF16)
    st.te = [sb("te0", [128, CC, SW], BF16), sb("te1", [128, CC, SW], BF16)]
    st.cbbf = [sb("cbbf0", [128, SW], BF16), sb("cbbf1", [128, SW], BF16)]
    st.bout = [sb("bout0", [128, CC, SW], F32), sb("bout1", [128, CC, SW], F32)]
    st.combT = sb("combT", [8, SW], BF16)
    st.rvar = sb("rvar", [1, SW], BF16)
    st.rt = sb("rt", [1, SW], BF16)
    st.rrstd = sb("rrstd", [1, SW], BF16)
    st.cm1 = sb("cm1", [128, 1], F32)
    st.cnm1 = sb("cnm1", [128, 1], F32)
    st.cm2 = sb("cm2", [128, 1], F32)
    st.cdi = sb("cdi", [128, 1], F32)
    st.cex = sb("cex", [128, 8], F32)
    st.cge = sb("cge", [128, 8], F32)
    st.cbig = sb("cbig", [128, 8], F32)
    st.cb16 = sb("cb16", [128, 8], BF16)
    ps = lambda name, shp, dt: stack.enter_context(nc.psum_tensor(pfx + name, shp, dt))
    st.PEX = [ps(f"pex{i}", [128, 2 * SW], F32) for i in range(4)]
    st.PG = ps("pg", [128, 2 * SW], F32)
    st.PG2 = ps("pg2", [128, 2 * SW], F32)
    st.PSTY = ps("psty", [128, 2 * SW], F32)
    st.RBC = ps("rbc", [128, 2 * SW], F32)
    st.pgbf = st.PG.ap().bitcast(BF16)
    st.pg2bf = st.PG2.ap().bitcast(BF16)
    return st


B3_CTRS = ["bh0", "bh1", "bst0", "bst1",
           "mnmm", "mnrow", "mnact", "mnbc", "mnxn",
           "glog", "gdve", "gact", "gtr", "gtrm", "gtev",
           "ebc", "ebcev", "ete", "emm", "eev"]


def emit_b3(nc, blk, st, c, dram, Ct, S, h32, b32):
    NSUB = S // SW
    gp = Ct["gp"]
    marks = {}

    def _(g):
        if "wcast" in Ct:
            Ct["wcast"].wait(g)
        en = "bexpT_bf" if "bexpT_bf" in dram else "bexpT"
        gp.dma(g, st.expw[:], dram[en].ap().rearrange("e (c p) k -> p e c k", p=128))
        gp.dma(g, st.gw[:], dram["bgwT"].ap().rearrange("(c p) k -> p c k", p=128))
    blk.gpsimd(_)

    def _(sync):
        Ct["bh0"].inc(sync.dma_start(out=st.hs[0][:], in_=feat_dram(h32, 0, SW)), 16)
        Ct["mnbc"].inc(sync.dma_start(out=st.fnw[:], in_=dram["bfnw"].ap()), 16)
        Ct["mnbc"].inc(sync.dma_start(out=st.gwf[:], in_=dram["bgwT"].ap().rearrange(
            "(c p) k -> p c k", p=128)), 16)
    blk.sync(_)

    def _(v):
        v.wait_ge(Ct["mnbc"].sem, 32)
        gpv = Ct["gp"]
        gpv.wait_all(v)
        ins = v.tensor_tensor(out=flat2(st.glo), in0=flat2(st.gwf), in1=flat2(st.gw), op=ALU.subtract)
        Ct["gdve"].inc(ins)
    blk.vector(_)

    B = {}
    bases = dict(mnact=Ct["mnact"].n, mnmm=Ct["mnmm"].n, mnrow=Ct["mnrow"].n,
                 mnbc=Ct["mnbc"].n, mnxn=Ct["mnxn"].n, glog=Ct["glog"].n,
                 gdve=Ct["gdve"].n, gtr=Ct["gtr"].n, gtrm=Ct["gtrm"].n,
                 gact=Ct["gact"].n, gtev=Ct["gtev"].n, ebc=Ct["ebc"].n,
                 ebcev=Ct["ebcev"].n, ete=Ct["ete"].n, emm=Ct["emm"].n, eev=Ct["eev"].n)
    for s_ in range(-2, NSUB):
        if s_ < 0:
            for k in bases:
                B[(s_, k)] = bases[k]
            continue
        B[(s_, "mnactsq")] = bases["mnact"] + 2 * s_ + 1
        B[(s_, "mnactr")] = bases["mnact"] + 2 * s_ + 2
        B[(s_, "mnmm")] = bases["mnmm"] + s_ + 1
        B[(s_, "mnrow")] = bases["mnrow"] + s_ + 1
        B[(s_, "mnbc")] = bases["mnbc"] + s_ + 1
        B[(s_, "mnxn")] = bases["mnxn"] + s_ + 1
        B[(s_, "glog")] = bases["glog"] + 2 * (s_ + 1)
        B[(s_, "gdve")] = bases["gdve"] + 2 * (s_ + 1)
        B[(s_, "gtr")] = bases["gtr"] + 2 * (s_ + 1)
        B[(s_, "gtrm")] = bases["gtrm"] + 2 * (s_ + 1)
        B[(s_, "gact")] = bases["gact"] + 4 * (s_ + 1)
        B[(s_, "gtev")] = bases["gtev"] + 2 * (s_ + 1)
        B[(s_, "ebc")] = bases["ebc"] + 8 * (s_ + 1)
        B[(s_, "ebcev")] = bases["ebcev"] + 8 * (s_ + 1)
        B[(s_, "ete")] = bases["ete"] + 8 * (s_ + 1)
        B[(s_, "emm")] = bases["emm"] + 8 * (s_ + 1)
        B[(s_, "eev")] = bases["eev"] + s_ + 1
    for k in ["mnactsq", "mnactr"]:
        for s_ in (-1, -2):
            B[(s_, k)] = bases["mnact"]

    def rms_scalar(s, sub):
        hc = Ct["bh" + str(sub % 2)]
        s.wait_ge(hc.sem, hc.n)
        if sub == 0:
            Ct["cld"].wait(s)
        Ct["ete"].wait(s, at=B[(sub - 2, "ete")])
        ss(s, Ct["sss"], s.copy(out=flat2(st.tbf[sub % 2]), in_=flat2(st.hs[sub % 2])))
        ins = s.square(out=flat2(st.bsq), in_=flat2(st.tbf[sub % 2]))
        Ct["mnact"].inc(ins)
        assert Ct["mnact"].n == B[(sub, "mnactsq")]

    def rms_mm(t, sub):
        if sub == 0:
            gp.wait_all(t)
        Ct["mnact"].wait(t, at=B[(sub, "mnactsq")])
        Ct["mnrow"].wait(t, at=B[(sub - 1, "mnrow")])
        Ct["mnxn"].wait(t, at=B[(sub - 1, "mnxn")])
        for cc in range(CC):
            ins = t.matmul(st.PSTY[32:33, 0:SW], c.ones[:, 0:1], st.bsq[:, cc, :],
                           start=(cc == 0), stop=(cc == CC - 1), tile_position=(0, 32))
        Ct["mnmm"].inc(ins)
        assert Ct["mnmm"].n == B[(sub, "mnmm")]

    def rows_dve(v, sub):
        Ct["mnmm"].wait(v, at=B[(sub, "mnmm")])
        Ct["mnact"].wait(v, at=B[(sub - 1, "mnactr")])
        ins = v.tensor_scalar(out=st.rvar[0:1, :], in0=st.PSTY[32:33, 0:SW], scalar1=1.0 / D,
                              scalar2=None, op0=ALU.mult)
        Ct["mnrow"].inc(ins)
        assert Ct["mnrow"].n == B[(sub, "mnrow")]

    def rows_act(s, sub):
        Ct["mnrow"].wait(s, at=B[(sub, "mnrow")])
        ss(s, Ct["sss"], s.activation(out=st.rt[0:1, :], in_=st.rvar[0:1, :], func=AF.Ln,
                                      bias=EPS_RMS, scale=1.0))
        ins = s.activation(out=st.rrstd[0:1, :], in_=st.rt[0:1, :], func=AF.Exp, bias=0.0, scale=-0.5)
        Ct["mnact"].inc(ins)
        assert Ct["mnact"].n == B[(sub, "mnactr")]

    def rms_bc(t, sub):
        Ct["mnact"].wait(t, at=B[(sub, "mnactr")])
        Ct["mnxn"].wait(t, at=B[(sub - 1, "mnxn")])
        ins = t.matmul(st.PSTY[:, SW:2 * SW], c.ones[0:1, 0:128], st.rrstd[0:1, :],
                       start=True, stop=True)
        Ct["mnbc"].inc(ins)
        assert Ct["mnbc"].n == B[(sub, "mnbc")]

    def fold_dve(v, sub):
        par = sub % 2
        Ct["mnbc"].wait(v, at=B[(sub, "mnbc")])
        bc_ = Ct["bst" + str(par)]
        v.wait_ge(bc_.sem, bc_.n)
        Ct["ebc"].wait(v, at=B[(sub - 1, "ebc")])
        Ct["glog"].wait(v, at=B[(sub - 1, "glog")])
        rb = st.PSTY.ap()[:, SW:2 * SW].unsqueeze(1).broadcast_to((128, CC, SW))
        ss(v, Ct["ssv"], v.tensor_tensor(out=st.bout[par][:], in0=st.hs[par][:], in1=rb, op=ALU.mult))
        for cc in range(CC):
            ss(v, Ct["ssv"], v.tensor_scalar(out=st.bout[par][:, cc, :], in0=st.bout[par][:, cc, :],
                                             scalar1=st.fnw[:, cc:cc + 1], scalar2=None, op0=ALU.mult))
        ss(v, Ct["ssv"], v.tensor_copy(out=flat2(st.tbf[par]), in_=flat2(st.bout[par])))
        ins = v.tensor_tensor(out=flat2(st.tlo), in0=flat2(st.bout[par]), in1=flat2(st.tbf[par]),
                              op=ALU.subtract)
        Ct["mnxn"].inc(ins)
        assert Ct["mnxn"].n == B[(sub, "mnxn")]

    def logits_t(t, sub):
        Ct["mnxn"].wait(t, at=B[(sub, "mnxn")])
        Ct["gtev"].wait(t, at=B[(sub - 1, "gtev")])
        for ic in range(2):
            pg = st.PG if ic == 0 else st.PG2
            for cc in range(CC):
                t.matmul(pg[:, 0:8], st.tbf[sub % 2][:, cc, ic * 128:(ic + 1) * 128],
                         st.gw[:, cc, :], start=(cc == 0), stop=False)
            for cc in range(CC):
                t.matmul(pg[:, 0:8], st.tbf[sub % 2][:, cc, ic * 128:(ic + 1) * 128],
                         st.glo[:, cc, :], start=False, stop=False)
            for cc in range(CC):
                ins = t.matmul(pg[:, 0:8], st.tlo[:, cc, ic * 128:(ic + 1) * 128],
                               st.gw[:, cc, :], start=False, stop=(cc == CC - 1))
            Ct["glog"].inc(ins)
        assert Ct["glog"].n == B[(sub, "glog")]

    def transpose_t(t, sub):
        for ic in range(2):
            pgb = st.pgbf if ic == 0 else st.pg2bf
            Ct["gtr"].wait(t, at=bases["gtr"] + 2 * sub + ic + 1)
            Ct["gact"].wait(t, at=bases["gact"] + 4 * sub + 2 * (ic + 1))
            ins = t.transpose(pgb[0:8, 512:640], st.cb16[:], c.ident[:])
            Ct["gtrm"].inc(ins)
        assert Ct["gtrm"].n == B[(sub, "gtrm")]

    def comb_dve(v, sub):
        for ic in range(2):
            pg = st.PG if ic == 0 else st.PG2
            pgb = st.pgbf if ic == 0 else st.pg2bf
            lg = pg[:, 0:8]
            v.wait_ge(Ct["glog"].sem, bases["glog"] + 2 * sub + ic + 1)
            ss(v, Ct["ssv"], v.reduce_max(st.cm1[:, 0:1], lg, axis=AX.X))
            ss(v, Ct["ssv"], v.tensor_scalar(out=st.cnm1[:, 0:1], in0=st.cm1[:, 0:1],
                                             scalar1=-1.0, scalar2=None, op0=ALU.mult))
            ss(v, Ct["ssv"], v.tensor_scalar(out=st.cbig[:, :], in0=lg, scalar1=st.cm1[:, 0:1],
                                             scalar2=1e30, op0=ALU.is_equal, op1=ALU.mult))
            ss(v, Ct["ssv"], v.tensor_tensor(out=st.cbig[:, :], in0=lg, in1=st.cbig[:, :],
                                             op=ALU.subtract))
            ss(v, Ct["ssv"], v.reduce_max(st.cm2[:, 0:1], st.cbig[:, :], axis=AX.X))
            ss(v, Ct["ssv"], v.tensor_scalar(out=st.cge[:, :], in0=lg, scalar1=st.cm2[:, 0:1],
                                             scalar2=None, op0=ALU.is_ge))
            ins = v.tensor_tensor(out=st.cm2[:, 0:1], in0=st.cm2[:, 0:1],
                                  in1=st.cm1[:, 0:1], op=ALU.subtract)
            Ct["gdve"].inc(ins)
            Ct["gdve"].wait(v)
            v.wait_ge(Ct["gact"].sem, bases["gact"] + 4 * sub + 2 * ic + 2)
            ss(v, Ct["ssv"], v.tensor_scalar(out=st.cdi[:, 0:1], in0=st.cdi[:, 0:1],
                                             scalar1=1.0, scalar2=None, op0=ALU.add))
            ss(v, Ct["ssv"], v.reciprocal(st.cdi[:, 0:1], st.cdi[:, 0:1]))
            ss(v, Ct["ssv"], v.tensor_scalar(out=st.cex[:, :], in0=st.cex[:, :],
                                             scalar1=st.cdi[:, 0:1], scalar2=None, op0=ALU.mult))
            ss(v, Ct["ssv"], v.tensor_tensor(out=st.cex[:, :], in0=st.cex[:, :], in1=st.cge[:, :],
                                             op=ALU.mult))
            ins = v.tensor_copy(out=st.cb16[:, :], in_=st.cex[:, :])
            Ct["gtr"].inc(ins)
            Ct["gtr"].wait(v)
            v.wait_ge(Ct["gtrm"].sem, bases["gtrm"] + 2 * sub + ic + 1)
            Ct["ebc"].wait(v, at=B[(sub - 1, "ebc")])
            ins = v.tensor_copy(out=st.combT[0:8, ic * 128:(ic + 1) * 128],
                                in_=pgb[0:8, 512:640])
            Ct["gtev"].inc(ins)
        assert Ct["gtev"].n == B[(sub, "gtev")]

    def comb_act(s, sub):
        for ic in range(2):
            pg = st.PG if ic == 0 else st.PG2
            lg = pg[:, 0:8]
            s.wait_ge(Ct["gdve"].sem, bases["gdve"] + 2 * sub + ic + 1)
            ins = s.activation(out=st.cex[:, :], in_=lg, func=AF.Exp, bias=st.cnm1[:, 0:1], scale=1.0)
            Ct["gact"].inc(ins)
            Ct["gact"].wait(s)
            ins = s.activation(out=st.cdi[:, 0:1], in_=st.cm2[:, 0:1], func=AF.Exp, bias=0.0, scale=1.0)
            Ct["gact"].inc(ins)
        assert Ct["gact"].n == B[(sub, "gact")]

    def _(s):
        rms_scalar(s, 0)
    blk.scalar(_)

    def _(t):
        rms_mm(t, 0)
    blk.tensor(_)

    def _(v):
        rows_dve(v, 0)
    blk.vector(_)

    def _(s):
        rows_act(s, 0)
    blk.scalar(_)

    def _(t):
        rms_bc(t, 0)
    blk.tensor(_)

    def _(v):
        fold_dve(v, 0)
    blk.vector(_)

    def _(t):
        logits_t(t, 0)
    blk.tensor(_)

    def _(s):
        comb_act(s, 0)
    blk.scalar(_)

    def _(v):
        comb_dve(v, 0)
    blk.vector(_)

    def _(t):
        transpose_t(t, 0)
    blk.tensor(_)

    for sub in range(NSUB):
        par = sub % 2

        if sub + 1 < NSUB:
            def _(sync, sub=sub):
                if sub >= 1:
                    Ct["eev"].wait(sync, at=B[(sub - 1, "eev")])
                Ct["bh" + str((sub + 1) % 2)].inc(
                    sync.dma_start(out=st.hs[(sub + 1) % 2][:], in_=feat_dram(h32, sub + 1, SW)), 16)
            blk.sync(_)

        def _(t, sub=sub):
            Ct["gtev"].wait(t, at=B[(sub, "gtev")])
            Ct["mnxn"].wait(t, at=B[(sub, "mnxn")])

            def bc(e):
                Ct["ebcev"].wait(t, at=max(B[(sub - 1, "ebcev")], bases["ebcev"] + 8 * sub + e))
                ins = t.matmul(st.RBC[:, SW:2 * SW], c.selrows[:, e, :], st.combT[0:8, :],
                               start=True, stop=True)
                Ct["ebc"].inc(ins)

            bc(0)
            bc(1)
            for e in range(NE):
                Ct["ete"].wait(t, at=bases["ete"] + 8 * sub + e + 1)
                if e == 0 and sub >= 1:
                    Ct["eev"].wait(t, at=B[(sub - 1, "eev")])
                for oc in range(CC):
                    for cc in range(CC):
                        ins = t.matmul(st.PEX[oc // 2][:, (oc % 2) * SW:(oc % 2 + 1) * SW],
                                       st.expw[:, e, cc, oc * 128:(oc + 1) * 128],
                                       st.te[e % 2][:, cc, :],
                                       start=(e == 0 and cc == 0 and oc % 2 == 0),
                                       stop=(e == NE - 1 and cc == CC - 1),
                                       skip_group_check=True)
                Ct["emm"].inc(ins)
                if e + 2 < NE:
                    bc(e + 2)
                if sub + 1 < NSUB:
                    if e == 1:
                        rms_mm(t, sub + 1)
                    elif e == 3:
                        rms_bc(t, sub + 1)
                    elif e == 6:
                        logits_t(t, sub + 1)
            if sub + 1 < NSUB:
                transpose_t(t, sub + 1)
            assert Ct["ebc"].n == B[(sub, "ebc")]
            assert Ct["emm"].n == B[(sub, "emm")]
        blk.tensor(_)

        def _(s, sub=sub):
            for e in range(NE):
                s.wait_ge(Ct["ebc"].sem, bases["ebc"] + 8 * sub + e + 1)
                if e >= 2:
                    s.wait_ge(Ct["ete"].sem, bases["ete"] + 8 * sub + e - 1)
                ins = s.copy(out=st.cbbf[e % 2][:], in_=st.RBC[:, SW:2 * SW])
                Ct["ebcev"].inc(ins)
                if sub + 1 < NSUB:
                    if e == 1:
                        rms_scalar(s, sub + 1)
                    elif e == 3:
                        rows_act(s, sub + 1)
            if sub + 1 < NSUB:
                comb_act(s, sub + 1)
        blk.scalar(_)

        def _(v, sub=sub):
            for e in range(NE):
                v.wait_ge(Ct["ebcev"].sem, bases["ebcev"] + 8 * sub + e + 1)
                if e >= 2:
                    v.wait_ge(Ct["emm"].sem, bases["emm"] + 8 * sub + e - 1)
                cb = st.cbbf[e % 2].ap().unsqueeze(1).broadcast_to((128, CC, SW))
                ins = v.tensor_tensor(out=st.te[e % 2][:], in0=st.tbf[sub % 2][:], in1=cb, op=ALU.mult)
                Ct["ete"].inc(ins)
                if sub + 1 < NSUB:
                    if e == 2:
                        rows_dve(v, sub + 1)
                    elif e == 5:
                        fold_dve(v, sub + 1)
            if sub + 1 < NSUB:
                comb_dve(v, sub + 1)
            assert Ct["ete"].n == B[(sub, "ete")]
        blk.vector(_)

        def _(v, sub=sub, par=par):
            v.wait_ge(Ct["emm"].sem, B[(sub, "emm")])
            if sub >= 2:
                sct = Ct["bst" + str(par)]
                v.wait_ge(sct.sem, sct.n)
            for oc in range(CC):
                ins = v.tensor_tensor(out=st.bout[par][:, oc, :],
                                      in0=st.PEX[oc // 2][:, (oc % 2) * SW:(oc % 2 + 1) * SW],
                                      in1=st.hs[par][:, oc, :], op=ALU.add)
            Ct["eev"].inc(ins)
            assert Ct["eev"].n == B[(sub, "eev")]
        blk.vector(_)

        def _(s, sub=sub, par=par):
            Ct["eev"].wait(s, at=B[(sub, "eev")])
            Ct["bst" + str(par)].inc(s.dma_start(out=feat_dram(b32, sub, SW), in_=st.bout[par][:]), 16)
        blk.scalar(_)

    def _(s):
        s.wait_ge(Ct["bst0"].sem, Ct["bst0"].n)
        s.wait_ge(Ct["bst1"].sem, Ct["bst1"].n)
    blk.scalar(_)


from concourse.bass_utils import run_bass_kernel_spmd

S_TOT = 4096
CONST_KEYS = ["ident", "ones", "rotmat", "maskT_bf", "gq", "gk",
              "qcos", "qsin", "kcos", "ksin", "colones", "selrows"]
CAST_NAMES = [f"r{w}_l{li}" for li in (0, 1)
              for w in ("wq", "wk", "wv", "wg", "wo", "w1", "w2")] + ["bexpT"]


def host_branch_weights(inputs, S):
    o = {}
    wq = np.asarray(inputs["L0_ffn_wq"])
    wk = np.asarray(inputs["L0_ffn_wk"])
    wv = np.asarray(inputs["L0_ffn_wv"])
    wo = np.asarray(inputs["L0_ffn_wo"])
    wqp = np.zeros((1024, 16, 128), np.float32)
    wqp[:, :, 0:64] = wq.T.reshape(1024, 16, 64)
    o["bwqT"] = np.ascontiguousarray(wqp.reshape(1024, 2048))
    wkp = np.zeros((1024, 4, 128), np.float32)
    wkp[:, :, 0:64] = wk.T.reshape(1024, 4, 64)
    o["bwkT"] = np.ascontiguousarray(wkp.reshape(1024, 512))
    o["bwvT"] = np.ascontiguousarray(wv.T)
    wop = np.zeros((16, 128, 1024), np.float32)
    wop[:, 0:64, :] = wo.T.reshape(16, 64, 1024)
    o["bwoT"] = np.ascontiguousarray(wop.reshape(2048, 1024))
    ka = np.zeros((128, 4, S), np.float32)
    ka[64] = -1.0
    o["kauginit"] = np.ascontiguousarray(ka.reshape(128, 4 * S)).astype(BF)
    va = np.zeros((128, S // 128, 4, 65), np.float32)
    va[:, :, :, 64] = 1.0
    o["vauginit"] = np.ascontiguousarray(va.reshape(128, -1)).astype(BF)
    o["zeros16"] = np.zeros((128, 16 * BLK), dtype=BF)
    o["banw"] = col8(np.asarray(inputs["L0_ffn_attn_norm_w"]))
    o["bfnw"] = col8(np.asarray(inputs["L0_ffn_ffn_norm_w"]))
    o["bgwT"] = np.ascontiguousarray(np.asarray(inputs["L0_ffn_gate_w"]).T)
    o["bexpT"] = np.ascontiguousarray(
        np.asarray(inputs["L0_ffn_expert_w"]).transpose(0, 2, 1))
    return o


def barrier(nc, deps):
    with nc.Block() as blk:
        def _(sync):
            for ctr, v in deps:
                sync.wait_ge(ctr.sem, v)
        blk.sync(_)

        def _(g):
            for ctr, v in deps:
                g.wait_ge(ctr.sem, v)
        blk.gpsimd(_)


def build(dram_shapes, S):
    nc = bass.Bass("TRN2", target_bir_lowering=False)
    dram = {}
    for name, (shp, dt, kind) in dram_shapes.items():
        dram[name] = nc.dram_tensor(name, list(shp), dt, kind=kind)
    NB = S // BLK
    stack = contextlib.ExitStack()
    with stack:
        c = alloc_consts(nc, stack)
        names = []
        for lst in (R1_CTRS, R2_CTRS, B1_CTRS, B3_CTRS):
            for n in lst:
                if n not in names:
                    names.append(n)
        Ct = {n: Ctr(stack, nc, n) for n in names}
        Ct["wcast"] = Ctr(stack, nc, "wcast")
        Ct["gp"] = GpPool(stack, nc, "gp", fresh=False)
        with nc.Block() as blk:
            load_consts(blk, c, dram, Ct["cld"])
        with contextlib.ExitStack() as pst:
            st = alloc_b1(nc, pst, S)
            with nc.Block() as blk:
                emit_b1(nc, blk, st, c, dram, Ct, S, dram["x0T"].ap(),
                        dram["xn32"].ap(), dram["h32"].ap())
        with nc.Block() as blk:
            def _(g):
                for nm in CAST_NAMES:
                    Ct["wcast"].inc(g.dma_start(out=dram[nm + "_bf"].ap(),
                                                in_=dram[nm].ap()), 16)
            blk.gpsimd(_)
        barrier(nc, [(Ct["hst"], 16 * NB), (Ct["xnst"], 16 * NB)])
        with contextlib.ExitStack() as pst:
            st = alloc_b3(nc, pst)
            with nc.Block() as blk:
                emit_b3(nc, blk, st, c, dram, Ct, S, dram["h32"].ap(), dram["b32"].ap())
        barrier(nc, [(Ct["bst0"], Ct["bst0"].n), (Ct["bst1"], Ct["bst1"].n)])
        for li in range(2):
            if li:
                barrier(nc, [(Ct["xout20"], Ct["xout20"].n),
                             (Ct["xout21"], Ct["xout21"].n)])
            with contextlib.ExitStack() as lst:
                w1buf = lst.enter_context(
                    nc.sbuf_tensor(f"l{li}_w1s", [128, CC, 4 * D], BF16))
                with contextlib.ExitStack() as pst:
                    st1 = alloc_r1(nc, pst, pfx=f"l{li}_", w1buf=w1buf)
                    with nc.Block() as blk:
                        emit_r1(nc, blk, st1, c, dram, Ct, li, S,
                                dram["xn32" if li == 0 else "a32"].ap(), dram["mid32"].ap())
                with contextlib.ExitStack() as pst:
                    st2 = alloc_r2(nc, pst, final=(li == 1), pfx=f"l{li}_", w1buf=w1buf)
                    with nc.Block() as blk:
                        emit_r2(nc, blk, st2, c, dram, Ct, li, S, dram["mid32"].ap(),
                                dram["a32" if li == 0 else "outT"].ap(),
                                final=(li == 1), x0T=dram["x0T"].ap(), b32=dram["b32"].ap())
    return nc


def kernel(**inputs):
    S = S_TOT
    X = np.asarray(inputs["L0_attn_X"])
    base = {}
    hc = host_consts(S)
    base.update({k: hc[k] for k in CONST_KEYS})
    base["fconst"] = np.tile(np.array([[1e-5, 1e-6, 1e-12, 1e-38, 0, 0, 0, 0]],
                                      dtype=np.float32), (8, 1))
    for li in range(2):
        base.update(retnet_layer_weights(inputs, li))
    base.update(host_branch_weights(inputs, S))
    percore = []
    for cid in range(8):
        d = dict(base)
        d["x0T"] = np.ascontiguousarray(X[cid].T).astype(np.float32)
        percore.append(d)
    dram_shapes = {"x0T": ((1024, S), F32, "ExternalInput"),
                   "outT": ((1024, S), F32, "ExternalOutput")}
    for nm in ["xn32", "h32", "b32", "mid32", "a32"]:
        dram_shapes[nm] = ((1024, S), F32, "Internal")
    for nm in CAST_NAMES:
        dram_shapes[nm + "_bf"] = (base[nm].shape, BF16, "Internal")
    for k, v in base.items():
        if k not in dram_shapes:
            dt = BF16 if v.dtype == ml_dtypes.bfloat16 else F32
            dram_shapes[k] = (v.shape, dt, "ExternalInput")
    nc = build(dram_shapes, S)
    res = run_bass_kernel_spmd(nc, percore, core_ids=list(range(8)))
    out = np.stack([np.asarray(res.results[cid]["outT"]).T for cid in range(8)], 0)
    return np.ascontiguousarray(out.astype(np.float32))


# version 16
# speedup vs baseline: 1.0334x; 1.0334x over previous
import contextlib

"""Megakernel builder pieces — shared by dev scripts and kernel.py assembly.

Layout conventions (per core = one batch element):
  - Activations feature-major: X^T stored [128 part, cc(8), i] ; feature f = cc*128 + p.
  - DRAM activation scratch: [1024, S] f32, row f, col i.
  - Weights W[c, k] DRAM [1024, K]: lhsT chunks [128 c, k-128] natural slices.
  - matmul: out[M, N] = lhsT.T @ rhs, lhsT = [Kc on part, M], rhs = [Kc on part, N].

RetNet layer (pass R1 = qkv/retention/gn/gate/wo, pass R2 = ln2/ffn):
  see emit_r1 / emit_r2.
"""
import numpy as np
import ml_dtypes
import concourse.bass as bass
import concourse.mybir as mybir
from concourse.alu_op_type import AluOpType as ALU

F32 = mybir.dt.float32
BF16 = mybir.dt.bfloat16
AF = mybir.ActivationFunctionType
AX = mybir.AxisListType
BF = ml_dtypes.bfloat16

D = 1024
CC = 8
BLK = 512
CH = 128
HD = 128
H = 8
EPS_LN = 1e-5
EPS_RMS = 1e-6

GAMMAS = 1.0 - np.exp(np.linspace(np.log(1.0 / 32), np.log(1.0 / 512), 8))


class Ctr:

    def __init__(self, stack, nc, name):
        self.sem = stack.enter_context(nc.semaphore(name))
        self.n = 0

    def inc(self, instr, k=1):
        instr.then_inc(self.sem, k)
        self.n += k
        return instr

    def wait(self, eng, at=None):
        eng.wait_ge(self.sem, self.n if at is None else at)


def host_consts(S):
    o = {}
    o["ident"] = np.eye(128, dtype=BF)
    o["ones"] = np.ones((128, 128), dtype=BF)
    R = np.zeros((128, 128), dtype=np.float32)
    for j in range(64):
        R[2 * j + 1, 2 * j] = -1.0
        R[2 * j, 2 * j + 1] = 1.0
    o["rotmat"] = R.astype(BF)
    cm = (np.arange(128)[None, :] >= np.arange(128)[:, None]).astype(np.float32)
    o["maskT_f32"] = cm
    o["maskT_bf"] = cm.astype(BF)
    SCALE_BASE = 512
    scale_vec = (np.arange(0, HD, 2) + 0.4 * HD) / (1.4 * HD)
    power = (np.arange(S, dtype=np.float64) / SCALE_BASE)[:, None]
    scale = scale_vec[None, :] ** power
    half = HD // 2
    inv_freq = 1.0 / 10000.0 ** (np.arange(half, dtype=np.float64) / half)
    ang = np.arange(S, dtype=np.float64)[:, None] * inv_freq[None, :]
    sin, cos = np.sin(ang), np.cos(ang)
    dup = lambda m: np.repeat(m, 2, axis=-1).astype(np.float32)
    o["qsin"] = np.ascontiguousarray(dup(sin * scale).T).astype(BF)
    o["qcos"] = np.ascontiguousarray(dup(cos * scale).T).astype(BF)
    o["ksin"] = np.ascontiguousarray(dup(sin / scale).T).astype(BF)
    o["kcos"] = np.ascontiguousarray(dup(cos / scale).T).astype(BF)
    j = np.arange(128, dtype=np.float64)
    gq = np.stack([np.broadcast_to(g**j, (128, 128)) for g in GAMMAS]).astype(np.float32)
    gk = np.stack([np.broadcast_to(g**(-j), (128, 128)) for g in GAMMAS]).astype(np.float32)
    o["gq"] = gq.astype(BF)
    o["gk"] = gk.astype(BF)
    o["gammaC"] = (GAMMAS ** 128).astype(np.float64)
    colones = np.zeros((128, 8, 8), dtype=np.float32)
    for h in range(8):
        colones[:, h, h] = 1.0
    o["colones"] = colones.astype(BF)
    selrows = np.zeros((8, 8, 128), dtype=np.float32)
    for h in range(8):
        selrows[h, h, :] = 1.0
    o["selrows"] = selrows.astype(BF)
    return o


def col8(v):
    return np.ascontiguousarray(v.reshape(8, 128).T).astype(np.float32)


def retnet_layer_weights(inp, li):
    s = lambda n: inp[f"L0_attn_{n}"][li]
    o = {}
    o[f"rwq_l{li}"] = np.ascontiguousarray(s("WQ").transpose(1, 0, 2).reshape(D, D))
    o[f"rwk_l{li}"] = np.ascontiguousarray(s("WK").transpose(1, 0, 2).reshape(D, D))
    o[f"rwv_l{li}"] = np.ascontiguousarray(s("WV").transpose(1, 0, 2).reshape(D, D))
    o[f"rwg_l{li}"] = np.ascontiguousarray(s("WG"))
    o[f"rwo_l{li}"] = np.ascontiguousarray(s("WO"))
    o[f"rw1_l{li}"] = np.ascontiguousarray(s("ffn_w1"))
    o[f"rw2_l{li}"] = np.ascontiguousarray(s("ffn_w2"))
    o[f"rln1w_l{li}"] = col8(s("ln1_w"))
    o[f"rln1b_l{li}"] = col8(s("ln1_b"))
    o[f"rln2w_l{li}"] = col8(s("ln2_w"))
    o[f"rln2b_l{li}"] = col8(s("ln2_b"))
    gsel = np.zeros((8, 8, 128), np.float32)
    gw = np.asarray(s("gn_w"))
    for h in range(8):
        gsel[h, h, :] = gw[h * 128:(h + 1) * 128]
    o[f"rgnsel_l{li}"] = gsel
    o[f"rgnb_l{li}"] = col8(s("gn_b"))
    o[f"rfb1_l{li}"] = np.ascontiguousarray(s("ffn_b1").reshape(32, 128).T).astype(np.float32)
    o[f"rfb2_l{li}"] = col8(s("ffn_b2"))
    return o


def feat_dram(ap, b, w=BLK):
    return ap.rearrange("(c p) s -> p c s", p=128)[:, :, b * w:(b + 1) * w]


class GpPool:

    def __init__(self, stack, nc, name, fresh=True):
        self.stack, self.nc, self.name = stack, nc, name
        self.fresh = fresh
        self.sems = []
        self.k = 0
        if not fresh:
            self.sem = stack.enter_context(nc.semaphore(name))

    def dma(self, g, out, in_):
        self.k += 1
        if not self.fresh:
            g.dma_start(out=out, in_=in_).then_inc(self.sem, 16)
            return
        sem = self.stack.enter_context(self.nc.semaphore(f"{self.name}{self.k}"))
        g.dma_start(out=out, in_=in_).then_inc(sem, 16)
        self.sems.append(sem)

    def wait_at(self, eng, v):
        if not self.fresh:
            eng.wait_ge(self.sem, v)
            return
        for sem in self.sems[:v // 16]:
            eng.wait_ge(sem, 16)

    def wait_all(self, eng):
        if not self.fresh:
            eng.wait_ge(self.sem, 16 * self.k)
            return
        for sem in self.sems:
            eng.wait_ge(sem, 16)




"""RetNet pass-1 v2 (ln1 + qkv + xpos + chunked retention + groupnorm + gate + WO + residual).

Emission style: per phase, ONE closure per engine with computed wait values
(the toolchain reloads the ACT table at basic-block entries: first activation
per closure pays ~1.28us).

Norms: broadcast mu and rstd (bf16 rows), x_hat = (x - mu_bc) * rstd_bc.

PSUM banks (all [128,512] f32):
  0: PJ0   1: PJ1  (projections / rot / WO)
  2: PB2   (k-transpose bf16, retention scores h even, g-bias, GN sumsq h0-3, GN bc-rho h even)
  3: PSTY  (ln1 stats rows 0/32 | retention Y h even | v-bias row | GN sums h0-3 | GN bc-mu h even)
  4: RBC   (ln1 bc-rstd | retention Y h odd | GN sums h4-7 | GN bc-mu h odd)
  5: PBC   (ln1 bc-mu | retention scores h odd | k-transpose jc odd | GN sumsq h4-7 | GN bc-rho h odd)
  6: PS0   7: PS1  (retention state, heads 0-3 / 4-7, persistent per layer)
"""


class NS:
    pass


def flat2(t):
    return t.ap().rearrange("p a b -> p (a b)")


def ss(eng, ctr, ins):
    ctr.inc(ins)
    ctr.wait(eng)


def alloc_consts(nc, stack):
    c = NS()
    sb = lambda name, shp, dt: stack.enter_context(nc.sbuf_tensor(name, shp, dt))
    c.fconst = sb("c_fconst", [8, 8], F32)
    for i, v in enumerate([1e-5, 1e-6, 1e-12, 1e-38]):
        nc.const_aps.register(F32, v, c.fconst[0:1, i:i + 1])
    c.ident = sb("c_ident", [128, 128], BF16)
    c.ones = sb("c_ones", [128, 128], BF16)
    c.rotm = sb("c_rotm", [128, 128], BF16)
    c.maskf = sb("c_maskf", [128, 128], BF16)
    c.gq = sb("c_gq", [128, H, 128], BF16)
    c.gk = sb("c_gk", [128, H, 128], BF16)
    c.colones = sb("c_colones", [128, H, H], BF16)
    c.selrows = sb("c_selrows", [H, H, 128], BF16)
    return c


def load_consts(blk, c, dram, ctr):
    def _(sync):
        ctr.inc(sync.dma_start(out=c.fconst[:], in_=dram["fconst"].ap()), 16)
        for name, dst in [("ident", c.ident), ("ones", c.ones), ("rotmat", c.rotm),
                          ("maskT_bf", c.maskf)]:
            ctr.inc(sync.dma_start(out=dst[:], in_=dram[name].ap()), 16)
        for name, dst in [("gq", c.gq), ("gk", c.gk)]:
            ctr.inc(sync.dma_start(out=dst[:], in_=dram[name].ap().rearrange("h p j -> p h j")), 16)
        ctr.inc(sync.dma_start(out=c.colones[:], in_=dram["colones"].ap()), 16)
        ctr.inc(sync.dma_start(out=c.selrows[:], in_=dram["selrows"].ap()), 16)
    blk.sync(_)


def alloc_r1(nc, stack, pfx="", w1buf=None):
    st = NS()
    sb = lambda name, shp, dt: stack.enter_context(nc.sbuf_tensor(pfx + name, shp, dt))
    st.xs = [sb("r1_xs0", [128, CC, BLK], F32), sb("r1_xs1", [128, CC, BLK], F32)]
    st.tab = sb("r1_tab0", [128, 2, BLK], BF16)
    st.xn1 = sb("r1_xn1", [128, CC, BLK], BF16)
    st.qtmp = sb("r1_qtmp0", [128, BLK], BF16)
    st.qs = sb("r1_qs", [128, H, BLK], BF16)
    st.ks = sb("r1_ks", [128, H, BLK], BF16)
    st.kst = sb("r1_kst", [128, 4, H, 128], BF16)
    st.vt = sb("r1_vt", [128, 4, D], BF16)
    st.gbf = sb("r1_gbf", [128, H, BLK], BF16)
    st.abf = [sb(f"r1_abf{i}", [128, 128], BF16) for i in range(4)]
    st.yblk = sb("r1_yblk", [128, H, BLK], BF16)
    st.rotq = st.yblk.ap()[:, 0, :]
    st.xmid = sb("r1_xmid", [128, CC, BLK], F32)
    st.sbf = [sb("r1_sbf0", [128, D], BF16), sb("r1_sbf1", [128, D], BF16)]
    st.g_mu = sb("r1_gmu", [H, BLK], BF16)
    st.g_var = sb("r1_gvar", [H, BLK], BF16)
    st.g_t = sb("r1_gt", [H, BLK], BF16)
    st.g_rstd = sb("r1_grstd", [H, BLK], BF16)
    st.rmu = st.g_mu.ap()[0:1, :]
    st.rvar = st.g_var.ap()[0:1, :]
    st.rrstd = st.g_rstd.ap()[0:1, :]
    st.rt = st.g_t
    if w1buf is not None:
        st.wq = w1buf.ap()[:, :, 0 * D:1 * D]
        st.wk = w1buf.ap()[:, :, 1 * D:2 * D]
        st.wv = w1buf.ap()[:, :, 2 * D:3 * D]
        st.wg = w1buf.ap()[:, :, 3 * D:4 * D]
    else:
        st.wq = sb("r1_wq", [128, CC, D], BF16)
        st.wk = sb("r1_wk", [128, CC, D], BF16)
        st.wv = sb("r1_wv", [128, CC, D], BF16)
        st.wg = sb("r1_wg", [128, CC, D], BF16)
    st.wo = sb("r1_wo", [128, CC, D], BF16)
    st.qb = sb("r1_qb", [128, CC], F32)
    st.kb = sb("r1_kb", [128, CC], F32)
    st.gb = sb("r1_gb", [128, CC], F32)
    st.vbrow = sb("r1_vbrow", [1, D], BF16)
    st.lnw = sb("r1_lnw", [128, CC], F32)
    st.lnbb = sb("r1_lnbb", [128, CC], BF16)
    st.gnsel = sb("r1_gnsel", [8, 8, 128], BF16)
    st.gnb = sb("r1_gnb", [128, CC], F32)
    st.xbf = st.qs
    st.sq = st.ks
    st.zbf = st.xn1
    st.xn1f = st.xmid
    ps = lambda name, shp, dt: stack.enter_context(nc.psum_tensor(pfx + name, shp, dt))
    st.PJ = [ps("r1_pj0", [128, BLK], F32), ps("r1_pj1", [128, BLK], F32)]
    st.PB2 = ps("r1_pb2", [128, BLK], F32)
    st.PSTY = ps("r1_psty", [128, BLK], F32)
    st.RBC = ps("r1_rbc", [128, BLK], F32)
    st.PBC = ps("r1_pbc", [128, BLK], F32)
    st.PS = [ps("r1_ps0", [128, BLK], F32), ps("r1_ps1", [128, BLK], F32)]
    st.pb2bf = st.PB2.ap().bitcast(BF16)
    st.pbcbf = st.PBC.ap().bitcast(BF16)
    return st


R1_CTRS = ["ssv", "sss", "cld", "wld", "xld0", "xld1", "tld", "bmm", "bevac", "fold",
           "nmm", "nrow", "nact", "nbc", "nxn",
           "pj", "pjev", "rot", "rotev", "xpos", "ktr", "ktrev",
           "sc", "msk", "ymm", "yev", "umm", "sts", "sbf",
           "gnz", "womm", "wadd", "xout"]


def emit_norm_rows(blk, st, Ct, n, eps, rms=False, width=BLK):
    inv = 1.0 / n
    MU, VAR, T = st.rmu[0:1, 0:width], st.rvar[0:1, 0:width], st.rt[0:1, 0:width]
    RSTD = st.rrstd[0:1, 0:width]

    def _(v):
        Ct["nmm"].wait(v)
        if not rms:
            v.tensor_scalar(out=MU, in0=st.PSTY[0:1, 0:width], scalar1=inv, scalar2=None, op0=ALU.mult)
            ss(v, Ct["ssv"], v.tensor_scalar(out=VAR, in0=st.PSTY[32:33, 0:width], scalar1=inv, scalar2=None, op0=ALU.mult))
            ss(v, Ct["ssv"], v.tensor_tensor(out=T, in0=MU, in1=MU, op=ALU.mult))
            ins = v.tensor_tensor(out=VAR, in0=VAR, in1=T, op=ALU.subtract)
        else:
            ins = v.tensor_scalar(out=VAR, in0=st.PSTY[32:33, 0:width], scalar1=inv, scalar2=None, op0=ALU.mult)
        Ct["nrow"].inc(ins)
    blk.vector(_)

    def _(s):
        Ct["nrow"].wait(s)
        ss(s, Ct["sss"], s.activation(out=T, in_=VAR, func=AF.Ln, bias=float(eps), scale=1.0))
        ins = s.activation(out=RSTD, in_=T, func=AF.Exp, bias=0.0, scale=-0.5)
        Ct["nact"].inc(ins)
    blk.scalar(_)


def emit_r1(nc, blk, st, c, dram, Ct, li, S, in32, out32):
    NB = S // BLK
    gp = Ct["gp"]
    marks = {}

    def _(g):
        Ct["womm"].wait(g)
        Ct["bmm"].wait(g)
        if "wcast" in Ct:
            Ct["wcast"].wait(g)
        for wname, dst in [(f"rwq_l{li}", st.wq), (f"rwk_l{li}", st.wk),
                           (f"rwv_l{li}", st.wv), (f"rwg_l{li}", st.wg),
                           (f"rwo_l{li}", st.wo)]:
            if wname + "_bf" in dram:
                wname = wname + "_bf"
            gp.dma(g, dst[:], dram[wname].ap().rearrange("(c p) k -> p c k", p=128))
        gp.dma(g, st.lnbb[:], dram[f"rln1b_l{li}"].ap())
        gp.dma(g, st.gnsel[:], dram[f"rgnsel_l{li}"].ap())
    blk.gpsimd(_)

    def _(sync):
        for vname, dst in [(f"rln1w_l{li}", st.lnw),
                           (f"rgnb_l{li}", st.gnb)]:
            Ct["wld"].inc(sync.dma_start(out=dst[:], in_=dram[vname].ap()), 16)
        Ct["wadd"].wait(sync)
        Ct["xpos"].wait(sync)
        Ct["xld0"].inc(sync.dma_start(out=st.xs[0][:], in_=feat_dram(in32, 0)), 16)
        for ti, tn in enumerate(["qcos", "qsin"]):
            Ct["tld"].inc(sync.dma_start(out=st.tab[:, ti, :], in_=dram[tn].ap()[:, 0:BLK]), 16)
    blk.sync(_)

    def _(t):
        gp.wait_all(t)
        Ct["wld"].wait(t)
        Ct["cld"].wait(t)
        Ct["wadd"].wait(t)
        Ct["sbf"].wait(t)
        for kc in range(CC):
            for cc in range(CC):
                t.matmul(st.PJ[0][:, kc:kc + 1], st.wq[:, cc, kc * 128:(kc + 1) * 128],
                         st.lnbb[:, cc:cc + 1], start=(cc == 0), stop=(cc == CC - 1))
            for cc in range(CC):
                t.matmul(st.PJ[1][:, kc:kc + 1], st.wk[:, cc, kc * 128:(kc + 1) * 128],
                         st.lnbb[:, cc:cc + 1], start=(cc == 0), stop=(cc == CC - 1))
            for cc in range(CC):
                ins = t.matmul(st.PB2[:, kc:kc + 1], st.wg[:, cc, kc * 128:(kc + 1) * 128],
                               st.lnbb[:, cc:cc + 1], start=(cc == 0), stop=(cc == CC - 1))
        for cc in range(CC):
            ins = t.matmul(st.PSTY[0:1, 0:BLK], st.lnbb[:, cc:cc + 1], st.wv[:, cc, 0:BLK],
                           start=(cc == 0), stop=(cc == CC - 1))
        Ct["bmm"].inc(ins)
    blk.tensor(_)

    def _(s):
        Ct["bmm"].wait(s)
        s.copy(out=st.qb[:], in_=st.PJ[0][:, 0:CC])
        s.copy(out=st.kb[:], in_=st.PJ[1][:, 0:CC])
        s.copy(out=st.gb[:], in_=st.PB2[:, 0:CC])
        ins = s.copy(out=st.vbrow[0:1, 0:BLK], in_=st.PSTY[0:1, 0:BLK])
        Ct["bevac"].inc(ins)
    blk.scalar(_)

    def _(t):
        Ct["bevac"].wait(t)
        for cc in range(CC):
            ins = t.matmul(st.PSTY[0:1, 0:BLK], st.lnbb[:, cc:cc + 1], st.wv[:, cc, BLK:D],
                           start=(cc == 0), stop=(cc == CC - 1))
        Ct["bmm"].inc(ins)
    blk.tensor(_)

    def _(s):
        Ct["bmm"].wait(s)
        ins = s.copy(out=st.vbrow[0:1, BLK:D], in_=st.PSTY[0:1, 0:BLK])
        Ct["bevac"].inc(ins)
    blk.scalar(_)

    def _(v):
        Ct["bevac"].wait(v)
        for w in [st.wq, st.wk, st.wv, st.wg]:
            for cc in range(CC):
                ins = v.tensor_scalar(out=w[:, cc, :], in0=w[:, cc, :],
                                      scalar1=st.lnw[:, cc:cc + 1], scalar2=None, op0=ALU.mult)
        Ct["fold"].inc(ins)
    blk.vector(_)

    for b in range(NB):
        par = b % 2

        if b + 1 < NB:
            def _(sync, b=b):
                if b >= 1:
                    Ct["wadd"].wait(sync, at=marks[("xfree", b - 1)])
                Ct["xld" + str((b + 1) % 2)].inc(
                    sync.dma_start(out=st.xs[(b + 1) % 2][:], in_=feat_dram(in32, b + 1)), 16)
            blk.sync(_)
        if b >= 1:
            def _(sync, b=b):
                Ct["xpos"].wait(sync, at=marks[("tfree", b)])
                for ti, tn in enumerate(["qcos", "qsin"]):
                    Ct["tld"].inc(sync.dma_start(out=st.tab[:, ti, :],
                                                 in_=dram[tn].ap()[:, b * BLK:(b + 1) * BLK]), 16)
            blk.sync(_)

        def _(s, b=b, par=par):
            xc = Ct["xld" + str(par)]
            s.wait_ge(xc.sem, xc.n)
            if b == 0:
                Ct["fold"].wait(s)
            if ("gnmm", b - 1) in marks:
                Ct["nmm"].wait(s, at=marks[("gnmm", b - 1)])
            ss(s, Ct["sss"], s.copy(out=flat2(st.xbf), in_=flat2(st.xs[par])))
            ins = s.square(out=flat2(st.sq), in_=flat2(st.xbf))
            Ct["nact"].inc(ins)
        blk.scalar(_)

        def _(t):
            Ct["nact"].wait(t)
            Ct["nrow"].wait(t)
            for cc in range(CC):
                t.matmul(st.PSTY[0:1, :], c.ones[:, 0:1], st.xbf[:, cc, :],
                         start=(cc == 0), stop=(cc == CC - 1))
            for cc in range(CC):
                ins = t.matmul(st.PSTY[32:33, :], c.ones[:, 0:1], st.sq[:, cc, :],
                               start=(cc == 0), stop=(cc == CC - 1), tile_position=(0, 32))
            Ct["nmm"].inc(ins)
        blk.tensor(_)

        emit_norm_rows(blk, st, Ct, D, EPS_LN)

        def _(t):
            Ct["nact"].wait(t)
            Ct["nrow"].wait(t)
            Ct["nxn"].wait(t)
            Ct["gnz"].wait(t)
            Ct["yev"].wait(t)
            Ct["msk"].wait(t)
            t.matmul(st.RBC[:, :], c.ones[0:1, 0:128], st.rrstd[0:1, :], start=True, stop=True)
            ins = t.matmul(st.PBC[:, :], c.ones[0:1, 0:128], st.rmu[0:1, :], start=True, stop=True)
            Ct["nbc"].inc(ins)
        blk.tensor(_)

        def _(v, par=par, b=b):
            Ct["nbc"].wait(v)
            v.wait_ge(Ct["xout"].sem, 16 * (li * NB + b))
            rb = st.RBC.ap().unsqueeze(1).broadcast_to((128, CC, BLK))
            mb = st.PBC.ap().unsqueeze(1).broadcast_to((128, CC, BLK))
            ss(v, Ct["ssv"], v.tensor_tensor(out=st.xn1f[:], in0=st.xs[par][:], in1=mb, op=ALU.subtract))
            ins = v.tensor_tensor(out=st.xn1[:], in0=st.xn1f[:], in1=rb, op=ALU.mult)
            Ct["nxn"].inc(ins)
        blk.vector(_)
        marks[("nxn", b)] = Ct["nxn"].n

        for side, wmat, bcol, gtile, dstbuf in [
                ("q", st.wq, st.qb, c.gq, st.qs),
                ("k", st.wk, st.kb, c.gk, st.ks)]:
            if side == "k":
                def _(sync, b=b):
                    Ct["xpos"].wait(sync)
                    for ti, tn in enumerate(["kcos", "ksin"]):
                        Ct["tld"].inc(sync.dma_start(out=st.tab[:, ti, :],
                                                     in_=dram[tn].ap()[:, b * BLK:(b + 1) * BLK]), 16)
                blk.sync(_)
            pj0, pjev0, rot0, rotev0, xpos0 = (Ct["pj"].n, Ct["pjev"].n, Ct["rot"].n,
                                               Ct["rotev"].n, Ct["xpos"].n)
            tld_tgt = Ct["tld"].n

            prjb = lambda kc: [st.PJ[0], st.PJ[1], st.PB2][kc % 3]
            rotb = lambda kc: [st.PSTY, st.RBC, st.PBC][kc % 3]

            def _(t, b=b, wmat=wmat, pjev0=pjev0, rotev0=rotev0):
                Ct["nxn"].wait(t, at=marks[("nxn", b)])
                Ct["wadd"].wait(t, at=marks.get("pjW", 0))
                for kc in range(CC):
                    Ct["pjev"].wait(t, at=max(pjev0, pjev0 + kc - 2))
                    for cc in range(CC):
                        ins = t.matmul(prjb(kc)[:, :], wmat[:, cc, kc * 128:(kc + 1) * 128],
                                       st.xn1[:, cc, :], start=(cc == 0), stop=(cc == CC - 1))
                    Ct["pj"].inc(ins)
                    if kc >= 1:
                        Ct["pjev"].wait(t, at=pjev0 + kc)
                        Ct["rotev"].wait(t, at=max(rotev0, rotev0 + kc - 3))
                        ins = t.matmul(rotb(kc - 1)[:, :], c.rotm[:], st.qtmp[:],
                                       start=True, stop=True)
                        Ct["rot"].inc(ins)
                Ct["pjev"].wait(t, at=pjev0 + CC)
                Ct["rotev"].wait(t, at=max(rotev0, rotev0 + CC - 3))
                ins = t.matmul(rotb(CC - 1)[:, :], c.rotm[:], st.qtmp[:],
                               start=True, stop=True)
                Ct["rot"].inc(ins)
            blk.tensor(_)

            def _(s, bcol=bcol, pj0=pj0, rot0=rot0, xpos0=xpos0):
                for kc in range(CC):
                    s.wait_ge(Ct["pj"].sem, pj0 + kc + 1)
                    s.wait_ge(Ct["rot"].sem, max(0, rot0 + kc))
                    s.wait_ge(Ct["xpos"].sem, max(0, xpos0 + kc))
                    ins = s.activation(out=st.qtmp[:], in_=prjb(kc)[:, :],
                                       func=AF.Identity, bias=bcol[:, kc:kc + 1], scale=1.0)
                    Ct["pjev"].inc(ins)
            blk.scalar(_)

            def _(v, gtile=gtile, dstbuf=dstbuf, pjev0=pjev0, rot0=rot0, tld_tgt=tld_tgt):
                v.wait_ge(Ct["tld"].sem, tld_tgt)
                for kc in range(CC):
                    v.wait_ge(Ct["pjev"].sem, pjev0 + kc + 1)
                    v.wait_ge(Ct["rot"].sem, rot0 + kc + 1)
                    ss(v, Ct["ssv"], v.tensor_tensor(out=st.qtmp[:], in0=st.qtmp[:],
                                                     in1=st.tab[:, 0, :], op=ALU.mult))
                    ins = v.tensor_tensor(out=st.rotq[:], in0=rotb(kc)[:, :],
                                          in1=st.tab[:, 1, :], op=ALU.mult)
                    Ct["rotev"].inc(ins)
                    Ct["rotev"].wait(v)
                    ss(v, Ct["ssv"], v.tensor_tensor(out=st.qtmp[:], in0=st.qtmp[:], in1=st.rotq[:], op=ALU.add))
                    g4 = gtile.ap()[:, kc, :].unsqueeze(1).broadcast_to((128, 4, 128))
                    t3v = st.qtmp.ap().rearrange("p (j i) -> p j i", j=4)
                    dv = dstbuf.ap()[:, kc, :].rearrange("p (j i) -> p j i", j=4)
                    ins = v.tensor_tensor(out=dv, in0=t3v, in1=g4, op=ALU.mult)
                    Ct["xpos"].inc(ins)
            blk.vector(_)
        marks[("tfree", b + 1)] = Ct["xpos"].n

        ktr0, ktrev0 = Ct["ktr"].n, Ct["ktrev"].n

        def _(t, ktrev0=ktrev0):
            Ct["xpos"].wait(t)
            Ct["nxn"].wait(t)
            i = 0
            for kc in range(CC):
                for jc in range(4):
                    tgt = st.pb2bf if jc % 2 == 0 else st.pbcbf
                    if i >= 2:
                        Ct["ktrev"].wait(t, at=ktrev0 + i - 1)
                    ins = t.transpose(tgt[:, 0:128], st.ks[:, kc, jc * 128:(jc + 1) * 128], c.ident[:])
                    Ct["ktr"].inc(ins)
                    i += 1
        blk.tensor(_)

        def _(v, ktr0=ktr0):
            i = 0
            for kc in range(CC):
                for jc in range(4):
                    tgt = st.pb2bf if jc % 2 == 0 else st.pbcbf
                    v.wait_ge(Ct["ktr"].sem, ktr0 + i + 1)
                    ins = v.tensor_copy(out=st.kst[:, jc, kc, :], in_=tgt[:, 0:128])
                    Ct["ktrev"].inc(ins)
                    i += 1
        blk.vector(_)

        pjv0, pjevv0 = Ct["pj"].n, Ct["pjev"].n

        def _(t, pjevv0=pjevv0):
            Ct["rotev"].wait(t)
            Ct["bevac"].wait(t)
            i = 0
            for jc in range(4):
                for half in range(2):
                    if i >= 2:
                        Ct["pjev"].wait(t, at=pjevv0 + i - 1)
                    for cc in range(CC):
                        t.matmul(st.PJ[i % 2][:, :], st.xn1[:, cc, jc * 128:(jc + 1) * 128],
                                 st.wv[:, cc, half * BLK:(half + 1) * BLK],
                                 start=(cc == 0), stop=False)
                    ins = t.matmul(st.PJ[i % 2][:, :], c.ones[0:1, 0:128],
                                   st.vbrow[0:1, half * BLK:(half + 1) * BLK], start=False, stop=True)
                    Ct["pj"].inc(ins)
                    i += 1
            for kc in range(CC):
                Ct["pjev"].wait(t, at=pjevv0 + i - 1)
                for cc in range(CC):
                    ins = t.matmul(st.PJ[i % 2][:, :], st.wg[:, cc, kc * 128:(kc + 1) * 128],
                                   st.xn1[:, cc, :], start=(cc == 0), stop=(cc == CC - 1))
                Ct["pj"].inc(ins)
                i += 1
        blk.tensor(_)

        def _(s, pjv0=pjv0):
            i = 0
            for jc in range(4):
                for half in range(2):
                    s.wait_ge(Ct["pj"].sem, pjv0 + i + 1)
                    ins = s.copy(out=st.vt[:, jc, half * BLK:(half + 1) * BLK], in_=st.PJ[i % 2][:, :])
                    Ct["pjev"].inc(ins)
                    i += 1
            for kc in range(CC):
                s.wait_ge(Ct["pj"].sem, pjv0 + i + 1)
                ins = s.activation(out=st.gbf[:, kc, :], in_=st.PJ[i % 2][:, :],
                                   func=AF.Silu, bias=st.gb[:, kc:kc + 1], scale=1.0)
                Ct["pjev"].inc(ins)
                i += 1
        blk.scalar(_)

        for ci in range(4):
            first = (b == 0 and ci == 0)
            gci = b * 4 + ci
            sc0, msk0, ymm0, yev0 = Ct["sc"].n, Ct["msk"].n, Ct["ymm"].n, Ct["yev"].n

            scob = lambda h: [st.PB2, st.PBC, st.PJ[0]][h % 3]
            yslb = lambda h: [st.PSTY, st.RBC, st.PJ[1]][h % 3]

            def _(t, ci=ci, gci=gci, first=first, msk0=msk0, yev0=yev0):
                if ci == 0:
                    Ct["xpos"].wait(t)
                    Ct["pjev"].wait(t)
                    Ct["ktrev"].wait(t)
                    Ct["nbc"].wait(t)
                Ct["msk"].wait(t, at=msk0)

                def scores(h):
                    ins = t.matmul(scob(h)[:, 0:128], st.ks[:, h, ci * 128:(ci + 1) * 128],
                                   st.qs[:, h, ci * 128:(ci + 1) * 128], start=True, stop=True)
                    Ct["sc"].inc(ins)

                for h in range(3):
                    scores(h)
                Ct["sbf"].wait(t)
                for h in range(H):
                    ins = t.matmul(st.PS[h // 4][:, (h % 4) * 128:(h % 4 + 1) * 128],
                                   st.kst[:, ci, h, :], st.vt[:, ci, h * 128:(h + 1) * 128],
                                   start=first, stop=True, skip_group_check=True)
                Ct["umm"].inc(ins)
                Ct["yev"].wait(t, at=yev0)
                for h in range(H):
                    ysl = yslb(h)[:, 0:128]
                    Ct["msk"].wait(t, at=msk0 + h + 1)
                    if h >= 3:
                        Ct["yev"].wait(t, at=yev0 + h - 2)
                    if first:
                        ins = t.matmul(ysl, st.vt[:, ci, h * 128:(h + 1) * 128], st.abf[h % 4][:],
                                       start=True, stop=True)
                    else:
                        t.matmul(ysl, st.vt[:, ci, h * 128:(h + 1) * 128], st.abf[h % 4][:],
                                 start=True, stop=False)
                        ins = t.matmul(ysl, st.sbf[(gci - 1) % 2][:, h * 128:(h + 1) * 128],
                                       st.qs[:, h, ci * 128:(ci + 1) * 128], start=False, stop=True)
                    Ct["ymm"].inc(ins)
                    if h + 3 < H:
                        scores(h + 3)
            blk.tensor(_)

            def _(v, sc0=sc0, ymm0=ymm0):
                def mask(h):
                    v.wait_ge(Ct["sc"].sem, sc0 + h + 1)
                    if h >= 4:
                        v.wait_ge(Ct["ymm"].sem, ymm0 + h - 3)
                    ins = v.tensor_tensor(out=st.abf[h % 4][:], in0=scob(h)[:, 0:128],
                                          in1=c.maskf[:], op=ALU.mult)
                    Ct["msk"].inc(ins)

                for h in range(3):
                    mask(h)
                Ct["umm"].wait(v)
                for h in range(H):
                    ins = v.tensor_scalar(out=st.PS[h // 4][:, (h % 4) * 128:(h % 4 + 1) * 128],
                                          in0=st.PS[h // 4][:, (h % 4) * 128:(h % 4 + 1) * 128],
                                          scalar1=float(GAMMAS[h] ** 128), scalar2=None, op0=ALU.mult)
                Ct["sts"].inc(ins)
                for h in range(3, H):
                    mask(h)
            blk.vector(_)

            def _(s, ci=ci, gci=gci, ymm0=ymm0):
                Ct["sts"].wait(s)
                s.copy(out=st.sbf[gci % 2][:, 0:BLK], in_=st.PS[0][:, :])
                ins = s.copy(out=st.sbf[gci % 2][:, BLK:D], in_=st.PS[1][:, :])
                Ct["sbf"].inc(ins)
                for h in range(H):
                    s.wait_ge(Ct["ymm"].sem, ymm0 + h + 1)
                    ins = s.copy(out=st.yblk[:, h, ci * 128:(ci + 1) * 128], in_=yslb(h)[:, 0:128])
                    Ct["yev"].inc(ins)
            blk.scalar(_)

        def _(s):
            Ct["yev"].wait(s)
            ins = s.square(out=flat2(st.sq), in_=flat2(st.yblk))
            Ct["nact"].inc(ins)
        blk.scalar(_)

        def _(t):
            Ct["nact"].wait(t)
            Ct["nrow"].wait(t)
            Ct["msk"].wait(t)
            Ct["gnz"].wait(t)
            Ct["nxn"].wait(t)
            for h in range(H):
                t.matmul(st.PSTY[0:8, :], c.colones[:, h, :], st.yblk[:, h, :],
                         start=(h == 0), stop=(h == H - 1), skip_group_check=True)
            for h in range(H):
                ins = t.matmul(st.RBC[0:8, :], c.colones[:, h, :], st.sq[:, h, :],
                               start=(h == 0), stop=(h == H - 1), skip_group_check=True)
            Ct["nmm"].inc(ins)
        blk.tensor(_)
        marks[("gnmm", b)] = Ct["nmm"].n

        def _(v):
            Ct["nmm"].wait(v)
            inv = 1.0 / HD
            v.tensor_scalar(out=st.g_mu[:, :], in0=st.PSTY[0:8, :], scalar1=inv,
                            scalar2=None, op0=ALU.mult)
            ss(v, Ct["ssv"], v.tensor_scalar(out=st.g_var[:, :], in0=st.RBC[0:8, :],
                                             scalar1=inv, scalar2=None, op0=ALU.mult))
            ss(v, Ct["ssv"], v.tensor_tensor(out=st.g_t[:, :], in0=st.g_mu[:, :],
                                             in1=st.g_mu[:, :], op=ALU.mult))
            ins = v.tensor_tensor(out=st.g_var[:, :], in0=st.g_var[:, :],
                                  in1=st.g_t[:, :], op=ALU.subtract)
            Ct["nrow"].inc(ins)
        blk.vector(_)
        marks[("gnrows", b)] = Ct["nrow"].n

        def _(s, b=b):
            s.wait_ge(Ct["nrow"].sem, marks[("gnrows", b)])
            ss(s, Ct["sss"], s.activation(out=st.g_t[:, :], in_=st.g_var[:, :],
                                          func=AF.Ln, bias=c.fconst[0:8, 0:1], scale=1.0))
            ins = s.activation(out=st.g_rstd[:, :], in_=st.g_t[:, :],
                               func=AF.Exp, bias=0.0, scale=-0.5)
            Ct["nact"].inc(ins)
        blk.scalar(_)
        marks[("gnact", b)] = Ct["nact"].n

        gnbc0, gnz0 = Ct["nbc"].n, Ct["gnz"].n

        gmub = lambda h: [st.PSTY, st.RBC, st.PJ[0]][h % 3][:, :]
        grsb = lambda h: [st.PB2, st.PBC, st.PJ[1]][h % 3][:, :]

        def _(t, b=b, gnz0=gnz0):
            Ct["nact"].wait(t, at=marks[("gnact", b)])
            Ct["nxn"].wait(t)
            for h in range(H):
                if h >= 3:
                    Ct["gnz"].wait(t, at=gnz0 + h - 2)
                t.matmul(gmub(h), c.selrows[:, h, :], st.g_mu[:, :], start=True, stop=True)
                ins = t.matmul(grsb(h), st.gnsel[:, h, :], st.g_rstd[:, :], start=True, stop=True)
                Ct["nbc"].inc(ins)
        blk.tensor(_)

        def _(v, gnbc0=gnbc0, gnz0=gnz0):
            Ct["pjev"].wait(v)
            ytp = st.xmid.ap()[:, 0, :]
            for h in range(H):
                v.wait_ge(Ct["nbc"].sem, gnbc0 + h + 1)
                if h >= 1:
                    v.wait_ge(Ct["gnz"].sem, gnz0 + h)
                ss(v, Ct["ssv"], v.tensor_tensor(out=ytp, in0=st.yblk[:, h, :], in1=gmub(h),
                                                 op=ALU.subtract))
                ss(v, Ct["ssv"], v.tensor_tensor(out=ytp, in0=ytp, in1=grsb(h), op=ALU.mult))
                ins = v.scalar_tensor_tensor(out=st.zbf[:, h, :], in0=ytp,
                                             scalar=st.gnb[:, h:h + 1], in1=st.gbf[:, h, :],
                                             op0=ALU.add, op1=ALU.mult)
                Ct["gnz"].inc(ins)
        blk.vector(_)

        womm0, wadd0 = Ct["womm"].n, Ct["wadd"].n

        def _(t, wadd0=wadd0):
            Ct["gnz"].wait(t)
            Ct["pjev"].wait(t)
            for kc in range(CC):
                if kc >= 2:
                    Ct["wadd"].wait(t, at=wadd0 + kc - 1)
                for cc in range(CC):
                    ins = t.matmul(st.PJ[kc % 2][:, :], st.wo[:, cc, kc * 128:(kc + 1) * 128],
                                   st.zbf[:, cc, :], start=(cc == 0), stop=(cc == CC - 1))
                Ct["womm"].inc(ins)
        blk.tensor(_)

        def _(v, par=par, womm0=womm0):
            for kc in range(CC):
                v.wait_ge(Ct["womm"].sem, womm0 + kc + 1)
                ins = v.tensor_tensor(out=st.xmid[:, kc, :], in0=st.PJ[kc % 2][:, :],
                                      in1=st.xs[par][:, kc, :], op=ALU.add)
                Ct["wadd"].inc(ins)
        blk.vector(_)
        marks["pjW"] = Ct["wadd"].n
        marks[("xfree", b)] = Ct["wadd"].n

        def _(s, b=b):
            Ct["wadd"].wait(s)
            Ct["xout"].inc(s.dma_start(out=feat_dram(out32, b), in_=st.xmid[:]), 16)
        blk.scalar(_)

    def _(s):
        s.wait_ge(Ct["xout"].sem, 16 * (li + 1) * NB)
    blk.scalar(_)




"""RetNet pass-2: ln2 + FFN (4096 hidden, silu) + residual [+ final combine].

PSUM banks: 0: PH (ffn1, two 256-halves)  1-4: POUT (8 oc slices, [128,256] each)
            5: PSTY (stats rows)  6: RBC  7: PBC
Sub-block width 256 (SW), 2 subs per 512-block.
"""

SW = 256
FC = 32


def alloc_r2(nc, stack, final, pfx="", w1buf=None):
    st = NS()
    sb = lambda name, shp, dt: stack.enter_context(nc.sbuf_tensor(pfx + name, shp, dt))
    st.w1 = w1buf if w1buf is not None else sb("r2_w1", [128, CC, 4096], BF16)
    st.w2 = sb("r2_w2", [128, FC, D], BF16)
    st.xin = [sb("r2_xin0", [128, CC, SW], F32), sb("r2_xin1", [128, CC, SW], F32)]
    st.xbf = sb("r2_xbf", [128, CC, SW], BF16)
    st.sq = sb("r2_sq", [128, CC, SW], BF16)
    st.xn2b = [sb("r2_xn2a", [128, CC, SW], BF16), sb("r2_xn2b", [128, CC, SW], BF16)]
    st.hbf = [sb(f"r2_hbf{i}", [128, SW], BF16) for i in range(3)]
    st.xnext = [sb("r2_xn0", [128, CC, SW], F32), sb("r2_xn1", [128, CC, SW], F32)]
    st.rmu = sb("r2_rmu", [1, SW], BF16)
    st.rvar = sb("r2_rvar", [1, SW], BF16)
    st.rt = sb("r2_rt", [1, SW], BF16)
    st.rrstd = sb("r2_rrstd", [1, SW], BF16)
    st.fbias = sb("r2_fbias", [128, FC], F32)
    st.fb1 = sb("r2_fb1", [128, FC], F32)
    st.fb2 = sb("r2_fb2", [128, CC], F32)
    st.lnw = sb("r2_lnw", [128, CC], F32)
    st.lnbb = sb("r2_lnbb", [128, CC], BF16)
    if final:
        st.x0s = sb("r2_x0s", [128, CC, SW], F32)
        st.bts = sb("r2_bts", [128, CC, SW], F32)
    ps = lambda name, shp, dt: stack.enter_context(nc.psum_tensor(pfx + name, shp, dt))
    st.PHB = [ps("r2_ph0", [128, 2 * SW], F32), ps("r2_ph1", [128, 2 * SW], F32)]
    st.PO = [ps(f"r2_po{i}", [128, 2 * SW], F32) for i in range(4)]
    st.PSTY = ps("r2_psty", [128, 2 * SW], F32)
    st.RP = ps("r2_rp", [128, 2 * SW], F32)
    return st


R2_CTRS = ["wld2", "xld20", "xld21", "fbmm", "fbev", "fold2",
           "f1", "silu", "f2", "oev", "xout20", "xout21", "fx0", "fbt"]


def emit_r2(nc, blk, st, c, dram, Ct, li, S, in32, out32, final, x0T=None, b32=None):
    NB = S // BLK
    NSUB = S // SW
    gp = Ct["gp"]
    marks = {}

    def _(g):
        Ct["pjev"].wait(g)
        if "wcast" in Ct:
            Ct["wcast"].wait(g)
        w1n = f"rw1_l{li}_bf" if f"rw1_l{li}_bf" in dram else f"rw1_l{li}"
        gp.dma(g, st.w1[:], dram[w1n].ap().rearrange("(c p) k -> p c k", p=128))
        Ct["wadd"].wait(g)
        Ct["xout"].wait(g)
        gp.dma(g, st.lnbb[:], dram[f"rln2b_l{li}"].ap())
        st.w1mark = 16 * gp.k
        w2n = f"rw2_l{li}_bf" if f"rw2_l{li}_bf" in dram else f"rw2_l{li}"
        gp.dma(g, st.w2[:], dram[w2n].ap().rearrange("(f p) o -> p f o", p=128))
        st.w2mark = 16 * gp.k
    blk.gpsimd(_)

    def _(sync):
        Ct["wld2"].inc(sync.dma_start(out=st.lnw[:], in_=dram[f"rln2w_l{li}"].ap()), 16)
        Ct["wld2"].inc(sync.dma_start(out=st.fb1[:], in_=dram[f"rfb1_l{li}"].ap()), 16)
        Ct["wld2"].inc(sync.dma_start(out=st.fb2[:], in_=dram[f"rfb2_l{li}"].ap()), 16)
        Ct["xld20"].inc(sync.dma_start(out=st.xin[0][:], in_=feat_dram(in32, 0, SW)), 16)
    blk.sync(_)

    def _(t):
        gp.wait_at(t, st.w1mark)
        Ct["wld2"].wait(t)
        Ct["cld"].wait(t)
        Ct["wadd"].wait(t)
        for fc in range(FC):
            for cc in range(CC):
                ins = t.matmul(st.PHB[0][:, fc:fc + 1], st.w1[:, cc, fc * 128:(fc + 1) * 128],
                               st.lnbb[:, cc:cc + 1], start=(cc == 0), stop=(cc == CC - 1))
        Ct["fbmm"].inc(ins)
    blk.tensor(_)

    def _(s):
        Ct["fbmm"].wait(s)
        for fc in range(FC):
            ins = s.activation(out=st.fbias[:, fc:fc + 1], in_=st.PHB[0][:, fc:fc + 1],
                               func=AF.Identity, bias=st.fb1[:, fc:fc + 1], scale=1.0)
        Ct["fbev"].inc(ins)
    blk.scalar(_)

    def _(v):
        Ct["fbev"].wait(v)
        for cc in range(CC):
            ins = v.tensor_scalar(out=st.w1[:, cc, :], in0=st.w1[:, cc, :],
                                  scalar1=st.lnw[:, cc:cc + 1], scalar2=None, op0=ALU.mult)
        Ct["fold2"].inc(ins)
    blk.vector(_)

    M = {}

    def stats_scalar(s, sub):
        par = sub % 2
        xc = Ct["xld2" + str(par)]
        s.wait_ge(xc.sem, xc.n)
        if sub == 0:
            Ct["fold2"].wait(s)
        if sub >= 1:
            Ct["nmm"].wait(s, at=M[(sub - 1, "nmm")])
        ss(s, Ct["sss"], s.copy(out=flat2(st.xbf), in_=flat2(st.xin[par])))
        ins = s.square(out=flat2(st.sq), in_=flat2(st.xbf))
        Ct["nact"].inc(ins)
        assert M[(sub, "nactsq")] == Ct["nact"].n, (sub, "nactsq")

    def stats_mm(t, sub):
        Ct["nact"].wait(t, at=M[(sub, "nactsq")])
        if sub >= 1:
            Ct["nxn"].wait(t, at=M[(sub - 1, "nxn")])
        for cc in range(CC):
            t.matmul(st.PSTY[0:1, 0:SW], c.ones[:, 0:1], st.xbf[:, cc, :],
                     start=(cc == 0), stop=(cc == CC - 1))
        for cc in range(CC):
            ins = t.matmul(st.PSTY[32:33, 0:SW], c.ones[:, 0:1], st.sq[:, cc, :],
                           start=(cc == 0), stop=(cc == CC - 1), tile_position=(0, 32))
        Ct["nmm"].inc(ins)
        assert M[(sub, "nmm")] == Ct["nmm"].n, (sub, "nmm")

    def rows_dve(v, sub):
        inv = 1.0 / D
        MU, VAR, T = st.rmu[0:1, 0:SW], st.rvar[0:1, 0:SW], st.rt[0:1, 0:SW]
        Ct["nmm"].wait(v, at=M[(sub, "nmm")])
        if sub >= 1:
            Ct["nbc"].wait(v, at=M[(sub - 1, "nbc")])
        v.tensor_scalar(out=MU, in0=st.PSTY[0:1, 0:SW], scalar1=inv, scalar2=None, op0=ALU.mult)
        ss(v, Ct["ssv"], v.tensor_scalar(out=VAR, in0=st.PSTY[32:33, 0:SW], scalar1=inv,
                                         scalar2=None, op0=ALU.mult))
        ss(v, Ct["ssv"], v.tensor_tensor(out=T, in0=MU, in1=MU, op=ALU.mult))
        ins = v.tensor_tensor(out=VAR, in0=VAR, in1=T, op=ALU.subtract)
        Ct["nrow"].inc(ins)
        assert M[(sub, "nrow")] == Ct["nrow"].n, (sub, "nrow")

    def rows_act(s, sub):
        T = st.rt[0:1, 0:SW]
        Ct["nrow"].wait(s, at=M[(sub, "nrow")])
        ss(s, Ct["sss"], s.activation(out=T, in_=st.rvar[0:1, 0:SW], func=AF.Ln,
                                      bias=EPS_LN, scale=1.0))
        ins = s.activation(out=st.rrstd[0:1, 0:SW], in_=T, func=AF.Exp, bias=0.0, scale=-0.5)
        Ct["nact"].inc(ins)
        assert M[(sub, "nactr")] == Ct["nact"].n, (sub, "nactr")

    def bc_mm(t, sub):
        Ct["nact"].wait(t, at=M[(sub, "nactr")])
        Ct["nrow"].wait(t, at=M[(sub, "nrow")])
        t.matmul(st.RP[:, 0:SW], c.ones[0:1, 0:128], st.rrstd[0:1, :], start=True, stop=True)
        ins = t.matmul(st.PSTY[:, 0:SW], c.ones[0:1, 0:128], st.rmu[0:1, :], start=True, stop=True)
        Ct["nbc"].inc(ins)
        assert M[(sub, "nbc")] == Ct["nbc"].n, (sub, "nbc")

    def xn2_dve(v, sub):
        par = sub % 2
        Ct["nbc"].wait(v, at=M[(sub, "nbc")])
        if sub >= 2:
            Ct["f1"].wait(v, at=M[(sub - 1, "f1end")])
        rb = st.RP.ap()[:, 0:SW].unsqueeze(1).broadcast_to((128, CC, SW))
        mb = st.PSTY.ap()[:, 0:SW].unsqueeze(1).broadcast_to((128, CC, SW))
        ss(v, Ct["ssv"], v.tensor_tensor(out=st.xn2b[par][:], in0=st.xin[par][:], in1=mb,
                                         op=ALU.subtract))
        ins = v.tensor_tensor(out=st.xn2b[par][:], in0=st.xn2b[par][:], in1=rb, op=ALU.mult)
        Ct["nxn"].inc(ins)
        assert M[(sub, "nxn")] == Ct["nxn"].n, (sub, "nxn")

    nact0, nmm0, nrow0 = Ct["nact"].n, Ct["nmm"].n, Ct["nrow"].n
    nbc0, nxn0, f1b0 = Ct["nbc"].n, Ct["nxn"].n, Ct["f1"].n
    for s_ in range(NSUB):
        M[(s_, "nactsq")] = nact0 + 2 * s_ + 1
        M[(s_, "nactr")] = nact0 + 2 * s_ + 2
        M[(s_, "nmm")] = nmm0 + s_ + 1
        M[(s_, "nrow")] = nrow0 + s_ + 1
        M[(s_, "nbc")] = nbc0 + s_ + 1
        M[(s_, "nxn")] = nxn0 + s_ + 1
        M[(s_, "f1end")] = f1b0 + 32 * (s_ + 1)

    def _(s):
        stats_scalar(s, 0)
    blk.scalar(_)

    def _(t):
        stats_mm(t, 0)
    blk.tensor(_)

    def _(v):
        rows_dve(v, 0)
    blk.vector(_)

    def _(s):
        rows_act(s, 0)
    blk.scalar(_)

    def _(t):
        bc_mm(t, 0)
    blk.tensor(_)

    def _(v):
        xn2_dve(v, 0)
    blk.vector(_)

    for sub in range(NSUB):
        par = sub % 2

        if sub + 1 < NSUB:
            def _(sync, sub=sub):
                if sub >= 1:
                    Ct["wadd"].wait(sync, at=marks[("xfree2", sub - 1)])
                Ct["xld2" + str((sub + 1) % 2)].inc(
                    sync.dma_start(out=st.xin[(sub + 1) % 2][:], in_=feat_dram(in32, sub + 1, SW)), 16)
            blk.sync(_)
        if final:
            def _(sync, sub=sub):
                if sub >= 1:
                    Ct["wadd"].wait(sync, at=marks[("ffree", sub - 1)])
                Ct["fx0"].inc(sync.dma_start(out=st.x0s[:], in_=feat_dram(x0T, sub, SW)), 16)
                Ct["fbt"].inc(sync.dma_start(out=st.bts[:], in_=feat_dram(b32, sub, SW)), 16)
            blk.sync(_)

        f10, silu0, f20 = Ct["f1"].n, Ct["silu"].n, Ct["f2"].n

        def _(t, sub=sub, par=par, f10=f10, silu0=silu0):
            Ct["nxn"].wait(t, at=M[(sub, "nxn")])

            def f1(fc):
                Ct["silu"].wait(t, at=max(silu0, silu0 + fc - 1))
                for cc in range(CC):
                    ins = t.matmul(st.PHB[fc % 2][:, 0:SW],
                                   st.w1[:, cc, fc * 128:(fc + 1) * 128], st.xn2b[par][:, cc, :],
                                   start=(cc == 0), stop=(cc == CC - 1))
                Ct["f1"].inc(ins)

            def f2(fc):
                Ct["silu"].wait(t, at=silu0 + fc + 1)
                if fc == 0 and sub == 0:
                    gp.wait_at(t, st.w2mark)
                if fc == 0 and ("odone", sub - 1) in marks:
                    Ct["wadd"].wait(t, at=marks[("odone", sub - 1)])
                for oc in range(CC):
                    ins = t.matmul(st.PO[oc // 2][:, (oc % 2) * SW:(oc % 2 + 1) * SW],
                                   st.w2[:, fc, oc * 128:(oc + 1) * 128], st.hbf[fc % 3][:],
                                   start=(fc == 0 and oc % 2 == 0), stop=(fc == FC - 1),
                                   skip_group_check=True)
                Ct["f2"].inc(ins)

            f1(0)
            f1(1)
            f1(2)
            for fc in range(FC):
                f2(fc)
                if fc + 3 < FC:
                    f1(fc + 3)
                if fc == 2 and sub + 1 < NSUB:
                    stats_mm(t, sub + 1)
                if fc == 8 and sub + 1 < NSUB:
                    bc_mm(t, sub + 1)
            assert M[(sub, "f1end")] == Ct["f1"].n
        blk.tensor(_)

        def _(s, sub=sub, f10=f10, f20=f20):
            for fc in range(FC):
                s.wait_ge(Ct["f1"].sem, f10 + fc + 1)
                ins = s.activation(out=st.hbf[fc % 3][:], in_=st.PHB[fc % 2][:, 0:SW],
                                   func=AF.Silu, bias=st.fbias[:, fc:fc + 1], scale=1.0)
                Ct["silu"].inc(ins)
                if fc == 1 and sub + 1 < NSUB:
                    stats_scalar(s, sub + 1)
                if fc == 6 and sub + 1 < NSUB:
                    rows_act(s, sub + 1)
        blk.scalar(_)
        marks[("f2done", sub)] = Ct["f2"].n

        def _(v, sub=sub, par=par):
            if sub + 1 < NSUB:
                rows_dve(v, sub + 1)
                xn2_dve(v, sub + 1)
            Ct["f2"].wait(v)
            if final:
                Ct["fx0"].wait(v)
                Ct["fbt"].wait(v)
            xo = Ct["xout2" + str(par)]
            v.wait_ge(xo.sem, xo.n)
            for oc in range(CC):
                po = st.PO[oc // 2][:, (oc % 2) * SW:(oc % 2 + 1) * SW]
                ins = v.scalar_tensor_tensor(out=st.xnext[par][:, oc, :], in0=po,
                                             scalar=st.fb2[:, oc:oc + 1], in1=st.xin[par][:, oc, :],
                                             op0=ALU.add, op1=ALU.add)
                if final:
                    ss(v, Ct["ssv"], ins)
                    ss(v, Ct["ssv"], v.tensor_tensor(out=st.xnext[par][:, oc, :],
                                                     in0=st.xnext[par][:, oc, :],
                                                     in1=st.x0s[:, oc, :], op=ALU.add))
                    ins = v.tensor_tensor(out=st.xnext[par][:, oc, :], in0=st.xnext[par][:, oc, :],
                                          in1=st.bts[:, oc, :], op=ALU.add)
                Ct["wadd"].inc(ins)
        blk.vector(_)
        marks[("odone", sub)] = Ct["wadd"].n
        marks[("xfree2", sub)] = Ct["wadd"].n
        marks[("ffree", sub)] = Ct["wadd"].n

        def _(s, sub=sub, par=par):
            Ct["wadd"].wait(s)
            Ct["xout2" + str(par)].inc(s.dma_start(out=feat_dram(out32, sub, SW), in_=st.xnext[par][:]), 16)
        blk.scalar(_)

    if True:
        def _(s):
            s.wait_ge(Ct["xout20"].sem, Ct["xout20"].n)
            s.wait_ge(Ct["xout21"].sem, Ct["xout21"].n)
        blk.scalar(_)




"""Branch-B pass 1: module LN + rmsnorm + GQA causal flash attention + wo + residual.

Per core: x0T -> xn (stored xn32) -> h = xn + gqa(rms(xn)) (stored h32).

Layouts (see kernel host prep):
  wqT_pad [1024, 2048]: q-head h -> out chunk h, rows 0:64 data, 64:128 zero.
  wkT_pad [1024, 512]:  kv-head g -> chunk g, rows 0:64 data.
  wvT     [1024, 256].
  woT_pad [2048, 1024]: rows = 16 heads x 128 (64 data + 64 zero).
  kauginit [128, 4, S] bf16: row 64 = -1 (score shift flag), rows 65.. = 0.
  vauginit [128, 32, 4, 65] bf16: col 64 = 1 (l accumulator), else 0.

Softmax: s~ = q.k - c;  P = exp(s~/8); normalize by l at evac. c = ||q||*max_j||k_j||
(prefix max per kv head) >= max_j q.k — exact softmax shift, P<=1.

PSUM: 0/1: PS scores ping-pong   2: PSTY (stats rows 0/32; qn2 row 64)
      3: RBC (bc / linv-bc even hq)   4: PBC (bc mu / linv-bc odd hq)
      5/6: PJ0/PJ1 (Q proj; flash O accum per hq%2; wo)   7: PK (K/V proj)
"""

NQH = 16
NKV = 4


def alloc_b1(nc, stack, S=4096, pfx="b1_"):
    st = NS()
    sb = lambda name, shp, dt: stack.enter_context(nc.sbuf_tensor(pfx + name, shp, dt))
    st.xnblk = sb("xnblk", [128, CC, BLK], F32)
    st.xrbf = sb("xrbf", [128, CC, BLK], BF16)
    st.anw = sb("anw", [128, CC], F32)
    st.wq = sb("wq", [128, CC, 2048], BF16)
    st.wk = sb("wk", [128, CC, 512], BF16)
    st.wv = sb("wv", [128, CC, 256], BF16)
    st.wo = sb("wo", [128, 16, D], BF16)
    st.kaug = sb("kaug", [128, NKV, S], BF16)
    st.vaug = sb("vaug", [128, S // 128, NKV, 65], BF16)
    st.qs = sb("qs", [128, NQH, BLK], BF16)
    st.obf = sb("obf", [128, NQH, BLK], BF16)
    st.bsqs = sb("bsqs", [128, BLK], BF16)
    st.otmp = sb("otmp", [64, BLK], BF16)
    st.lrows = sb("lrows", [65, BLK], F32)
    st.lrowsb = sb("lrowsb", [65, BLK], BF16)
    st.pt = [sb("pt0", [128, BLK], BF16), sb("pt1", [128, BLK], BF16),
             sb("pt2", [128, BLK], BF16)]
    st.rmu = sb("rmu", [1, BLK], BF16)
    st.rvar = sb("rvar", [1, BLK], BF16)
    st.rrstd = sb("rrstd", [1, BLK], BF16)
    st.maxk = sb("maxk", [1, 8], F32)
    st.mtmp = sb("mtmp", [1, 2], F32)
    st.crow = st.lrows.ap().bitcast(BF16)[:, 0:BLK]
    st.rt = st.bsqs[0:1, :]
    st.qsA = st.qs.ap()[:, 0:8, :]
    st.sqA = st.qs.ap()[:, 8:16, :]
    ps = lambda name, shp, dt: stack.enter_context(nc.psum_tensor(pfx + name, shp, dt))
    st.PS = [ps("ps0", [128, BLK], F32), ps("ps1", [128, BLK], F32)]
    st.PSTY = ps("psty", [128, BLK], F32)
    st.RBC = ps("rbc", [128, BLK], F32)
    st.PBC = ps("pbc", [128, BLK], F32)
    st.PJ = [ps("pj0", [128, BLK], F32), ps("pj1", [128, BLK], F32)]
    st.PK = ps("pk", [128, BLK], F32)
    return st


B1_CTRS = ["bx", "binit", "bfold", "xnst", "hst",
           "bnmm", "bnrow", "bnact", "bnbc", "bnxn",
           "bpj", "bpjev", "bkv", "bkvev",
           "bsq", "bqn", "bkn", "bqnact",
           "fsc", "fexp", "fmsk", "fpv", "foev", "fbc", "fodve",
           "bwo", "bwadd"]


def emit_b1(nc, blk, st, c, dram, Ct, S, x0T, xn32, h32):
    NB = S // BLK
    gp = Ct["gp"]
    marks = {}

    def _(g):
        gp.dma(g, st.wq[:], dram["bwqT"].ap().rearrange("(c p) k -> p c k", p=128))
        gp.dma(g, st.wk[:], dram["bwkT"].ap().rearrange("(c p) k -> p c k", p=128))
        gp.dma(g, st.wv[:], dram["bwvT"].ap().rearrange("(c p) k -> p c k", p=128))
        st.qkvmark = 16 * gp.k
        gp.dma(g, st.wo[:], dram["bwoT"].ap().rearrange("(c p) k -> p c k", p=128))
        st.womark = 16 * gp.k
    blk.gpsimd(_)

    def _(sync):
        Ct["binit"].inc(sync.dma_start(out=st.kaug[:], in_=dram["kauginit"].ap().rearrange(
            "p (g s) -> p g s", g=NKV)), 16)
        Ct["binit"].inc(sync.dma_start(out=st.vaug[:], in_=dram["vauginit"].ap().rearrange(
            "p (j g a) -> p j g a", j=S // 128, g=NKV)), 16)
        Ct["binit"].inc(sync.dma_start(out=flat2(st.obf), in_=dram["zeros16"].ap()), 16)
        Ct["binit"].inc(sync.dma_start(out=st.anw[:], in_=dram["banw"].ap()), 16)
        Ct["bx"].inc(sync.dma_start(out=st.xnblk[:], in_=feat_dram(x0T, 0)), 16)
    blk.sync(_)

    def _(v):
        gp.wait_at(v, st.qkvmark)
        Ct["binit"].wait(v)
        for w in [st.wq, st.wk, st.wv]:
            for cc in range(CC):
                ins = v.tensor_scalar(out=w[:, cc, :], in0=w[:, cc, :],
                                      scalar1=st.anw[:, cc:cc + 1], scalar2=None, op0=ALU.mult)
        Ct["bfold"].inc(ins)
    blk.vector(_)

    for b in range(NB):
        def _(s, b=b):
            s.wait_ge(Ct["bx"].sem, Ct["bx"].n)
            if b == 0:
                Ct["cld"].wait(s)
                Ct["binit"].wait(s)
            if b >= 1:
                Ct["fsc"].wait(s, at=marks[("qsfree", b - 1)])
                Ct["fodve"].wait(s, at=marks[("fldone", b - 1)])
            ss(s, Ct["sss"], s.copy(out=flat2s(st.qsA), in_=flat2(st.xnblk)))
            ins = s.square(out=flat2s(st.sqA), in_=flat2s(st.qsA))
            Ct["bnact"].inc(ins)
        blk.scalar(_)

        def _(t):
            Ct["bnact"].wait(t)
            Ct["bnrow"].wait(t)
            Ct["bqn"].wait(t)
            for cc in range(CC):
                t.matmul(st.PSTY[0:1, :], c.ones[:, 0:1], st.qsA[:, cc, :],
                         start=(cc == 0), stop=(cc == CC - 1))
            for cc in range(CC):
                ins = t.matmul(st.PSTY[32:33, :], c.ones[:, 0:1], st.sqA[:, cc, :],
                               start=(cc == 0), stop=(cc == CC - 1), tile_position=(0, 32))
            Ct["bnmm"].inc(ins)
        blk.tensor(_)

        def _(v):
            Ct["bnmm"].wait(v)
            v.tensor_scalar(out=st.rmu[0:1, :], in0=st.PSTY[0:1, :], scalar1=1.0 / D, scalar2=None, op0=ALU.mult)
            ss(v, Ct["ssv"], v.tensor_scalar(out=st.rvar[0:1, :], in0=st.PSTY[32:33, :], scalar1=1.0 / D,
                                             scalar2=None, op0=ALU.mult))
            ss(v, Ct["ssv"], v.tensor_tensor(out=st.rt[0:1, :], in0=st.rmu[0:1, :], in1=st.rmu[0:1, :],
                                             op=ALU.mult))
            ins = v.tensor_tensor(out=st.rvar[0:1, :], in0=st.rvar[0:1, :], in1=st.rt[0:1, :], op=ALU.subtract)
            Ct["bnrow"].inc(ins)
        blk.vector(_)

        def _(s):
            Ct["bnrow"].wait(s)
            ss(s, Ct["sss"], s.activation(out=st.rt[0:1, :], in_=st.rvar[0:1, :], func=AF.Ln,
                                          bias=EPS_LN, scale=1.0))
            ins = s.activation(out=st.rrstd[0:1, :], in_=st.rt[0:1, :], func=AF.Exp, bias=0.0, scale=-0.5)
            Ct["bnact"].inc(ins)
        blk.scalar(_)

        def _(t):
            Ct["bnact"].wait(t)
            Ct["bnrow"].wait(t)
            Ct["bnxn"].wait(t)
            t.matmul(st.RBC[:, :], c.ones[0:1, 0:128], st.rrstd[0:1, :], start=True, stop=True)
            ins = t.matmul(st.PBC[:, :], c.ones[0:1, 0:128], st.rmu[0:1, :], start=True, stop=True)
            Ct["bnbc"].inc(ins)
        blk.tensor(_)

        def _(v, b=b):
            Ct["bnbc"].wait(v)
            rb = st.RBC.ap().unsqueeze(1).broadcast_to((128, CC, BLK))
            mb = st.PBC.ap().unsqueeze(1).broadcast_to((128, CC, BLK))
            ss(v, Ct["ssv"], v.tensor_tensor(out=st.xnblk[:], in0=st.xnblk[:], in1=mb, op=ALU.subtract))
            ins = v.tensor_tensor(out=st.xnblk[:], in0=st.xnblk[:], in1=rb, op=ALU.mult)
            Ct["bnxn"].inc(ins)
        blk.vector(_)
        marks[("xn", b)] = Ct["bnxn"].n

        def _(s, b=b):
            Ct["bnxn"].wait(s, at=marks[("xn", b)])
            Ct["xnst"].inc(s.dma_start(out=feat_dram(xn32, b), in_=st.xnblk[:]), 16)
        blk.scalar(_)

        def _(s, b=b):
            Ct["bnxn"].wait(s, at=marks[("xn", b)])
            ss(s, Ct["sss"], s.copy(out=flat2s(st.qsA), in_=flat2(st.xnblk)))
            ins = s.square(out=flat2s(st.sqA), in_=flat2s(st.qsA))
            Ct["bnact"].inc(ins)
        blk.scalar(_)

        def _(t):
            Ct["bnact"].wait(t)
            Ct["bnrow"].wait(t)
            for cc in range(CC):
                ins = t.matmul(st.PSTY[32:33, :], c.ones[:, 0:1], st.sqA[:, cc, :],
                               start=(cc == 0), stop=(cc == CC - 1), tile_position=(0, 32))
            Ct["bnmm"].inc(ins)
        blk.tensor(_)

        def _(v):
            Ct["bnmm"].wait(v)
            ins = v.tensor_scalar(out=st.rvar[0:1, :], in0=st.PSTY[32:33, :], scalar1=1.0 / D,
                                  scalar2=None, op0=ALU.mult)
            Ct["bnrow"].inc(ins)
        blk.vector(_)

        def _(s):
            Ct["bnrow"].wait(s)
            ss(s, Ct["sss"], s.activation(out=st.rt[0:1, :], in_=st.rvar[0:1, :], func=AF.Ln,
                                          bias=EPS_RMS, scale=1.0))
            ins = s.activation(out=st.rrstd[0:1, :], in_=st.rt[0:1, :], func=AF.Exp, bias=0.0, scale=-0.5)
            Ct["bnact"].inc(ins)
        blk.scalar(_)

        def _(t):
            Ct["bnact"].wait(t)
            Ct["bnxn"].wait(t)
            ins = t.matmul(st.RBC[:, :], c.ones[0:1, 0:128], st.rrstd[0:1, :], start=True, stop=True)
            Ct["bnbc"].inc(ins)
        blk.tensor(_)

        def _(v):
            Ct["bnbc"].wait(v)
            rb = st.RBC.ap().unsqueeze(1).broadcast_to((128, CC, BLK))
            ins = v.tensor_tensor(out=st.xrbf[:], in0=st.xnblk[:], in1=rb, op=ALU.mult)
            Ct["bnxn"].inc(ins)
        blk.vector(_)
        marks[("xr", b)] = Ct["bnxn"].n

        bpj0, bpjev0 = Ct["bpj"].n, Ct["bpjev"].n
        bkv0, bkvev0 = Ct["bkv"].n, Ct["bkvev"].n

        def _(t, b=b, bpjev0=bpjev0, bkvev0=bkvev0):
            if b == 0:
                Ct["bfold"].wait(t)
            Ct["bnxn"].wait(t, at=marks[("xr", b)])
            Ct["bwadd"].wait(t)
            Ct["fodve"].wait(t)
            for hq in range(NQH):
                Ct["bpjev"].wait(t, at=max(bpjev0, bpjev0 + hq - 1))
                for cc in range(CC):
                    ins = t.matmul(st.PJ[hq % 2][:, :], st.wq[:, cc, hq * 128:(hq + 1) * 128],
                                   st.xrbf[:, cc, :], start=(cc == 0), stop=(cc == CC - 1))
                Ct["bpj"].inc(ins)
            for g in range(NKV):
                Ct["bkvev"].wait(t, at=max(bkvev0, bkvev0 + g))
                for cc in range(CC):
                    ins = t.matmul(st.PK[:, :], st.wk[:, cc, g * 128:(g + 1) * 128],
                                   st.xrbf[:, cc, :], start=(cc == 0), stop=(cc == CC - 1))
                Ct["bkv"].inc(ins)
            for jc in range(4):
                Ct["bkvev"].wait(t, at=bkvev0 + 4 + jc)
                for cc in range(CC):
                    ins = t.matmul(st.PK[:, 0:256], st.xrbf[:, cc, jc * 128:(jc + 1) * 128],
                                   st.wv[:, cc, :], start=(cc == 0), stop=(cc == CC - 1))
                Ct["bkv"].inc(ins)
        blk.tensor(_)

        def _(s, b=b, bpj0=bpj0, bkv0=bkv0):
            for hq in range(NQH):
                s.wait_ge(Ct["bpj"].sem, bpj0 + hq + 1)
                ins = s.copy(out=st.qs[:, hq, :], in_=st.PJ[hq % 2][:, :])
                Ct["bpjev"].inc(ins)
            for g in range(NKV):
                s.wait_ge(Ct["bkv"].sem, bkv0 + g + 1)
                ins = s.copy(out=st.kaug[0:64, g, b * BLK:(b + 1) * BLK], in_=st.PK[0:64, :])
                Ct["bkvev"].inc(ins)
            for jc in range(4):
                s.wait_ge(Ct["bkv"].sem, bkv0 + 4 + jc + 1)
                src = st.PK.ap()[:, 0:256].rearrange("p (g a) -> p g a", g=4)
                dst = st.vaug.ap()[:, b * 4 + jc, :, 0:64]
                ins = s.copy(out=dst, in_=src)
                Ct["bkvev"].inc(ins)
        blk.scalar(_)
        marks[("kv", b)] = Ct["bkvev"].n
        marks[("qev", b)] = Ct["bpjev"].n

        bsq0, bqn0, bkn0, bqa0 = Ct["bsq"].n, Ct["bqn"].n, Ct["bkn"].n, Ct["bqnact"].n

        def _(s, b=b, bqn0=bqn0):
            Ct["bkvev"].wait(s, at=marks[("kv", b)])
            for g in range(NKV):
                s.wait_ge(Ct["bqn"].sem, max(bqn0, bqn0 + g))
                ins = s.square(out=st.bsqs[0:64, :], in_=st.kaug[0:64, g, b * BLK:(b + 1) * BLK])
                Ct["bsq"].inc(ins)
        blk.scalar(_)

        def _(t, bsq0=bsq0, bkn0=bkn0):
            for g in range(NKV):
                Ct["bsq"].wait(t, at=bsq0 + g + 1)
                Ct["bkn"].wait(t, at=max(bkn0, bkn0 + g))
                ins = t.matmul(st.PSTY[0:1, :], c.ones[0:64, 0:1], st.bsqs[0:64, :],
                               start=True, stop=True)
                Ct["bqn"].inc(ins)
        blk.tensor(_)

        def _(v, b=b, bqn0=bqn0):
            for g in range(NKV):
                v.wait_ge(Ct["bqn"].sem, bqn0 + g + 1)
                if b == 0:
                    ins = v.reduce_max(st.maxk[0:1, g:g + 1], st.PSTY[0:1, :], axis=AX.X)
                else:
                    ss(v, Ct["ssv"], v.reduce_max(st.mtmp[0:1, 0:1], st.PSTY[0:1, :], axis=AX.X))
                    ins = v.tensor_tensor(out=st.maxk[0:1, g:g + 1], in0=st.maxk[0:1, g:g + 1],
                                          in1=st.mtmp[0:1, 0:1], op=ALU.max)
                Ct["bkn"].inc(ins)
                Ct["bkn"].wait(v)
        blk.vector(_)

        def _(s, b=b, bkn0=bkn0):
            Ct["bkn"].wait(s, at=bkn0 + NKV)
            ins = s.activation(out=st.maxk[0:1, 4:8], in_=st.maxk[0:1, 0:4], func=AF.Ln,
                               bias=1e-38, scale=1.0)
            Ct["bqnact"].inc(ins)
        blk.scalar(_)

        def _(v):
            Ct["bqnact"].wait(v)
            ins = v.tensor_scalar(out=st.maxk[0:1, 4:8], in0=st.maxk[0:1, 4:8], scalar1=0.5,
                                  scalar2=None, op0=ALU.mult)
            Ct["bkn"].inc(ins)
        blk.vector(_)
        lnmk_mark = Ct["bkn"].n

        bsq1, bqn1 = Ct["bsq"].n, Ct["bqn"].n
        bqa1 = Ct["bqnact"].n

        qbuf = lambda h: st.bsqs[0:64, :] if h % 2 == 0 else st.crow[0:64, :]
        qbank = lambda h: st.PK

        def _(s, b=b, bqn1=bqn1, lnmk_mark=lnmk_mark):
            Ct["bkn"].wait(s, at=lnmk_mark)
            Ct["bpjev"].wait(s, at=marks[("qev", b)])
            for h in (0, 1):
                ins = s.square(out=qbuf(h), in_=st.qs[0:64, h, :])
                Ct["bsq"].inc(ins)
            s.wait_ge(Ct["bqn"].sem, bqn1 + 1)
            ins = s.activation(out=st.rvar[0:1, :], in_=qbank(0)[64:65, :],
                               func=AF.Ln, bias=1e-12, scale=1.0)
            Ct["bqnact"].inc(ins)
            Ct["bqnact"].wait(s)
            ins = s.activation(out=st.qs[64:65, 0, :], in_=st.rvar[0:1, :],
                               func=AF.Exp, bias=st.maxk[0:1, 4:5], scale=0.5)
            Ct["bqnact"].inc(ins)
        blk.scalar(_)

        def _(t, b=b, bsq1=bsq1, bqa1=bqa1):
            Ct["bkvev"].wait(t, at=marks[("kv", b)])
            for h in (0, 1):
                Ct["bsq"].wait(t, at=bsq1 + h + 1)
                if h == 1:
                    Ct["bqnact"].wait(t, at=bqa1 + 1)
                ins = t.matmul(qbank(h)[64:65, :], c.ones[0:64, 0:1], qbuf(h),
                               start=True, stop=True, tile_position=(0, 64))
                Ct["bqn"].inc(ins)
        blk.tensor(_)

        nj = 4 * (b + 1)
        fsc0, fexp0, fmsk0, fpv0 = Ct["fsc"].n, Ct["fexp"].n, Ct["fmsk"].n, Ct["fpv"].n
        foev0, fbc0, fodve0 = Ct["foev"].n, Ct["fbc"].n, Ct["fodve"].n

        def _(t, b=b, nj=nj, bsq1=bsq1, bqa1=bqa1, fexp0=fexp0, fmsk0=fmsk0,
              foev0=foev0, fodve0=fodve0, fbc0=fbc0):
            for hq in range(NQH):
                kvh = hq // 4
                po = st.PJ[hq % 2]
                Ct["foev"].wait(t, at=max(foev0, foev0 + hq - 1))
                Ct["fodve"].wait(t, at=max(fodve0, fodve0 + 3 * hq - 5))
                e0 = fexp0 + hq * nj
                m0 = fmsk0 + hq * 4
                G0 = hq * nj
                psb = lambda g: (st.PS + [st.PSTY])[g % 3]
                Ct["bqnact"].wait(t, at=bqa1 + 2 * hq + 2)
                for jp in (0, 1):
                    if jp < nj:
                        Ct["fexp"].wait(t, at=max(fexp0, fexp0 + G0 + jp - 2))
                        ins = t.matmul(psb(G0 + jp)[:, 0:BLK], st.kaug[:, kvh, jp * 128:(jp + 1) * 128],
                                       st.qs[:, hq, :], start=True, stop=True)
                        Ct["fsc"].inc(ins)
                if hq + 2 < NQH:
                    Ct["bsq"].wait(t, at=bsq1 + hq + 3)
                    Ct["bqnact"].wait(t, at=bqa1 + 2 * hq + 3)
                    ins = t.matmul(qbank(hq)[64:65, :], c.ones[0:64, 0:1], qbuf(hq),
                                   start=True, stop=True, tile_position=(0, 64))
                    Ct["bqn"].inc(ins)
                for j in range(nj):
                    diag = j >= nj - 4
                    m = j - (nj - 4)
                    if j + 2 < nj:
                        Ct["fexp"].wait(t, at=max(fexp0, fexp0 + G0 + j))
                        ins = t.matmul(psb(G0 + j + 2)[:, 0:BLK],
                                       st.kaug[:, kvh, (j + 2) * 128:(j + 3) * 128],
                                       st.qs[:, hq, :], start=True, stop=True)
                        Ct["fsc"].inc(ins)
                    if j == min(3, nj - 2) and hq >= 1:
                        Ct["fodve"].wait(t, at=fodve0 + 3 * (hq - 1) + 2)
                        bcb = st.RBC if (hq - 1) % 2 == 0 else st.PBC
                        Ct["fbc"].wait(t, at=max(fbc0, fbc0 + hq - 2))
                        ins = t.matmul(bcb[0:64, :], c.ones[64:65, 0:64], st.lrowsb[64:65, :],
                                       start=True, stop=True)
                        Ct["fbc"].inc(ins)
                    Ct["fexp"].wait(t, at=e0 + j + 1)
                    if diag:
                        Ct["fmsk"].wait(t, at=m0 + m + 1)
                        s0 = m * 128
                    else:
                        s0 = 0
                    ins = t.matmul(po[0:65, s0:BLK], st.vaug[:, j, kvh, :],
                                   st.pt[(G0 + j) % 3][:, s0:BLK],
                                   start=(j == 0), stop=(j == nj - 1), skip_group_check=True)
                    Ct["fpv"].inc(ins)
            Ct["fodve"].wait(t, at=fodve0 + 3 * (NQH - 1) + 2)
            Ct["fbc"].wait(t, at=max(fbc0, fbc0 + NQH - 2))
            ins = t.matmul(st.PBC[0:64, :], c.ones[64:65, 0:64], st.lrowsb[64:65, :],
                           start=True, stop=True)
            Ct["fbc"].inc(ins)
        blk.tensor(_)
        marks[("qsfree", b)] = Ct["fsc"].n

        def _(s, b=b, nj=nj, fsc0=fsc0, fpv0=fpv0, fodve0=fodve0, bqn1=bqn1):
            for hq in range(NQH):
                if hq + 2 < NQH:
                    s.wait_ge(Ct["bqn"].sem, bqn1 + hq + 2)
                    ins = s.square(out=qbuf(hq + 2), in_=st.qs[0:64, hq + 2, :])
                    Ct["bsq"].inc(ins)
                if hq + 1 < NQH:
                    s.wait_ge(Ct["bqn"].sem, bqn1 + hq + 2)
                    Ct["bqnact"].wait(s)
                    ins = s.activation(out=st.rvar[0:1, :], in_=qbank(hq + 1)[64:65, :],
                                       func=AF.Ln, bias=1e-12, scale=1.0)
                    Ct["bqnact"].inc(ins)
                    Ct["bqnact"].wait(s)
                    ins = s.activation(out=st.qs[64:65, hq + 1, :], in_=st.rvar[0:1, :],
                                       func=AF.Exp,
                                       bias=st.maxk[0:1, 4 + (hq + 1) // 4:5 + (hq + 1) // 4],
                                       scale=0.5)
                    Ct["bqnact"].inc(ins)
                s0 = fsc0 + hq * nj
                p0 = fpv0 + hq * nj
                G0 = hq * nj
                psb = lambda g: (st.PS + [st.PSTY])[g % 3]
                for j in range(nj):
                    s.wait_ge(Ct["fsc"].sem, s0 + j + 1)
                    s.wait_ge(Ct["fpv"].sem, max(p0, p0 + j - 2))
                    ins = s.activation(out=st.pt[(G0 + j) % 3][:], in_=psb(G0 + j)[:, 0:BLK],
                                       func=AF.Exp, bias=0.0, scale=0.125)
                    Ct["fexp"].inc(ins)
                s.wait_ge(Ct["fpv"].sem, p0 + nj)
                s.wait_ge(Ct["fodve"].sem, fodve0 + 3 * hq)
                ins = s.copy(out=st.otmp[0:64, :], in_=st.PJ[hq % 2][0:64, :])
                Ct["foev"].inc(ins)
        blk.scalar(_)

        def _(v, b=b, nj=nj, fexp0=fexp0, foev0=foev0, fbc0=fbc0):
            def rct(hq):
                v.wait_ge(Ct["foev"].sem, foev0 + hq + 1)
                v.wait_ge(Ct["fbc"].sem, max(fbc0, fbc0 + hq))
                ins = v.reciprocal(st.lrows[64:65, :], st.PJ[hq % 2][64:65, :])
                Ct["fodve"].inc(ins)
                Ct["fodve"].wait(v)
                ins = v.tensor_copy(out=st.lrowsb[64:65, :], in_=st.lrows[64:65, :])
                Ct["fodve"].inc(ins)
                v.wait_ge(Ct["fbc"].sem, fbc0 + hq + 1)
                bcb = st.RBC if hq % 2 == 0 else st.PBC
                ins = v.tensor_tensor(out=st.obf[0:64, hq, :], in0=st.otmp[0:64, :],
                                      in1=bcb[0:64, :], op=ALU.mult)
                Ct["fodve"].inc(ins)

            for hq in range(NQH):
                e0 = fexp0 + hq * nj
                G0 = hq * nj

                def masks(hq=hq, e0=e0, G0=G0):
                    for m in range(4):
                        j = nj - 4 + m
                        v.wait_ge(Ct["fexp"].sem, e0 + j + 1)
                        ins = v.tensor_tensor(out=st.pt[(G0 + j) % 3][:, m * 128:(m + 1) * 128],
                                              in0=st.pt[(G0 + j) % 3][:, m * 128:(m + 1) * 128],
                                              in1=c.maskf[:], op=ALU.mult)
                        Ct["fmsk"].inc(ins)

                if nj > 4:
                    if hq >= 1:
                        rct(hq - 1)
                    masks()
                else:
                    masks()
                    if hq >= 1:
                        rct(hq - 1)
            rct(NQH - 1)
        blk.vector(_)
        marks[("po", 0)] = Ct["foev"].n
        marks[("po", 1)] = Ct["foev"].n
        marks[("fldone", b)] = Ct["fodve"].n

        bwo0 = Ct["bwo"].n

        bwadd0 = Ct["bwadd"].n

        def _(t, b=b, bwadd0=bwadd0):
            if b == 0:
                gp.wait_at(t, st.womark)
            Ct["fodve"].wait(t, at=marks[("fldone", b)])
            for oc in range(CC):
                Ct["bwadd"].wait(t, at=max(bwadd0, bwadd0 + oc - 1))
                for cc in range(NQH):
                    ins = t.matmul(st.PJ[oc % 2][:, :], st.wo[:, cc, oc * 128:(oc + 1) * 128],
                                   st.obf[:, cc, :], start=(cc == 0), stop=(cc == NQH - 1))
                Ct["bwo"].inc(ins)
        blk.tensor(_)

        def _(v, b=b, bwo0=bwo0):
            v.wait_ge(Ct["xnst"].sem, 16 * (b + 1))
            for oc in range(CC):
                v.wait_ge(Ct["bwo"].sem, bwo0 + oc + 1)
                ins = v.tensor_tensor(out=st.xnblk[:, oc, :], in0=st.PJ[oc % 2][:, :],
                                      in1=st.xnblk[:, oc, :], op=ALU.add)
                Ct["bwadd"].inc(ins)
        blk.vector(_)

        def _(s, b=b):
            Ct["bwadd"].wait(s)
            Ct["hst"].inc(s.dma_start(out=feat_dram(h32, b), in_=st.xnblk[:]), 16)
        blk.scalar(_)

        if b + 1 < NB:
            def _(sync, b=b):
                sync.wait_ge(Ct["hst"].sem, 16 * (b + 1))
                sync.wait_ge(Ct["xnst"].sem, 16 * (b + 1))
                Ct["bx"].inc(sync.dma_start(out=st.xnblk[:], in_=feat_dram(x0T, b + 1)), 16)
            blk.sync(_)

    def _(s):
        s.wait_ge(Ct["hst"].sem, 16 * NB)
        s.wait_ge(Ct["xnst"].sem, 16 * NB)
    blk.scalar(_)


def flat2s(ap):
    return ap.rearrange("p a b -> p (a b)")




"""Branch-B pass 3: h -> rms -> top-2-of-8 gating -> dense-weighted experts -> + h -> B32.

Dense formulation (exactly the reference's math): out = sum_e comb[:,e] * (t @ We.T),
computed as sum_e ((comb_e*t) @ We.T) accumulated in PSUM across (e, cc).
Gating: top-2 softmax: comb_e = [l_e >= m2] * exp(l_e - m1) / (1 + exp(m2 - m1)).

Sub width 256. PSUM: 0-3: PEX (8 oc slices) 4: PG (logits, comb transpose via bitcast)
5: PSTY (rms stats) 6: RBC (bc) 7: spare.
"""

SW = 256
NE = 8


def alloc_b3(nc, stack, pfx="b3_"):
    st = NS()
    sb = lambda name, shp, dt: stack.enter_context(nc.sbuf_tensor(pfx + name, shp, dt))
    st.expw = sb("expw", [128, NE, CC, D], BF16)
    st.gw = sb("gw", [128, CC, 8], BF16)
    st.gwf = sb("gwf", [128, CC, 8], F32)
    st.glo = sb("glo", [128, CC, 8], BF16)
    st.tlo = sb("tlo", [128, CC, SW], BF16)
    st.fnw = sb("fnw", [128, CC], F32)
    st.hs = [sb("hs0", [128, CC, SW], F32), sb("hs1", [128, CC, SW], F32)]
    st.tbf = [sb("tbf0", [128, CC, SW], BF16), sb("tbf1", [128, CC, SW], BF16)]
    st.bsq = sb("bsq", [128, CC, SW], BF16)
    st.te = [sb("te0", [128, CC, SW], BF16), sb("te1", [128, CC, SW], BF16)]
    st.cbbf = [sb("cbbf0", [128, SW], BF16), sb("cbbf1", [128, SW], BF16)]
    st.bout = [sb("bout0", [128, CC, SW], F32), sb("bout1", [128, CC, SW], F32)]
    st.combT = sb("combT", [8, SW], BF16)
    st.rvar = sb("rvar", [1, SW], BF16)
    st.rt = sb("rt", [1, SW], BF16)
    st.rrstd = sb("rrstd", [1, SW], BF16)
    st.cm1 = sb("cm1", [128, 1], F32)
    st.cnm1 = sb("cnm1", [128, 1], F32)
    st.cm2 = sb("cm2", [128, 1], F32)
    st.cdi = sb("cdi", [128, 1], F32)
    st.cex = sb("cex", [128, 8], F32)
    st.cge = sb("cge", [128, 8], F32)
    st.cbig = sb("cbig", [128, 8], F32)
    st.cb16 = sb("cb16", [128, 8], BF16)
    ps = lambda name, shp, dt: stack.enter_context(nc.psum_tensor(pfx + name, shp, dt))
    st.PEX = [ps(f"pex{i}", [128, 2 * SW], F32) for i in range(4)]
    st.PG = ps("pg", [128, 2 * SW], F32)
    st.PG2 = ps("pg2", [128, 2 * SW], F32)
    st.PSTY = ps("psty", [128, 2 * SW], F32)
    st.RBC = ps("rbc", [128, 2 * SW], F32)
    st.pgbf = st.PG.ap().bitcast(BF16)
    st.pg2bf = st.PG2.ap().bitcast(BF16)
    return st


B3_CTRS = ["bh0", "bh1", "bst0", "bst1",
           "mnmm", "mnrow", "mnact", "mnbc", "mnxn",
           "glog", "gdve", "gact", "gtr", "gtrm", "gtev",
           "ebc", "ebcev", "ete", "emm", "eev"]


def emit_b3(nc, blk, st, c, dram, Ct, S, h32, b32):
    NSUB = S // SW
    gp = Ct["gp"]
    marks = {}

    def _(g):
        if "wcast" in Ct:
            Ct["wcast"].wait(g)
        en = "bexpT_bf" if "bexpT_bf" in dram else "bexpT"
        gp.dma(g, st.expw[:], dram[en].ap().rearrange("e (c p) k -> p e c k", p=128))
        gp.dma(g, st.gw[:], dram["bgwT"].ap().rearrange("(c p) k -> p c k", p=128))
    blk.gpsimd(_)

    def _(sync):
        Ct["bh0"].inc(sync.dma_start(out=st.hs[0][:], in_=feat_dram(h32, 0, SW)), 16)
        Ct["mnbc"].inc(sync.dma_start(out=st.fnw[:], in_=dram["bfnw"].ap()), 16)
        Ct["mnbc"].inc(sync.dma_start(out=st.gwf[:], in_=dram["bgwT"].ap().rearrange(
            "(c p) k -> p c k", p=128)), 16)
    blk.sync(_)

    def _(v):
        v.wait_ge(Ct["mnbc"].sem, 32)
        gpv = Ct["gp"]
        gpv.wait_all(v)
        ins = v.tensor_tensor(out=flat2(st.glo), in0=flat2(st.gwf), in1=flat2(st.gw), op=ALU.subtract)
        Ct["gdve"].inc(ins)
    blk.vector(_)

    B = {}
    bases = dict(mnact=Ct["mnact"].n, mnmm=Ct["mnmm"].n, mnrow=Ct["mnrow"].n,
                 mnbc=Ct["mnbc"].n, mnxn=Ct["mnxn"].n, glog=Ct["glog"].n,
                 gdve=Ct["gdve"].n, gtr=Ct["gtr"].n, gtrm=Ct["gtrm"].n,
                 gact=Ct["gact"].n, gtev=Ct["gtev"].n, ebc=Ct["ebc"].n,
                 ebcev=Ct["ebcev"].n, ete=Ct["ete"].n, emm=Ct["emm"].n, eev=Ct["eev"].n)
    for s_ in range(-2, NSUB):
        if s_ < 0:
            for k in bases:
                B[(s_, k)] = bases[k]
            continue
        B[(s_, "mnactsq")] = bases["mnact"] + 2 * s_ + 1
        B[(s_, "mnactr")] = bases["mnact"] + 2 * s_ + 2
        B[(s_, "mnmm")] = bases["mnmm"] + s_ + 1
        B[(s_, "mnrow")] = bases["mnrow"] + s_ + 1
        B[(s_, "mnbc")] = bases["mnbc"] + s_ + 1
        B[(s_, "mnxn")] = bases["mnxn"] + s_ + 1
        B[(s_, "glog")] = bases["glog"] + 2 * (s_ + 1)
        B[(s_, "gdve")] = bases["gdve"] + 2 * (s_ + 1)
        B[(s_, "gtr")] = bases["gtr"] + 2 * (s_ + 1)
        B[(s_, "gtrm")] = bases["gtrm"] + 2 * (s_ + 1)
        B[(s_, "gact")] = bases["gact"] + 4 * (s_ + 1)
        B[(s_, "gtev")] = bases["gtev"] + 2 * (s_ + 1)
        B[(s_, "ebc")] = bases["ebc"] + 8 * (s_ + 1)
        B[(s_, "ebcev")] = bases["ebcev"] + 8 * (s_ + 1)
        B[(s_, "ete")] = bases["ete"] + 8 * (s_ + 1)
        B[(s_, "emm")] = bases["emm"] + 8 * (s_ + 1)
        B[(s_, "eev")] = bases["eev"] + s_ + 1
    for k in ["mnactsq", "mnactr"]:
        for s_ in (-1, -2):
            B[(s_, k)] = bases["mnact"]

    def rms_scalar(s, sub):
        hc = Ct["bh" + str(sub % 2)]
        s.wait_ge(hc.sem, hc.n)
        if sub == 0:
            Ct["cld"].wait(s)
        Ct["ete"].wait(s, at=B[(sub - 2, "ete")])
        ss(s, Ct["sss"], s.copy(out=flat2(st.tbf[sub % 2]), in_=flat2(st.hs[sub % 2])))
        ins = s.square(out=flat2(st.bsq), in_=flat2(st.tbf[sub % 2]))
        Ct["mnact"].inc(ins)
        assert Ct["mnact"].n == B[(sub, "mnactsq")]

    def rms_mm(t, sub):
        if sub == 0:
            gp.wait_all(t)
        Ct["mnact"].wait(t, at=B[(sub, "mnactsq")])
        Ct["mnrow"].wait(t, at=B[(sub - 1, "mnrow")])
        Ct["mnxn"].wait(t, at=B[(sub - 1, "mnxn")])
        for cc in range(CC):
            ins = t.matmul(st.PSTY[32:33, 0:SW], c.ones[:, 0:1], st.bsq[:, cc, :],
                           start=(cc == 0), stop=(cc == CC - 1), tile_position=(0, 32))
        Ct["mnmm"].inc(ins)
        assert Ct["mnmm"].n == B[(sub, "mnmm")]

    def rows_dve(v, sub):
        Ct["mnmm"].wait(v, at=B[(sub, "mnmm")])
        Ct["mnact"].wait(v, at=B[(sub - 1, "mnactr")])
        ins = v.tensor_scalar(out=st.rvar[0:1, :], in0=st.PSTY[32:33, 0:SW], scalar1=1.0 / D,
                              scalar2=None, op0=ALU.mult)
        Ct["mnrow"].inc(ins)
        assert Ct["mnrow"].n == B[(sub, "mnrow")]

    def rows_act(s, sub):
        Ct["mnrow"].wait(s, at=B[(sub, "mnrow")])
        ss(s, Ct["sss"], s.activation(out=st.rt[0:1, :], in_=st.rvar[0:1, :], func=AF.Ln,
                                      bias=EPS_RMS, scale=1.0))
        ins = s.activation(out=st.rrstd[0:1, :], in_=st.rt[0:1, :], func=AF.Exp, bias=0.0, scale=-0.5)
        Ct["mnact"].inc(ins)
        assert Ct["mnact"].n == B[(sub, "mnactr")]

    def rms_bc(t, sub):
        Ct["mnact"].wait(t, at=B[(sub, "mnactr")])
        Ct["mnxn"].wait(t, at=B[(sub - 1, "mnxn")])
        ins = t.matmul(st.PSTY[:, SW:2 * SW], c.ones[0:1, 0:128], st.rrstd[0:1, :],
                       start=True, stop=True)
        Ct["mnbc"].inc(ins)
        assert Ct["mnbc"].n == B[(sub, "mnbc")]

    def fold_dve(v, sub):
        par = sub % 2
        Ct["mnbc"].wait(v, at=B[(sub, "mnbc")])
        bc_ = Ct["bst" + str(par)]
        v.wait_ge(bc_.sem, bc_.n)
        Ct["ebc"].wait(v, at=B[(sub - 1, "ebc")])
        Ct["glog"].wait(v, at=B[(sub - 1, "glog")])
        rb = st.PSTY.ap()[:, SW:2 * SW].unsqueeze(1).broadcast_to((128, CC, SW))
        ss(v, Ct["ssv"], v.tensor_tensor(out=st.bout[par][:], in0=st.hs[par][:], in1=rb, op=ALU.mult))
        for cc in range(CC):
            ss(v, Ct["ssv"], v.tensor_scalar(out=st.bout[par][:, cc, :], in0=st.bout[par][:, cc, :],
                                             scalar1=st.fnw[:, cc:cc + 1], scalar2=None, op0=ALU.mult))
        ss(v, Ct["ssv"], v.tensor_copy(out=flat2(st.tbf[par]), in_=flat2(st.bout[par])))
        ins = v.tensor_tensor(out=flat2(st.tlo), in0=flat2(st.bout[par]), in1=flat2(st.tbf[par]),
                              op=ALU.subtract)
        Ct["mnxn"].inc(ins)
        assert Ct["mnxn"].n == B[(sub, "mnxn")]

    def logits_t(t, sub):
        Ct["mnxn"].wait(t, at=B[(sub, "mnxn")])
        Ct["gtev"].wait(t, at=B[(sub - 1, "gtev")])
        for ic in range(2):
            pg = st.PG if ic == 0 else st.PG2
            for cc in range(CC):
                t.matmul(pg[:, 0:8], st.tbf[sub % 2][:, cc, ic * 128:(ic + 1) * 128],
                         st.gw[:, cc, :], start=(cc == 0), stop=False)
            for cc in range(CC):
                t.matmul(pg[:, 0:8], st.tbf[sub % 2][:, cc, ic * 128:(ic + 1) * 128],
                         st.glo[:, cc, :], start=False, stop=False)
            for cc in range(CC):
                ins = t.matmul(pg[:, 0:8], st.tlo[:, cc, ic * 128:(ic + 1) * 128],
                               st.gw[:, cc, :], start=False, stop=(cc == CC - 1))
            Ct["glog"].inc(ins)
        assert Ct["glog"].n == B[(sub, "glog")]

    def transpose_t(t, sub):
        for ic in range(2):
            pgb = st.pgbf if ic == 0 else st.pg2bf
            Ct["gtr"].wait(t, at=bases["gtr"] + 2 * sub + ic + 1)
            Ct["gact"].wait(t, at=bases["gact"] + 4 * sub + 2 * (ic + 1))
            ins = t.transpose(pgb[0:8, 512:640], st.cb16[:], c.ident[:])
            Ct["gtrm"].inc(ins)
        assert Ct["gtrm"].n == B[(sub, "gtrm")]

    def comb_dve(v, sub):
        for ic in range(2):
            pg = st.PG if ic == 0 else st.PG2
            pgb = st.pgbf if ic == 0 else st.pg2bf
            lg = pg[:, 0:8]
            v.wait_ge(Ct["glog"].sem, bases["glog"] + 2 * sub + ic + 1)
            ss(v, Ct["ssv"], v.reduce_max(st.cm1[:, 0:1], lg, axis=AX.X))
            ss(v, Ct["ssv"], v.tensor_scalar(out=st.cnm1[:, 0:1], in0=st.cm1[:, 0:1],
                                             scalar1=-1.0, scalar2=None, op0=ALU.mult))
            ss(v, Ct["ssv"], v.tensor_scalar(out=st.cbig[:, :], in0=lg, scalar1=st.cm1[:, 0:1],
                                             scalar2=1e30, op0=ALU.is_equal, op1=ALU.mult))
            ss(v, Ct["ssv"], v.tensor_tensor(out=st.cbig[:, :], in0=lg, in1=st.cbig[:, :],
                                             op=ALU.subtract))
            ss(v, Ct["ssv"], v.reduce_max(st.cm2[:, 0:1], st.cbig[:, :], axis=AX.X))
            ss(v, Ct["ssv"], v.tensor_scalar(out=st.cge[:, :], in0=lg, scalar1=st.cm2[:, 0:1],
                                             scalar2=None, op0=ALU.is_ge))
            ins = v.tensor_tensor(out=st.cm2[:, 0:1], in0=st.cm2[:, 0:1],
                                  in1=st.cm1[:, 0:1], op=ALU.subtract)
            Ct["gdve"].inc(ins)
            Ct["gdve"].wait(v)
            v.wait_ge(Ct["gact"].sem, bases["gact"] + 4 * sub + 2 * ic + 2)
            ss(v, Ct["ssv"], v.tensor_scalar(out=st.cdi[:, 0:1], in0=st.cdi[:, 0:1],
                                             scalar1=1.0, scalar2=None, op0=ALU.add))
            ss(v, Ct["ssv"], v.reciprocal(st.cdi[:, 0:1], st.cdi[:, 0:1]))
            ss(v, Ct["ssv"], v.tensor_scalar(out=st.cex[:, :], in0=st.cex[:, :],
                                             scalar1=st.cdi[:, 0:1], scalar2=None, op0=ALU.mult))
            ss(v, Ct["ssv"], v.tensor_tensor(out=st.cex[:, :], in0=st.cex[:, :], in1=st.cge[:, :],
                                             op=ALU.mult))
            ins = v.tensor_copy(out=st.cb16[:, :], in_=st.cex[:, :])
            Ct["gtr"].inc(ins)
            Ct["gtr"].wait(v)
            v.wait_ge(Ct["gtrm"].sem, bases["gtrm"] + 2 * sub + ic + 1)
            Ct["ebc"].wait(v, at=B[(sub - 1, "ebc")])
            ins = v.tensor_copy(out=st.combT[0:8, ic * 128:(ic + 1) * 128],
                                in_=pgb[0:8, 512:640])
            Ct["gtev"].inc(ins)
        assert Ct["gtev"].n == B[(sub, "gtev")]

    def comb_act(s, sub):
        for ic in range(2):
            pg = st.PG if ic == 0 else st.PG2
            lg = pg[:, 0:8]
            s.wait_ge(Ct["gdve"].sem, bases["gdve"] + 2 * sub + ic + 1)
            ins = s.activation(out=st.cex[:, :], in_=lg, func=AF.Exp, bias=st.cnm1[:, 0:1], scale=1.0)
            Ct["gact"].inc(ins)
            Ct["gact"].wait(s)
            ins = s.activation(out=st.cdi[:, 0:1], in_=st.cm2[:, 0:1], func=AF.Exp, bias=0.0, scale=1.0)
            Ct["gact"].inc(ins)
        assert Ct["gact"].n == B[(sub, "gact")]

    def _(s):
        rms_scalar(s, 0)
    blk.scalar(_)

    def _(t):
        rms_mm(t, 0)
    blk.tensor(_)

    def _(v):
        rows_dve(v, 0)
    blk.vector(_)

    def _(s):
        rows_act(s, 0)
    blk.scalar(_)

    def _(t):
        rms_bc(t, 0)
    blk.tensor(_)

    def _(v):
        fold_dve(v, 0)
    blk.vector(_)

    def _(t):
        logits_t(t, 0)
    blk.tensor(_)

    def _(s):
        comb_act(s, 0)
    blk.scalar(_)

    def _(v):
        comb_dve(v, 0)
    blk.vector(_)

    def _(t):
        transpose_t(t, 0)
    blk.tensor(_)

    for sub in range(NSUB):
        par = sub % 2

        if sub + 1 < NSUB:
            def _(sync, sub=sub):
                if sub >= 1:
                    Ct["eev"].wait(sync, at=B[(sub - 1, "eev")])
                Ct["bh" + str((sub + 1) % 2)].inc(
                    sync.dma_start(out=st.hs[(sub + 1) % 2][:], in_=feat_dram(h32, sub + 1, SW)), 16)
            blk.sync(_)

        def _(t, sub=sub):
            Ct["gtev"].wait(t, at=B[(sub, "gtev")])
            Ct["mnxn"].wait(t, at=B[(sub, "mnxn")])

            def bc(e):
                Ct["ebcev"].wait(t, at=max(B[(sub - 1, "ebcev")], bases["ebcev"] + 8 * sub + e))
                ins = t.matmul(st.RBC[:, SW:2 * SW], c.selrows[:, e, :], st.combT[0:8, :],
                               start=True, stop=True)
                Ct["ebc"].inc(ins)

            bc(0)
            bc(1)
            for e in range(NE):
                Ct["ete"].wait(t, at=bases["ete"] + 8 * sub + e + 1)
                if e == 0 and sub >= 1:
                    Ct["eev"].wait(t, at=B[(sub - 1, "eev")])
                for oc in range(CC):
                    for cc in range(CC):
                        ins = t.matmul(st.PEX[oc // 2][:, (oc % 2) * SW:(oc % 2 + 1) * SW],
                                       st.expw[:, e, cc, oc * 128:(oc + 1) * 128],
                                       st.te[e % 2][:, cc, :],
                                       start=(e == 0 and cc == 0 and oc % 2 == 0),
                                       stop=(e == NE - 1 and cc == CC - 1),
                                       skip_group_check=True)
                Ct["emm"].inc(ins)
                if e + 2 < NE:
                    bc(e + 2)
                if sub + 1 < NSUB:
                    if e == 1:
                        rms_mm(t, sub + 1)
                    elif e == 3:
                        rms_bc(t, sub + 1)
                    elif e == 6:
                        logits_t(t, sub + 1)
            if sub + 1 < NSUB:
                transpose_t(t, sub + 1)
            assert Ct["ebc"].n == B[(sub, "ebc")]
            assert Ct["emm"].n == B[(sub, "emm")]
        blk.tensor(_)

        def _(s, sub=sub):
            for e in range(NE):
                s.wait_ge(Ct["ebc"].sem, bases["ebc"] + 8 * sub + e + 1)
                if e >= 2:
                    s.wait_ge(Ct["ete"].sem, bases["ete"] + 8 * sub + e - 1)
                ins = s.copy(out=st.cbbf[e % 2][:], in_=st.RBC[:, SW:2 * SW])
                Ct["ebcev"].inc(ins)
                if sub + 1 < NSUB:
                    if e == 1:
                        rms_scalar(s, sub + 1)
                    elif e == 3:
                        rows_act(s, sub + 1)
            if sub + 1 < NSUB:
                comb_act(s, sub + 1)
        blk.scalar(_)

        def _(v, sub=sub):
            for e in range(NE):
                v.wait_ge(Ct["ebcev"].sem, bases["ebcev"] + 8 * sub + e + 1)
                if e >= 2:
                    v.wait_ge(Ct["emm"].sem, bases["emm"] + 8 * sub + e - 1)
                cb = st.cbbf[e % 2].ap().unsqueeze(1).broadcast_to((128, CC, SW))
                ins = v.tensor_tensor(out=st.te[e % 2][:], in0=st.tbf[sub % 2][:], in1=cb, op=ALU.mult)
                Ct["ete"].inc(ins)
                if sub + 1 < NSUB:
                    if e == 2:
                        rows_dve(v, sub + 1)
                    elif e == 5:
                        fold_dve(v, sub + 1)
            if sub + 1 < NSUB:
                comb_dve(v, sub + 1)
            assert Ct["ete"].n == B[(sub, "ete")]
        blk.vector(_)

        def _(v, sub=sub, par=par):
            v.wait_ge(Ct["emm"].sem, B[(sub, "emm")])
            if sub >= 2:
                sct = Ct["bst" + str(par)]
                v.wait_ge(sct.sem, sct.n)
            for oc in range(CC):
                ins = v.tensor_tensor(out=st.bout[par][:, oc, :],
                                      in0=st.PEX[oc // 2][:, (oc % 2) * SW:(oc % 2 + 1) * SW],
                                      in1=st.hs[par][:, oc, :], op=ALU.add)
            Ct["eev"].inc(ins)
            assert Ct["eev"].n == B[(sub, "eev")]
        blk.vector(_)

        def _(s, sub=sub, par=par):
            Ct["eev"].wait(s, at=B[(sub, "eev")])
            Ct["bst" + str(par)].inc(s.dma_start(out=feat_dram(b32, sub, SW), in_=st.bout[par][:]), 16)
        blk.scalar(_)

    def _(s):
        s.wait_ge(Ct["bst0"].sem, Ct["bst0"].n)
        s.wait_ge(Ct["bst1"].sem, Ct["bst1"].n)
    blk.scalar(_)


from concourse.bass_utils import run_bass_kernel_spmd

S_TOT = 4096
CONST_KEYS = ["ident", "ones", "rotmat", "maskT_bf", "gq", "gk",
              "qcos", "qsin", "kcos", "ksin", "colones", "selrows"]
CAST_NAMES = [f"r{w}_l{li}" for li in (0, 1)
              for w in ("wq", "wk", "wv", "wg", "wo", "w1", "w2")] + ["bexpT"]


def host_branch_weights(inputs, S):
    o = {}
    wq = np.asarray(inputs["L0_ffn_wq"])
    wk = np.asarray(inputs["L0_ffn_wk"])
    wv = np.asarray(inputs["L0_ffn_wv"])
    wo = np.asarray(inputs["L0_ffn_wo"])
    wqp = np.zeros((1024, 16, 128), np.float32)
    wqp[:, :, 0:64] = wq.T.reshape(1024, 16, 64)
    o["bwqT"] = np.ascontiguousarray(wqp.reshape(1024, 2048))
    wkp = np.zeros((1024, 4, 128), np.float32)
    wkp[:, :, 0:64] = wk.T.reshape(1024, 4, 64)
    o["bwkT"] = np.ascontiguousarray(wkp.reshape(1024, 512))
    o["bwvT"] = np.ascontiguousarray(wv.T)
    wop = np.zeros((16, 128, 1024), np.float32)
    wop[:, 0:64, :] = wo.T.reshape(16, 64, 1024)
    o["bwoT"] = np.ascontiguousarray(wop.reshape(2048, 1024))
    ka = np.zeros((128, 4, S), np.float32)
    ka[64] = -1.0
    o["kauginit"] = np.ascontiguousarray(ka.reshape(128, 4 * S)).astype(BF)
    va = np.zeros((128, S // 128, 4, 65), np.float32)
    va[:, :, :, 64] = 1.0
    o["vauginit"] = np.ascontiguousarray(va.reshape(128, -1)).astype(BF)
    o["zeros16"] = np.zeros((128, 16 * BLK), dtype=BF)
    o["banw"] = col8(np.asarray(inputs["L0_ffn_attn_norm_w"]))
    o["bfnw"] = col8(np.asarray(inputs["L0_ffn_ffn_norm_w"]))
    o["bgwT"] = np.ascontiguousarray(np.asarray(inputs["L0_ffn_gate_w"]).T)
    o["bexpT"] = np.ascontiguousarray(
        np.asarray(inputs["L0_ffn_expert_w"]).transpose(0, 2, 1))
    return o


def barrier(nc, deps):
    with nc.Block() as blk:
        def _(sync):
            for ctr, v in deps:
                sync.wait_ge(ctr.sem, v)
        blk.sync(_)

        def _(g):
            for ctr, v in deps:
                g.wait_ge(ctr.sem, v)
        blk.gpsimd(_)


def build(dram_shapes, S):
    nc = bass.Bass("TRN2", target_bir_lowering=False)
    dram = {}
    for name, (shp, dt, kind) in dram_shapes.items():
        dram[name] = nc.dram_tensor(name, list(shp), dt, kind=kind)
    NB = S // BLK
    stack = contextlib.ExitStack()
    with stack:
        c = alloc_consts(nc, stack)
        names = []
        for lst in (R1_CTRS, R2_CTRS, B1_CTRS, B3_CTRS):
            for n in lst:
                if n not in names:
                    names.append(n)
        Ct = {n: Ctr(stack, nc, n) for n in names}
        Ct["wcast"] = Ctr(stack, nc, "wcast")
        Ct["gp"] = GpPool(stack, nc, "gp", fresh=False)
        with nc.Block() as blk:
            load_consts(blk, c, dram, Ct["cld"])
        with contextlib.ExitStack() as pst:
            st = alloc_b1(nc, pst, S)
            with nc.Block() as blk:
                emit_b1(nc, blk, st, c, dram, Ct, S, dram["x0T"].ap(),
                        dram["xn32"].ap(), dram["h32"].ap())
        barrier(nc, [(Ct["hst"], 16 * NB), (Ct["xnst"], 16 * NB)])
        with contextlib.ExitStack() as pst:
            st = alloc_b3(nc, pst)
            with nc.Block() as blk:
                emit_b3(nc, blk, st, c, dram, Ct, S, dram["h32"].ap(), dram["b32"].ap())
        barrier(nc, [(Ct["bst0"], Ct["bst0"].n), (Ct["bst1"], Ct["bst1"].n)])
        for li in range(2):
            if li:
                barrier(nc, [(Ct["xout20"], Ct["xout20"].n),
                             (Ct["xout21"], Ct["xout21"].n)])
            with contextlib.ExitStack() as lst:
                w1buf = lst.enter_context(
                    nc.sbuf_tensor(f"l{li}_w1s", [128, CC, 4 * D], BF16))
                with contextlib.ExitStack() as pst:
                    st1 = alloc_r1(nc, pst, pfx=f"l{li}_", w1buf=w1buf)
                    with nc.Block() as blk:
                        emit_r1(nc, blk, st1, c, dram, Ct, li, S,
                                dram["xn32" if li == 0 else "a32"].ap(), dram["mid32"].ap())
                with contextlib.ExitStack() as pst:
                    st2 = alloc_r2(nc, pst, final=(li == 1), pfx=f"l{li}_", w1buf=w1buf)
                    with nc.Block() as blk:
                        emit_r2(nc, blk, st2, c, dram, Ct, li, S, dram["mid32"].ap(),
                                dram["a32" if li == 0 else "outT"].ap(),
                                final=(li == 1), x0T=dram["x0T"].ap(), b32=dram["b32"].ap())
    return nc


def kernel(**inputs):
    S = S_TOT
    X = np.asarray(inputs["L0_attn_X"])
    base = {}
    hc = host_consts(S)
    base.update({k: hc[k] for k in CONST_KEYS})
    base["fconst"] = np.tile(np.array([[1e-5, 1e-6, 1e-12, 1e-38, 0, 0, 0, 0]],
                                      dtype=np.float32), (8, 1))
    for li in range(2):
        base.update(retnet_layer_weights(inputs, li))
    base.update(host_branch_weights(inputs, S))
    percore = []
    for cid in range(8):
        d = dict(base)
        d["x0T"] = np.ascontiguousarray(X[cid].T).astype(np.float32)
        percore.append(d)
    dram_shapes = {"x0T": ((1024, S), F32, "ExternalInput"),
                   "outT": ((1024, S), F32, "ExternalOutput")}
    for nm in ["xn32", "h32", "b32", "mid32", "a32"]:
        dram_shapes[nm] = ((1024, S), F32, "Internal")
    for k, v in base.items():
        if k not in dram_shapes:
            dt = BF16 if v.dtype == ml_dtypes.bfloat16 else F32
            dram_shapes[k] = (v.shape, dt, "ExternalInput")
    nc = build(dram_shapes, S)
    res = run_bass_kernel_spmd(nc, percore, core_ids=list(range(8)))
    out = np.stack([np.asarray(res.results[cid]["outT"]).T for cid in range(8)], 0)
    return np.ascontiguousarray(out.astype(np.float32))
